# version 06
# speedup vs baseline: 1.2725x; 1.1466x over previous
import numpy as np
from contextlib import ExitStack
import concourse.bass as bass
import concourse.mybir as mybir
from concourse.bass_utils import run_bass_kernel_spmd
from concourse.bass import IndirectOffsetOnAxis

F32 = mybir.dt.float32
BF16 = mybir.dt.bfloat16
I32 = mybir.dt.int32
U32 = mybir.dt.uint32
U8 = mybir.dt.uint8
AF = mybir.ActivationFunctionType
ALU = mybir.AluOpType
AX = mybir.AxisListType

NCORES = 8
ENG_ATTR = {'t': 'tensor', 'v': 'vector', 's': 'scalar', 'g': 'gpsimd', 'y': 'sync'}
ESEM = {'t': 'et', 'v': 'ev', 's': 'es', 'g': 'eg', 'y': 'ey'}
DSEM = {'g': 'dg', 'y': 'dy', 's': 'ds'}
ALL_SEMS = ('et', 'ev', 'es', 'eg', 'ey', 'dg', 'dy', 'ds', 'cc')


class Pr:

    def __init__(self, nc):
        self.nc = nc
        self.items = {e: [] for e in ENG_ATTR}
        self.cnt = {}
        self.last_w = {}
        self.readers = {}
        self.ensured = {e: {} for e in ENG_ATTR}
        self.consts = {}
        self.T = {}
        self.stack = ExitStack()
        self.stack.__enter__()

    @staticmethod
    def key(x):
        if isinstance(x, str):
            return x
        n = x.tensor.name if hasattr(x, 'tensor') else x.name
        if n[:3] in ('pA_', 'pB_', 'pC_', 'pD_'):
            return n[3:]
        return n

    STRICT_DMA_WAITS = True

    def _wait(self, e, sem, val):
        if self.STRICT_DMA_WAITS and sem in ('dy', 'dg', 'ds'):
            val = max(val, self.cnt.get(sem, 0))
        if self.ensured[e].get(sem, 0) >= val:
            return
        self.ensured[e][sem] = val
        self.items[e].append(('w', sem, val))

    def _edges(self, e, reads, writes):
        for r in reads:
            k = self.key(r)
            lw = self.last_w.get(k)
            if lw is not None:
                sem, val, peng = lw
                if peng == e and e == 't':
                    continue
                self._wait(e, sem, val)
        for w in writes:
            k = self.key(w)
            lw = self.last_w.get(k)
            if lw is not None:
                sem, val, peng = lw
                if not (peng == e and e == 't'):
                    self._wait(e, sem, val)
            for sem, (val, reng) in self.readers.get(k, {}).items():
                if reng == e:
                    continue
                self._wait(e, sem, val)

    def _record(self, e, sem, val, reads, writes):
        for w in writes:
            k = self.key(w)
            self.last_w[k] = (sem, val, e)
            self.readers[k] = {}
        for r in reads:
            k = self.key(r)
            d = self.readers.setdefault(k, {})
            old = d.get(sem)
            d[sem] = (val if old is None else max(old[0], val), e)

    def op(self, e, fn, reads=(), writes=()):
        self._edges(e, list(reads), list(writes))
        sem = ESEM[e]
        self.cnt[sem] = self.cnt.get(sem, 0) + 1
        self.items[e].append(('i', fn, sem, 1))
        self._record(e, sem, self.cnt[sem], list(reads), list(writes))

    def dma(self, e, out, in_, extra_reads=(), **kw):
        reads = [in_] + list(extra_reads)
        self._edges(e, reads, [out])
        sem = DSEM[e]
        self.cnt[sem] = self.cnt.get(sem, 0) + 16
        fn = (lambda eng, _o=out, _i=in_, _k=dict(kw): eng.dma_start(out=_o, in_=_i, **_k))
        self.items[e].append(('i', fn, sem, 16))
        self._record(e, sem, self.cnt[sem], reads, [out])

    def idma_gather(self, out, in_, idx_ap, **kw):
        e = 'g'
        reads = [in_, idx_ap]
        self._edges(e, reads, [out])
        sem = DSEM[e]
        self.cnt[sem] = self.cnt.get(sem, 0) + 16
        fn = (lambda eng, _o=out, _i=in_, _x=idx_ap, _k=dict(kw):
              eng.indirect_dma_start(_o, None, _i, IndirectOffsetOnAxis(_x, axis=0), **_k))
        self.items[e].append(('i', fn, sem, 16))
        self._record(e, sem, self.cnt[sem], reads, [out])

    def idma_scatter(self, out, idx_ap, in_, **kw):
        e = 'g'
        reads = [in_, idx_ap]
        self._edges(e, reads, [out])
        sem = DSEM[e]
        self.cnt[sem] = self.cnt.get(sem, 0) + 16
        fn = (lambda eng, _o=out, _i=in_, _x=idx_ap, _k=dict(kw):
              eng.indirect_dma_start(_o, IndirectOffsetOnAxis(_x, axis=0), _i, None, **_k))
        self.items[e].append(('i', fn, sem, 16))
        self._record(e, sem, self.cnt[sem], reads, [out])

    def coll(self, kind, alu, ins, outs):
        e = 'g'
        self._edges(e, list(ins), list(outs))
        sem = 'cc'
        self.cnt[sem] = self.cnt.get(sem, 0) + 1
        fn = (lambda eng, _k=kind, _op=alu, _i=list(ins), _o=list(outs):
              eng.collective_compute(_k, _op, replica_groups=[list(range(NCORES))],
                                     ins=_i, outs=_o))
        self.items[e].append(('i', fn, sem, 1))
        self._record(e, sem, self.cnt[sem], list(ins), list(outs))

    def constcol(self, val):
        if val not in self.consts:
            name = f"_cc{len(self.consts)}"
            self.consts[val] = name
            self.T[name] = self.stack.enter_context(
                self.nc.sbuf_tensor(name, [128, 1], F32))
            self.op('g', (lambda e, _n=name, _v=val: e.memset(self.T[_n][:, :], _v)),
                    writes=[name])
        return self.consts[val]

    def final_waits(self, e='y'):
        for sname in ('dy', 'dg', 'ds'):
            if self.cnt.get(sname):
                self._wait(e, sname, self.cnt[sname])

    def alloc(self, sbuf_specs, psum_specs, prefix=""):
        stack = self.scopes[-1] if getattr(self, "scopes", None) else self.stack
        E = stack.enter_context
        for name, shape, dtype in sbuf_specs:
            self.T[name] = E(self.nc.sbuf_tensor(prefix + name, list(shape), dtype))
        for name, shape, dtype in psum_specs:
            self.T[name] = E(self.nc.psum_tensor(prefix + name, list(shape), dtype))

    def push_scope(self):
        if not hasattr(self, "scopes"):
            self.scopes = []
        st = ExitStack()
        st.__enter__()
        self.scopes.append(st)

    def pop_scope(self):
        self.scopes.pop().close()

    def barrier(self):
        snap = dict(self.cnt)
        for e in ENG_ATTR:
            for sem, val in snap.items():
                if val:
                    self._wait(e, sem, val)

    def finalize(self):
        E = self.stack.enter_context
        sems = {s: E(self.nc.semaphore(s)) for s in ALL_SEMS}
        block = E(self.nc.Block())
        for e, attr in ENG_ATTR.items():
            if not self.items[e]:
                continue
            dec = getattr(block, attr)

            def section(eng, _items=self.items[e], _sems=sems):
                for it in _items:
                    if it[0] == 'w':
                        eng.wait_ge(_sems[it[1]], it[2])
                    else:
                        it[1](eng).then_inc(_sems[it[2]], it[3])
            dec(section)
        self.stack.close()


TC = 256
NTC = 2048 // TC


def build_launchA():
    nc = bass.Bass("TRN2", target_bir_lowering=False)
    x = nc.dram_tensor("x", [2, 2048, 1024], F32, kind="ExternalInput").ap()
    W_in = nc.dram_tensor("W_in", [1024, 4096], F32, kind="ExternalInput").ap()
    conv_w = nc.dram_tensor("conv_w", [2048, 4], F32, kind="ExternalInput").ap()
    conv_b = nc.dram_tensor("conv_b", [2048], F32, kind="ExternalInput").ap()
    W_xproj = nc.dram_tensor("W_xproj", [2048, 96], F32, kind="ExternalInput").ap()
    W_dt = nc.dram_tensor("W_dt", [64, 2048], F32, kind="ExternalInput").ap()
    b_dt = nc.dram_tensor("b_dt", [2048], F32, kind="ExternalInput").ap()
    A_log = nc.dram_tensor("A_log", [2048, 16], F32, kind="ExternalInput").ap()
    D_skip = nc.dram_tensor("D_skip", [2048], F32, kind="ExternalInput").ap()
    W_out = nc.dram_tensor("W_out", [2048, 1024], F32, kind="ExternalInput").ap()
    ln_g = nc.dram_tensor("ln_g", [1024], F32, kind="ExternalInput").ap()
    ln_b = nc.dram_tensor("ln_b", [1024], F32, kind="ExternalInput").ap()
    rw = nc.dram_tensor("rw", [8, 1024], F32, kind="ExternalInput").ap()

    s_fm = nc.dram_tensor("s_fm", [2, 1024, 2048], F32, kind="ExternalOutput").ap()
    xn_tok = nc.dram_tensor("xn_tok", [4096, 1024], F32, kind="ExternalOutput").ap()
    route = nc.dram_tensor("route", [4096, 2], F32, kind="ExternalOutput").ap()

    pr = Pr(nc)
    T = pr.T
    sbufs = [
        ("cw", [128, 64], F32), ("cb", [128, 16], F32), ("bdt", [128, 16], F32),
        ("dsk", [128, 16], F32), ("alog", [128, 256], F32), ("aneg", [128, 256], F32),
        ("lng", [128, 8], F32), ("lnb", [128, 8], F32),
        ("rwsb", [8, 1024], F32), ("rwT", [128, 64], F32),
        ("wdt", [64, 2048], BF16), ("wxp", [128, 16 * 96], BF16),
        ("idf", [128, 128], F32), ("ioti", [128, 128], I32),
        ("onescol", [128, 1], BF16), ("ones1f", [1, 128], F32),
        ("wbuf0", [128, 8 * 128], BF16), ("wbuf1", [128, 8 * 128], BF16),
        ("obuf0", [128, 16 * 128], BF16), ("obuf1", [128, 16 * 128], BF16),
        ("xtok0", [128, 1024], F32), ("xtok1", [128, 1024], F32),
        ("x_fm", [128, 2048], F32), ("x_bf", [128, 2048], BF16),
        ("xn_f", [128, 2048], F32), ("xn_bf", [128, 2048], BF16),
        ("tmp2", [128, 256], F32), ("xn2_bf", [128, 2048], BF16),
        ("srow", [1, 1024], F32), ("u_s", [128, 16 * 260], F32),
        ("zs", [128, 4096], F32), ("us", [128, 4096], F32), ("usbf", [128, 4096], BF16),
        ("du", [128, 4096], F32), ("delta", [128, 4096], F32),
        ("y_acc", [128, 4096], F32), ("padt", [128, 48], F32), ("hstate", [128, 256], F32),
        ("dtbf", [64, 256], BF16), ("bcsb", [16, 512], F32), ("bcrow", [1, 512], F32),
        ("brep", [128, 256], F32), ("crep", [128, 256], F32),
        ("dAt", [128, 256], F32), ("duBt", [128, 256], F32), ("ht", [128, 256], F32),
        ("ytmp", [128, 256], F32), ("ctmp", [128, 256], F32),
        ("rsrep", [128, 256], F32), ("mrsrep", [128, 256], F32),
        ("spabs", [128, 256], F32), ("spexp", [128, 256], F32), ("spln", [128, 256], F32),
        ("yg_bf", [128, 4096], BF16), ("s_sb", [128, 256], F32),
        ("rsb", [8, 256], F32), ("rtok", [128, 16], F32), ("mx8", [128, 8], F32),
        ("mi8", [128, 8], U32), ("negmax", [128, 1], F32), ("etok", [128, 8], F32),
        ("esum", [128, 1], F32), ("routesb", [128, 4], F32),
    ]
    psums = [
        ("psA", [128, 512], F32), ("psB", [128, 512], F32),
        ("psC", [128, 512], F32), ("psD", [128, 512], F32),
        ("psE", [128, 512], F32), ("psF", [128, 512], F32),
        ("psG", [128, 512], F32), ("psH", [128, 512], F32),
    ]

    pr.alloc(sbufs, psums)

    def mm(ps_ap, ps_key, lhsT, rhs, start, stop, reads):
        pr.op('t', (lambda e, _p=ps_ap, _l=lhsT, _r=rhs, _s=start, _e2=stop:
                    e.matmul(_p, _l, _r, start=_s, stop=_e2)),
              reads=reads, writes=[ps_key])

    def tr(ps_ap, ps_key, in_ap, reads, pin=128):
        ident = T["idf"][0:pin, 0:pin]
        pr.op('t', (lambda e, _p=ps_ap, _i=in_ap, _id=ident: e.transpose(_p, _i, _id)),
              reads=list(reads) + ["idf"], writes=[ps_key])

    c_one = pr.constcol(1.0)
    c_neg1 = pr.constcol(-1.0)

    pr.dma('y', T["cw"][:, :].rearrange("p (c j) -> p c j", j=4),
           conv_w[:].rearrange("(c p) j -> p c j", p=128))
    pr.dma('y', T["cb"][:, :], conv_b[:].rearrange("(c p) -> p c", p=128),
           allow_slow_non_contiguous=True)
    pr.dma('y', T["bdt"][:, :], b_dt[:].rearrange("(c p) -> p c", p=128),
           allow_slow_non_contiguous=True)
    pr.dma('y', T["dsk"][:, :], D_skip[:].rearrange("(c p) -> p c", p=128),
           allow_slow_non_contiguous=True)
    pr.dma('y', T["alog"][:, :].rearrange("p (c n) -> p c n", n=16),
           A_log[:].rearrange("(c p) n -> p c n", p=128))
    pr.dma('y', T["lng"][:, :], ln_g[:].rearrange("(c p) -> p c", p=128),
           allow_slow_non_contiguous=True)
    pr.dma('y', T["lnb"][:, :], ln_b[:].rearrange("(c p) -> p c", p=128),
           allow_slow_non_contiguous=True)
    pr.dma('y', T["rwsb"][:, :], rw[:, :])
    pr.dma('g', T["wdt"][:, :], W_dt[:, :])
    pr.dma('g', T["wxp"][:, :].rearrange("p (c f) -> p c f", f=96),
           W_xproj[:].rearrange("(c p) f -> p c f", p=128))
    pr.op('g', lambda e: e.iota(T["ioti"][:, :], pattern=[[-1, 128]], base=0,
                                channel_multiplier=1), writes=["ioti"])
    pr.op('v', lambda e: e.tensor_copy(T["idf"][:, :], T["ioti"][:, :]),
          reads=["ioti"], writes=["idf"])
    pr.op('v', lambda e: e.tensor_single_scalar(T["idf"][:, :], T["idf"][:, :], 0.0,
                                                ALU.is_equal),
          reads=["idf"], writes=["idf"])
    pr.op('g', lambda e: e.memset(T["onescol"][:, :], 1.0), writes=["onescol"])
    pr.op('g', lambda e: e.memset(T["ones1f"][:, :], 1.0), writes=["ones1f"])
    pr.op('s', lambda e: e.activation(T["aneg"][:, :], T["alog"][:, :], AF.Exp),
          reads=["alog"], writes=["aneg"])
    pr.op('v', lambda e: e.tensor_scalar_mul(T["aneg"][:, :], T["aneg"][:, :], -1.0),
          reads=["aneg"], writes=["aneg"])
    for c in range(8):
        tr(T["psA"][:, 0:8], "psA", T["rwsb"][0:8, c * 128:(c + 1) * 128], ["rwsb"], pin=8)
        pr.op('v', (lambda e, _c=c: e.tensor_copy(T["rwT"][:, _c * 8:(_c + 1) * 8],
                                                  T["psA"][:, 0:8])),
              reads=["psA"], writes=["rwT"])

    def stats(src_bf):
        for c in range(8):
            mm(T["psA"][0:1, 0:TC], "psA", T["onescol"][:, 0:1],
               src_bf[:, c * TC:(c + 1) * TC], c == 0, c == 7, ["onescol", src_bf])
        pr.op('s', (lambda e, _s=src_bf: e.activation(_s[:, :], _s[:, :], AF.Square)),
              reads=[src_bf], writes=[src_bf])
        for c in range(8):
            mm(T["psB"][0:1, 0:TC], "psB", T["onescol"][:, 0:1],
               src_bf[:, c * TC:(c + 1) * TC], c == 0, c == 7, ["onescol", src_bf])
        pr.op('v', lambda e: e.tensor_scalar_mul(T["srow"][0:1, 0:TC],
                                                 T["psA"][0:1, 0:TC], 1.0 / 1024),
              reads=["psA"], writes=["srow"])
        pr.op('v', lambda e: e.tensor_scalar_mul(T["srow"][0:1, 256:256 + TC],
                                                 T["psB"][0:1, 0:TC], 1.0 / 1024),
              reads=["psB"], writes=["srow"])
        pr.op('v', lambda e: e.tensor_mul(T["srow"][0:1, 512:512 + TC],
                                          T["srow"][0:1, 0:TC], T["srow"][0:1, 0:TC]),
              reads=["srow"], writes=["srow"])
        pr.op('v', lambda e: e.tensor_sub(T["srow"][0:1, 256:256 + TC],
                                          T["srow"][0:1, 256:256 + TC],
                                          T["srow"][0:1, 512:512 + TC]),
              reads=["srow"], writes=["srow"])
        pr.op('v', lambda e: e.tensor_scalar_add(T["srow"][0:1, 256:256 + TC],
                                                 T["srow"][0:1, 256:256 + TC], 1e-5),
              reads=["srow"], writes=["srow"])
        pr.op('s', lambda e: e.activation(T["srow"][0:1, 256:256 + TC],
                                          T["srow"][0:1, 256:256 + TC], AF.Ln),
              reads=["srow"], writes=["srow"])
        pr.op('v', lambda e: e.tensor_scalar_mul(T["srow"][0:1, 256:256 + TC],
                                                 T["srow"][0:1, 256:256 + TC], -0.5),
              reads=["srow"], writes=["srow"])
        pr.op('s', lambda e: e.activation(T["srow"][0:1, 256:256 + TC],
                                          T["srow"][0:1, 256:256 + TC], AF.Exp),
              reads=["srow"], writes=["srow"])
        pr.op('v', lambda e: e.tensor_mul(T["srow"][0:1, 0:TC], T["srow"][0:1, 0:TC],
                                          T["srow"][0:1, 256:256 + TC]),
              reads=["srow"], writes=["srow"])

    def bcast(row_ap, dst_name):
        mm(T["psC"][:, 0:TC], "psC", T["ones1f"][0:1, :], row_ap, True, True,
           ["srow", "ones1f"])
        pr.op('v', (lambda e, _d=dst_name: e.tensor_copy(T[_d][:, 0:TC], T["psC"][:, 0:TC])),
              reads=["psC"], writes=[dst_name])

    for b in range(2):
        pr.op('g', lambda e: e.memset(T["hstate"][:, :], 0.0), writes=["hstate"])
        pr.op('g', lambda e: e.memset(T["padt"][:, :], 0.0), writes=["padt"])
        for tci in range(NTC):
            t0 = tci * TC
            xt = ["xtok0", "xtok1"]
            for i in range(2):
                pr.dma('y', T[xt[i]][:, :], x[b, t0 + i * 128: t0 + (i + 1) * 128, :])
            for c in range(8):
                for i in range(2):
                    tr(T["psH"][:, 0:128], "psH",
                       T[xt[i]][:, c * 128:(c + 1) * 128], [xt[i]])
                    col = c * 256 + i * 128
                    pr.op('v', (lambda e, _col=col: e.tensor_copy(
                        T["x_fm"][:, _col:_col + 128], T["psH"][:, 0:128])),
                          reads=["psH"], writes=["x_fm"])
            pr.op('s', lambda e: e.copy(T["x_bf"][:, :], T["x_fm"][:, :]),
                  reads=["x_fm"], writes=["x_bf"])

            stats(T["x_bf"])
            bcast(T["srow"][0:1, 256:256 + TC], "rsrep")
            bcast(T["srow"][0:1, 0:TC], "mrsrep")
            for c in range(8):
                pr.op('v', (lambda e, _c=c: e.tensor_mul(
                    T["xn_f"][:, _c * TC:(_c + 1) * TC],
                    T["x_fm"][:, _c * TC:(_c + 1) * TC], T["rsrep"][:, 0:TC])),
                      reads=["x_fm", "rsrep"], writes=["xn_f"])
                pr.op('v', (lambda e, _c=c: e.tensor_sub(
                    T["xn_f"][:, _c * TC:(_c + 1) * TC],
                    T["xn_f"][:, _c * TC:(_c + 1) * TC], T["mrsrep"][:, 0:TC])),
                      reads=["xn_f", "mrsrep"], writes=["xn_f"])
            pr.op('s', lambda e: e.copy(T["xn_bf"][:, :], T["xn_f"][:, :]),
                  reads=["xn_f"], writes=["xn_bf"])

            stats(T["xn_bf"])
            bcast(T["srow"][0:1, 256:256 + TC], "rsrep")
            bcast(T["srow"][0:1, 0:TC], "mrsrep")
            for c in range(8):
                pr.op('v', (lambda e, _c=c: e.tensor_mul(
                    T["tmp2"][:, 0:TC],
                    T["xn_f"][:, _c * TC:(_c + 1) * TC], T["rsrep"][:, 0:TC])),
                      reads=["xn_f", "rsrep"], writes=["tmp2"])
                pr.op('v', lambda e: e.tensor_sub(
                    T["tmp2"][:, 0:TC], T["tmp2"][:, 0:TC], T["mrsrep"][:, 0:TC]),
                      reads=["tmp2", "mrsrep"], writes=["tmp2"])
                pr.op('v', (lambda e, _c=c: e.tensor_scalar(
                    T["xn2_bf"][:, _c * TC:(_c + 1) * TC], T["tmp2"][:, 0:TC],
                    T["lng"][:, _c:_c + 1], T["lnb"][:, _c:_c + 1],
                    ALU.mult, ALU.add)),
                      reads=["tmp2", "lnb", "lng"], writes=["xn2_bf"])

            for f in range(32):
                wb = "wbuf0" if f % 2 == 0 else "wbuf1"
                pr.dma('g', T[wb][:, :].rearrange("p (c k) -> p c k", k=128),
                       W_in[:].rearrange("(c p) (f k) -> p c f k", p=128, f=32)[:, :, f, :])
                for c in range(8):
                    mm(T["psD"][:, 0:TC], "psD", T[wb][:, c * 128:(c + 1) * 128],
                       T["xn2_bf"][:, c * TC:(c + 1) * TC], c == 0, c == 7, [wb, "xn2_bf"])
                if f < 16:
                    pr.op('v', (lambda e, _f=f: e.tensor_copy(
                        T["u_s"][:, _f * 260 + 3:_f * 260 + 3 + TC], T["psD"][:, 0:TC])),
                          reads=["psD"], writes=["u_s"])
                else:
                    pr.op('s', (lambda e, _z=f - 16: e.activation(
                        T["zs"][:, _z * TC:(_z + 1) * TC], T["psD"][:, 0:TC], AF.Silu)),
                          reads=["psD"], writes=["zs"])

            for c in range(16):
                pr.op('v', (lambda e, _c=c: e.tensor_copy(
                    T["u_s"][:, _c * 260:_c * 260 + 3], T["padt"][:, _c * 3:_c * 3 + 3])),
                      reads=["padt"], writes=["u_s"])
            for c in range(16):
                base = c * 260
                pr.op('v', (lambda e, _b=base, _c=c: e.tensor_scalar(
                    T["ytmp"][:, 0:TC], T["u_s"][:, _b:_b + TC],
                    T["cw"][:, 4 * _c:4 * _c + 1], None, ALU.mult)),
                      reads=["u_s", "cw"], writes=["ytmp"])
                for j in range(1, 4):
                    pr.op('v', (lambda e, _b=base, _c=c, _j=j: e.scalar_tensor_tensor(
                        T["ytmp"][:, 0:TC], T["u_s"][:, _b + _j:_b + _j + TC],
                        T["cw"][:, 4 * _c + _j:4 * _c + _j + 1], T["ytmp"][:, 0:TC],
                        ALU.mult, ALU.add)),
                          reads=["u_s", "cw", "ytmp"], writes=["ytmp"])
                pr.op('v', (lambda e, _b=base, _c=c: e.tensor_copy(
                    T["padt"][:, _c * 3:_c * 3 + 3], T["u_s"][:, _b + TC:_b + TC + 3])),
                      reads=["u_s"], writes=["padt"])
                pr.op('s', (lambda e, _c=c: e.activation(
                    T["us"][:, _c * TC:(_c + 1) * TC], T["ytmp"][:, 0:TC], AF.Silu,
                    bias=T["cb"][:, _c:_c + 1])),
                      reads=["ytmp", "cb"], writes=["us"])
            pr.op('s', lambda e: e.copy(T["usbf"][:, :], T["us"][:, :]),
                  reads=["us"], writes=["usbf"])

            for c in range(16):
                mm(T["psE"][0:64, 0:TC], "psE", T["wxp"][:, c * 96:c * 96 + 64],
                   T["usbf"][:, c * TC:(c + 1) * TC], c == 0, c == 15, ["wxp", "usbf"])
            for c in range(16):
                mm(T["psF"][0:16, 0:TC], "psF", T["wxp"][:, c * 96 + 64:c * 96 + 80],
                   T["usbf"][:, c * TC:(c + 1) * TC], c == 0, c == 15, ["wxp", "usbf"])
            for c in range(16):
                mm(T["psG"][0:16, 0:TC], "psG", T["wxp"][:, c * 96 + 80:c * 96 + 96],
                   T["usbf"][:, c * TC:(c + 1) * TC], c == 0, c == 15, ["wxp", "usbf"])
            pr.op('s', lambda e: e.copy(T["dtbf"][0:64, 0:TC], T["psE"][0:64, 0:TC]),
                  reads=["psE"], writes=["dtbf"])
            pr.op('v', lambda e: e.tensor_copy(T["bcsb"][0:16, 0:TC], T["psF"][0:16, 0:TC]),
                  reads=["psF"], writes=["bcsb"])
            pr.op('v', lambda e: e.tensor_copy(T["bcsb"][0:16, 256:256 + TC],
                                               T["psG"][0:16, 0:TC]),
                  reads=["psG"], writes=["bcsb"])

            for c in range(16):
                mm(T["psE"][:, 256:256 + TC], "psE", T["wdt"][0:64, c * 128:(c + 1) * 128],
                   T["dtbf"][0:64, 0:TC], True, True, ["wdt", "dtbf"])
                pr.op('s', (lambda e, _c=c: e.activation(
                    T["spabs"][:, 0:TC], T["psE"][:, 256:256 + TC], AF.Abs,
                    bias=T["bdt"][:, _c:_c + 1])),
                      reads=["psE", "bdt"], writes=["spabs"])
                pr.op('s', (lambda e, _c=c: e.activation(
                    T["delta"][:, _c * TC:(_c + 1) * TC], T["psE"][:, 256:256 + TC],
                    AF.Relu, bias=T["bdt"][:, _c:_c + 1])),
                      reads=["psE", "bdt"], writes=["delta"])
                pr.op('s', lambda e: e.activation(
                    T["spexp"][:, 0:TC], T["spabs"][:, 0:TC], AF.Exp,
                    scale=T[c_neg1][:, 0:1]),
                      reads=["spabs", c_neg1], writes=["spexp"])
                pr.op('s', lambda e: e.activation(
                    T["spln"][:, 0:TC], T["spexp"][:, 0:TC], AF.Ln,
                    bias=T[c_one][:, 0:1]),
                      reads=["spexp", c_one], writes=["spln"])
                pr.op('v', (lambda e, _c=c: e.tensor_add(
                    T["delta"][:, _c * TC:(_c + 1) * TC],
                    T["delta"][:, _c * TC:(_c + 1) * TC], T["spln"][:, 0:TC])),
                      reads=["delta", "spln"], writes=["delta"])

            for c in range(16):
                pr.op('v', (lambda e, _c=c: e.tensor_mul(
                    T["du"][:, _c * TC:(_c + 1) * TC],
                    T["delta"][:, _c * TC:(_c + 1) * TC],
                    T["us"][:, _c * TC:(_c + 1) * TC])),
                      reads=["delta", "us"], writes=["du"])

            for n in range(16):
                pr.dma('y', T["bcrow"][0:1, 0:512], T["bcsb"][n:n + 1, 0:512])
                mm(T["psC"][:, 0:TC], "psC", T["ones1f"][0:1, :],
                   T["bcrow"][0:1, 0:TC], True, True, ["bcrow", "ones1f"])
                pr.op('v', lambda e: e.tensor_copy(T["brep"][:, 0:TC], T["psC"][:, 0:TC]),
                      reads=["psC"], writes=["brep"])
                mm(T["psC"][:, 0:TC], "psC", T["ones1f"][0:1, :],
                   T["bcrow"][0:1, 256:256 + TC], True, True, ["bcrow", "ones1f"])
                pr.op('v', lambda e: e.tensor_copy(T["crep"][:, 0:TC], T["psC"][:, 0:TC]),
                      reads=["psC"], writes=["crep"])
                for c in range(16):
                    pr.op('s', (lambda e, _c=c, _n=n: e.activation(
                        T["dAt"][:, 0:TC], T["delta"][:, _c * TC:(_c + 1) * TC], AF.Exp,
                        scale=T["aneg"][:, 16 * _c + _n:16 * _c + _n + 1])),
                          reads=["delta", "aneg"], writes=["dAt"])
                    pr.op('v', (lambda e, _c=c: e.tensor_mul(
                        T["duBt"][:, 0:TC], T["du"][:, _c * TC:(_c + 1) * TC],
                        T["brep"][:, 0:TC])),
                          reads=["du", "brep"], writes=["duBt"])
                    pr.op('v', (lambda e, _c=c, _n=n: e.tensor_tensor_scan(
                        T["ht"][:, 0:TC], T["dAt"][:, 0:TC], T["duBt"][:, 0:TC],
                        T["hstate"][:, 16 * _c + _n:16 * _c + _n + 1],
                        ALU.mult, ALU.add)),
                          reads=["dAt", "duBt", "hstate"], writes=["ht"])
                    pr.op('v', (lambda e, _c=c, _n=n: e.tensor_copy(
                        T["hstate"][:, 16 * _c + _n:16 * _c + _n + 1],
                        T["ht"][:, TC - 1:TC])),
                          reads=["ht"], writes=["hstate"])
                    pr.op('v', lambda e: e.tensor_mul(
                        T["ctmp"][:, 0:TC], T["ht"][:, 0:TC], T["crep"][:, 0:TC]),
                          reads=["ht", "crep"], writes=["ctmp"])
                    if n == 0:
                        pr.op('v', (lambda e, _c=c: e.scalar_tensor_tensor(
                            T["y_acc"][:, _c * TC:(_c + 1) * TC],
                            T["us"][:, _c * TC:(_c + 1) * TC],
                            T["dsk"][:, _c:_c + 1], T["ctmp"][:, 0:TC],
                            ALU.mult, ALU.add)),
                              reads=["us", "dsk", "ctmp"], writes=["y_acc"])
                    else:
                        pr.op('v', (lambda e, _c=c: e.tensor_add(
                            T["y_acc"][:, _c * TC:(_c + 1) * TC],
                            T["y_acc"][:, _c * TC:(_c + 1) * TC], T["ctmp"][:, 0:TC])),
                              reads=["y_acc", "ctmp"], writes=["y_acc"])

            for c in range(16):
                pr.op('v', (lambda e, _c=c: e.tensor_mul(
                    T["yg_bf"][:, _c * TC:(_c + 1) * TC],
                    T["y_acc"][:, _c * TC:(_c + 1) * TC],
                    T["zs"][:, _c * TC:(_c + 1) * TC])),
                      reads=["y_acc", "zs"], writes=["yg_bf"])

            for m in range(8):
                ob = "obuf0" if m % 2 == 0 else "obuf1"
                pr.dma('g', T[ob][:, :].rearrange("p (c k) -> p c k", k=128),
                       W_out[:].rearrange("(c p) (m k) -> p c m k", p=128, m=8)[:, :, m, :])
                for c in range(16):
                    mm(T["psD"][:, 256:256 + TC], "psD", T[ob][:, c * 128:(c + 1) * 128],
                       T["yg_bf"][:, c * TC:(c + 1) * TC], c == 0, c == 15, [ob, "yg_bf"])
                pr.op('v', (lambda e, _m=m: e.tensor_add(
                    T["s_sb"][:, 0:TC], T["psD"][:, 256:256 + TC],
                    T["xn_f"][:, _m * TC:(_m + 1) * TC])),
                      reads=["psD", "xn_f"], writes=["s_sb"])
                pr.dma('y', s_fm[b, m * 128:(m + 1) * 128, t0:t0 + TC], T["s_sb"][:, 0:TC])

            for i in range(2):
                for c in range(8):
                    tr(T["psH"][:, 128:256], "psH",
                       T["xn_f"][:, c * 256 + i * 128:c * 256 + i * 128 + 128], ["xn_f"])
                    pr.op('v', (lambda e, _c=c, _i=i: e.tensor_copy(
                        T[xt[_i]][:, _c * 128:(_c + 1) * 128], T["psH"][:, 128:256])),
                          reads=["psH"], writes=[xt[i]])
                pr.dma('y', xn_tok[b * 2048 + t0 + i * 128: b * 2048 + t0 + (i + 1) * 128, :],
                       T[xt[i]][:, :])

            for c in range(8):
                mm(T["psA"][0:8, 256:256 + TC], "psA", T["rwT"][:, c * 8:(c + 1) * 8],
                   T["xn_f"][:, c * TC:(c + 1) * TC], c == 0, c == 7, ["rwT", "xn_f"])
            pr.op('v', lambda e: e.tensor_copy(T["rsb"][0:8, 0:TC],
                                               T["psA"][0:8, 256:256 + TC]),
                  reads=["psA"], writes=["rsb"])
            for i in range(2):
                tr(T["psB"][:, 256:264], "psB", T["rsb"][0:8, i * 128:(i + 1) * 128], ["rsb"], pin=8)
                pr.op('v', lambda e: e.tensor_copy(T["rtok"][:, 0:8], T["psB"][:, 256:264]),
                      reads=["psB"], writes=["rtok"])
                pr.op('v', lambda e: e.max(T["mx8"][:, :], T["rtok"][:, 0:8]),
                      reads=["rtok"], writes=["mx8"])
                pr.op('v', lambda e: e.max_index(T["mi8"][:, :], T["mx8"][:, :],
                                                 T["rtok"][:, 0:8]),
                      reads=["mx8", "rtok"], writes=["mi8"])
                pr.op('v', lambda e: e.tensor_scalar_mul(T["negmax"][:, 0:1],
                                                         T["mx8"][:, 0:1], -1.0),
                      reads=["mx8"], writes=["negmax"])
                pr.op('s', lambda e: e.activation(T["etok"][:, :], T["rtok"][:, 0:8],
                                                  AF.Exp, bias=T["negmax"][:, 0:1],
                                                  accum_out=T["esum"][:, 0:1]),
                      reads=["rtok", "negmax"], writes=["etok", "esum"])
                pr.op('v', lambda e: e.reciprocal(T["esum"][:, 0:1], T["esum"][:, 0:1]),
                      reads=["esum"], writes=["esum"])
                pr.op('v', lambda e: e.tensor_copy(T["routesb"][:, 0:1], T["mi8"][:, 0:1]),
                      reads=["mi8"], writes=["routesb"])
                pr.op('v', lambda e: e.tensor_copy(T["routesb"][:, 1:2], T["esum"][:, 0:1]),
                      reads=["esum"], writes=["routesb"])
                pr.dma('y', route[b * 2048 + t0 + i * 128: b * 2048 + t0 + (i + 1) * 128, :],
                       T["routesb"][:, 0:2])

    pr.final_waits('y')
    pr.finalize()
    return nc




CLOC = 1024
NSLOT = 8 * CLOC


def build_launchB2(dbg=False):
    nc = bass.Bass("TRN2", target_bir_lowering=False)
    x = nc.dram_tensor("x", [16, 2048, 1024], F32, kind="ExternalInput").ap()
    xnf = nc.dram_tensor("xnf", [4097, 1024], F32, kind="ExternalInput").ap()
    myexp = nc.dram_tensor("myexp", [1, 1], F32, kind="ExternalInput").ap()
    rw = nc.dram_tensor("rw", [8, 1024], F32, kind="ExternalInput").ap()
    w1 = nc.dram_tensor("w1", [8, 1024, 4096], F32, kind="ExternalInput").ap()
    b1 = nc.dram_tensor("b1", [8, 4096], F32, kind="ExternalInput").ap()
    w2 = nc.dram_tensor("w2", [8, 4096, 1024], F32, kind="ExternalInput").ap()
    b2 = nc.dram_tensor("b2", [8, 1024], F32, kind="ExternalInput").ap()
    eo = nc.dram_tensor("eo", [NSLOT, 1024], F32, kind="ExternalOutput").ap()
    gidxT = nc.dram_tensor("gidxT", [256, 128], F32, kind="ExternalOutput").ap()
    srcd = nc.dram_tensor("srcd", [NSLOT + 2, 1], I32).ap()

    pr = Pr(nc)
    T = pr.T
    sbufs = [
        ("evsb", [128, 256], F32), ("EQ", [128, 2048], BF16),
        ("rank", [128, 2048], F32), ("cnt8", [8, 256], F32), ("pfx", [8, 256], F32),
        ("cntrow", [1, 2048], F32), ("pfxrow", [1, 2048], F32), ("ones8", [8, 256], F32),
        ("base", [128, 256], F32), ("rtk", [128, 256], F32), ("postk", [128, 256], F32),
        ("brd", [128, 256], F32), ("tmpb", [128, 256], F32),
        ("tri", [128, 128], BF16), ("ioti", [128, 128], I32),
        ("onescol", [128, 1], BF16), ("ones1f", [1, 128], F32), ("idf", [128, 128], F32),
        ("idb", [128, 128], BF16), ("mycol", [128, 1], F32), ("myc32", [128, 1], F32),
        ("xt0", [128, 1024], F32),
        ("xfmT", [128, 1024], F32),
        ("rwsb", [8, 1024], F32), ("s8", [8, 1], F32), ("rwc", [8, 1024], F32),
        ("rwcT", [128, 64], F32),
        ("rtok", [128, 8], F32), ("mx8", [128, 8], F32), ("mi8", [128, 8], U32),
        ("iotf", [128, 256], F32),
        ("colsel", [8, 256], F32), ("tmp8", [8, 256], F32), ("base8", [8, 1], F32),
        ("base8row", [1, 8], F32), ("brdcol", [128, 1], F32),
        ("baseTok", [128, 256], F32), ("lr", [128, 256], F32),
        ("slotf", [128, 256], F32), ("kgm", [128, 256], F32), ("klm", [128, 256], F32),
        ("keepA", [128, 256], F32), ("mOwn", [128, 256], F32), ("dcol", [128, 256], F32),
        ("gidxf", [128, 256], F32),
        ("tokidf", [128, 256], F32), ("gstg", [128, 128], F32),
        ("mu8b", [128, 256], U8),
        ("tokid", [128, 256], I32), ("slot", [128, 256], I32), ("sh", [128, 256], I32),
        ("rem", [128, 256], I32), ("tgt", [128, 256], I32), ("tgt2", [128, 256], I32),
        ("trash", [128, 256], I32),
        ("fill", [128, 64], I32), ("srcidx", [128, 64], I32),
        ("w1sb0", [128, 8 * 2048], BF16), ("w1sb1", [128, 8 * 2048], BF16),
        ("b1sb", [128, 32], F32), ("b2sb", [128, 8], F32),
        ("w2b0", [128, 32 * 128], BF16), ("w2b1", [128, 32 * 128], BF16),
        ("gbuf", [128, 1024], F32), ("gbf", [128, 1024], BF16),
        ("bufT", [128, 4096], BF16), ("hbf", [128, 16384], BF16),
        ("eofm", [128, 512], F32), ("estage", [128, 128], F32),
    ]
    psums = [
        ("psA", [128, 512], F32), ("psB", [128, 512], F32),
        ("psC", [128, 512], F32), ("psD", [128, 512], F32),
        ("psT", [128, 512], BF16), ("psE", [128, 512], F32),
        ("psR", [128, 8], F32),
    ]
    pr.alloc(sbufs, psums)

    def mm(ps_ap, ps_key, lhsT, rhs, start, stop, reads):
        pr.op('t', (lambda e, _p=ps_ap, _l=lhsT, _r=rhs, _s=start, _e2=stop:
                    e.matmul(_p, _l, _r, start=_s, stop=_e2)),
              reads=reads, writes=[ps_key])

    def tr(ps_ap, ps_key, in_ap, reads, pin=128, bf=False):
        ident = (T["idb"] if bf else T["idf"])[0:pin, 0:pin]
        pr.op('t', (lambda e, _p=ps_ap, _i=in_ap, _id=ident: e.transpose(_p, _i, _id)),
              reads=list(reads) + ["idb" if bf else "idf"], writes=[ps_key])

    pr.op('g', lambda e: e.iota(T["ioti"][:, :], pattern=[[-1, 128]], base=0,
                                channel_multiplier=1), writes=["ioti"])
    pr.op('v', lambda e: e.tensor_copy(T["idf"][:, :], T["ioti"][:, :]),
          reads=["ioti"], writes=["idf"])
    pr.op('v', lambda e: e.tensor_single_scalar(T["idf"][:, :], T["idf"][:, :], 0.0,
                                                ALU.is_equal),
          reads=["idf"], writes=["idf"])
    pr.op('v', lambda e: e.tensor_copy(T["idb"][:, :], T["idf"][:, :]),
          reads=["idf"], writes=["idb"])
    pr.op('v', lambda e: e.tensor_copy(T["rank"][:, 0:128], T["ioti"][:, :]),
          reads=["ioti"], writes=["rank"])
    pr.op('v', lambda e: e.tensor_single_scalar(T["tri"][:, :], T["rank"][:, 0:128], 0.0,
                                                ALU.is_lt),
          reads=["rank"], writes=["tri"])
    pr.op('g', lambda e: e.memset(T["onescol"][:, :], 1.0), writes=["onescol"])
    pr.op('g', lambda e: e.memset(T["ones1f"][:, :], 1.0), writes=["ones1f"])
    pr.dma('y', T["mycol"][0:1, 0:1], myexp[0:1, 0:1])
    mm(T["psA"][:, 0:1], "psA", T["ones1f"][0:1, :], T["mycol"][0:1, 0:1], True, True,
       ["mycol", "ones1f"])
    pr.op('v', lambda e: e.tensor_copy(T["mycol"][:, 0:1], T["psA"][:, 0:1]),
          reads=["psA"], writes=["mycol"])
    pr.op('v', lambda e: e.tensor_scalar_mul(T["myc32"][:, 0:1], T["mycol"][:, 0:1],
                                             32.0),
          reads=["mycol"], writes=["myc32"])

    pr.dma('y', T["rwsb"][0:8, :], rw[:, :])
    pr.op('v', lambda e: e.reduce_sum(T["s8"][0:8, 0:1], T["rwsb"][0:8, :], AX.X),
          reads=["rwsb"], writes=["s8"])
    pr.op('v', lambda e: e.tensor_scalar_mul(T["s8"][0:8, 0:1], T["s8"][0:8, 0:1],
                                             1.0 / 1024),
          reads=["s8"], writes=["s8"])
    pr.op('v', lambda e: e.tensor_scalar(T["rwc"][0:8, :], T["rwsb"][0:8, :],
                                         T["s8"][0:8, 0:1], None, ALU.subtract),
          reads=["rwsb", "s8"], writes=["rwc"])
    for c in range(8):
        tr(T["psA"][:, 0:8], "psA", T["rwc"][0:8, c * 128:(c + 1) * 128], ["rwc"],
           pin=8)
        pr.op('v', (lambda e, _c=c: e.tensor_copy(T["rwcT"][:, _c * 8:(_c + 1) * 8],
                                                  T["psA"][:, 0:8])),
              reads=["psA"], writes=["rwcT"])
    for bb in range(16):
        for jt in range(16):
            col = bb * 16 + jt
            xt = "xt0"
            pr.dma('y', T[xt][:, :], x[bb, jt * 128:(jt + 1) * 128, :])
            for h in range(2):
                for c4 in range(4):
                    c = h * 4 + c4
                    tr(T["psB"][:, c4 * 128:(c4 + 1) * 128], "psB",
                       T[xt][:, c * 128:(c + 1) * 128], [xt])
                pr.op('v', (lambda e, _h=h: e.tensor_copy(
                    T["xfmT"][:, _h * 512:(_h + 1) * 512], T["psB"][:, 0:512])),
                      reads=["psB"], writes=["xfmT"])
            for c in range(8):
                mm(T["psR"][:, 0:8], "psR", T["xfmT"][:, c * 128:(c + 1) * 128],
                   T["rwcT"][:, c * 8:(c + 1) * 8], c == 0, c == 7, ["xfmT", "rwcT"])
            pr.op('v', lambda e: e.tensor_copy(T["rtok"][:, :], T["psR"][:, 0:8]),
                  reads=["psR"], writes=["rtok"])
            pr.op('v', lambda e: e.max(T["mx8"][:, :], T["rtok"][:, :]),
                  reads=["rtok"], writes=["mx8"])
            pr.op('v', lambda e: e.max_index(T["mi8"][:, :], T["mx8"][:, :],
                                             T["rtok"][:, :]),
                  reads=["mx8", "rtok"], writes=["mi8"])
            pr.op('v', (lambda e, _col=col: e.tensor_copy(
                T["evsb"][:, _col:_col + 1], T["mi8"][:, 0:1])),
                  reads=["mi8"], writes=["evsb"])

    for e8 in range(8):
        pr.op('v', (lambda e, _e=e8: e.tensor_single_scalar(
            T["EQ"][:, _e * 256:(_e + 1) * 256], T["evsb"][:, :], float(_e),
            ALU.is_equal)),
              reads=["evsb"], writes=["EQ"])
    for q in range(4):
        mm(T["psA"][0:1, 0:512], "psA", T["onescol"][:, 0:1],
           T["EQ"][:, q * 512:(q + 1) * 512], True, True, ["onescol", "EQ"])
        pr.op('v', (lambda e, _q=q: e.tensor_copy(
            T["cntrow"][0:1, _q * 512:(_q + 1) * 512], T["psA"][0:1, 0:512])),
              reads=["psA"], writes=["cntrow"])
        mm(T["psB"][:, 0:512], "psB", T["tri"][:, :],
           T["EQ"][:, q * 512:(q + 1) * 512], True, True, ["tri", "EQ"])
        pr.op('v', (lambda e, _q=q: e.tensor_copy(
            T["rank"][:, _q * 512:(_q + 1) * 512], T["psB"][:, 0:512])),
              reads=["psB"], writes=["rank"])
    pr.op('g', lambda e: e.memset(T["ones8"][:, :], 1.0), writes=["ones8"])
    pr.dma('y', T["cnt8"][0:8, 0:256], T["cntrow"][0:1, :].rearrange("o (n t) -> o n t", t=256))
    pr.op('v', lambda e: e.tensor_tensor_scan(T["pfx"][0:8, 0:256], T["ones8"][0:8, 0:256],
                                              T["cnt8"][0:8, 0:256], 0.0, ALU.mult, ALU.add),
          reads=["cnt8", "ones8"], writes=["pfx"])
    pr.op('v', lambda e: e.tensor_sub(T["pfx"][0:8, 0:256], T["pfx"][0:8, 0:256],
                                      T["cnt8"][0:8, 0:256]),
          reads=["pfx", "cnt8"], writes=["pfx"])
    pr.dma('y', T["pfxrow"][0:1, :].rearrange("o (n t) -> o n t", t=256), T["pfx"][0:8, 0:256])
    for e8 in range(8):
        mm(T["psC"][:, 0:256], "psC", T["ones1f"][0:1, :],
           T["pfxrow"][0:1, e8 * 256:(e8 + 1) * 256], True, True, ["pfxrow", "ones1f"])
        pr.op('v', lambda e: e.tensor_copy(T["brd"][:, 0:256], T["psC"][:, 0:256]),
              reads=["psC"], writes=["brd"])
        if e8 == 0:
            pr.op('v', lambda e: e.tensor_mul(T["base"][:, :], T["EQ"][:, 0:256],
                                              T["brd"][:, :]),
                  reads=["EQ", "brd"], writes=["base"])
            pr.op('v', lambda e: e.tensor_mul(T["rtk"][:, :], T["EQ"][:, 0:256],
                                              T["rank"][:, 0:256]),
                  reads=["EQ", "rank"], writes=["rtk"])
        else:
            pr.op('v', (lambda e, _e=e8: e.tensor_mul(T["tmpb"][:, :],
                                                      T["EQ"][:, _e * 256:(_e + 1) * 256],
                                                      T["brd"][:, :])),
                  reads=["EQ", "brd"], writes=["tmpb"])
            pr.op('v', lambda e: e.tensor_add(T["base"][:, :], T["base"][:, :],
                                              T["tmpb"][:, :]),
                  reads=["base", "tmpb"], writes=["base"])
            pr.op('v', (lambda e, _e=e8: e.tensor_mul(T["tmpb"][:, :],
                                                      T["EQ"][:, _e * 256:(_e + 1) * 256],
                                                      T["rank"][:, _e * 256:(_e + 1) * 256])),
                  reads=["EQ", "rank"], writes=["tmpb"])
            pr.op('v', lambda e: e.tensor_add(T["rtk"][:, :], T["rtk"][:, :], T["tmpb"][:, :]),
                  reads=["rtk", "tmpb"], writes=["rtk"])
    pr.op('v', lambda e: e.tensor_add(T["postk"][:, :], T["base"][:, :], T["rtk"][:, :]),
          reads=["base", "rtk"], writes=["postk"])
    pr.op('g', lambda e: e.iota(T["tokid"][:, :], pattern=[[1, 256]], base=0,
                                channel_multiplier=0), writes=["tokid"])
    pr.op('v', lambda e: e.tensor_copy(T["iotf"][:, :], T["tokid"][:, :]),
          reads=["tokid"], writes=["iotf"])
    pr.op('v', lambda e: e.tensor_scalar(T["colsel"][0:8, :], T["iotf"][0:8, :],
                                         T["myc32"][0:8, 0:1], None, ALU.is_equal),
          reads=["iotf", "myc32"], writes=["colsel"])
    pr.op('v', lambda e: e.tensor_mul(T["tmp8"][0:8, :], T["pfx"][0:8, :],
                                      T["colsel"][0:8, :]),
          reads=["pfx", "colsel"], writes=["tmp8"])
    pr.op('v', lambda e: e.reduce_sum(T["base8"][0:8, 0:1], T["tmp8"][0:8, :], AX.X),
          reads=["tmp8"], writes=["base8"])
    pr.dma('y', T["base8row"][0:1, :].rearrange("o (n t) -> o n t", t=1),
           T["base8"][0:8, 0:1])
    for e8 in range(8):
        mm(T["psC"][:, 0:1], "psC", T["ones1f"][0:1, :],
           T["base8row"][0:1, e8:e8 + 1], True, True, ["base8row", "ones1f"])
        pr.op('v', lambda e: e.tensor_copy(T["brdcol"][:, 0:1], T["psC"][:, 0:1]),
              reads=["psC"], writes=["brdcol"])
        if e8 == 0:
            pr.op('v', lambda e: e.tensor_scalar(T["baseTok"][:, :], T["EQ"][:, 0:256],
                                                 T["brdcol"][:, 0:1], None, ALU.mult),
                  reads=["EQ", "brdcol"], writes=["baseTok"])
        else:
            pr.op('v', (lambda e, _e=e8: e.tensor_scalar(
                T["tmpb"][:, :], T["EQ"][:, _e * 256:(_e + 1) * 256],
                T["brdcol"][:, 0:1], None, ALU.mult)),
                  reads=["EQ", "brdcol"], writes=["tmpb"])
            pr.op('v', lambda e: e.tensor_add(T["baseTok"][:, :], T["baseTok"][:, :],
                                              T["tmpb"][:, :]),
                  reads=["baseTok", "tmpb"], writes=["baseTok"])
    pr.op('v', lambda e: e.tensor_sub(T["lr"][:, :], T["postk"][:, :],
                                      T["baseTok"][:, :]),
          reads=["postk", "baseTok"], writes=["lr"])
    pr.op('v', lambda e: e.tensor_single_scalar(T["slotf"][:, :], T["evsb"][:, :],
                                                float(CLOC), ALU.mult),
          reads=["evsb"], writes=["slotf"])
    pr.op('v', lambda e: e.tensor_add(T["slotf"][:, :], T["slotf"][:, :], T["lr"][:, :]),
          reads=["slotf", "lr"], writes=["slotf"])
    pr.op('v', lambda e: e.tensor_single_scalar(T["kgm"][:, :], T["postk"][:, :],
                                                6144.0, ALU.is_lt),
          reads=["postk"], writes=["kgm"])
    pr.op('v', lambda e: e.tensor_single_scalar(T["klm"][:, :], T["lr"][:, :],
                                                float(CLOC) - 0.5, ALU.is_lt),
          reads=["lr"], writes=["klm"])
    pr.op('v', lambda e: e.tensor_mul(T["keepA"][:, :], T["kgm"][:, :], T["klm"][:, :]),
          reads=["kgm", "klm"], writes=["keepA"])
    pr.op('v', lambda e: e.tensor_single_scalar(T["gidxf"][:, :], T["slotf"][:, :],
                                                float(NSLOT), ALU.subtract),
          reads=["slotf"], writes=["gidxf"])
    pr.op('v', lambda e: e.tensor_mul(T["gidxf"][:, :], T["gidxf"][:, :],
                                      T["keepA"][:, :]),
          reads=["gidxf", "keepA"], writes=["gidxf"])
    pr.op('v', lambda e: e.tensor_single_scalar(T["gidxf"][:, :], T["gidxf"][:, :],
                                                float(NSLOT), ALU.add),
          reads=["gidxf"], writes=["gidxf"])
    dd = getattr(pr, "dbgdump", None)
    if dd is not None:
        for i, nm in enumerate(("evsb", "postk", "baseTok", "lr", "kgm", "klm",
                                "slotf", "gidxf")):
            pr.dma('y', dd[i * 128:(i + 1) * 128, :], T[nm][:, :])
    for h in range(2):
        tr(T["psA"][:, 0:128], "psA", T["gidxf"][:, h * 128:(h + 1) * 128], ["gidxf"])
        pr.op('v', lambda e: e.tensor_copy(T["gstg"][:, :], T["psA"][:, 0:128]),
              reads=["psA"], writes=["gstg"])
        pr.dma('y', gidxT[h * 128:(h + 1) * 128, :], T["gstg"][:, :])
    pr.op('v', lambda e: e.tensor_scalar(T["dcol"][:, :], T["iotf"][:, :],
                                         T["myc32"][:, 0:1], None, ALU.subtract),
          reads=["iotf", "myc32"], writes=["dcol"])
    pr.op('v', lambda e: e.tensor_single_scalar(T["mOwn"][:, :], T["dcol"][:, :], -0.5,
                                                ALU.is_gt),
          reads=["dcol"], writes=["mOwn"])
    pr.op('v', lambda e: e.tensor_single_scalar(T["tmpb"][:, :], T["dcol"][:, :], 31.5,
                                                ALU.is_lt),
          reads=["dcol"], writes=["tmpb"])
    pr.op('v', lambda e: e.tensor_mul(T["mOwn"][:, :], T["mOwn"][:, :], T["tmpb"][:, :]),
          reads=["mOwn", "tmpb"], writes=["mOwn"])
    pr.op('v', lambda e: e.tensor_mul(T["keepA"][:, :], T["keepA"][:, :],
                                      T["mOwn"][:, :]),
          reads=["keepA", "mOwn"], writes=["keepA"])
    pr.op('v', lambda e: e.tensor_single_scalar(T["mu8b"][:, :], T["keepA"][:, :], 0.5,
                                                ALU.is_gt),
          reads=["keepA"], writes=["mu8b"])
    pr.op('g', lambda e: e.iota(T["tokid"][:, :], pattern=[[128, 256]], base=0,
                                channel_multiplier=1), writes=["tokid"])
    pr.op('v', lambda e: e.tensor_copy(T["tokidf"][:, :], T["tokid"][:, :]),
          reads=["tokid"], writes=["tokidf"])
    pr.op('v', lambda e: e.tensor_scalar_mul(T["brdcol"][:, 0:1], T["mycol"][:, 0:1],
                                             4096.0),
          reads=["mycol"], writes=["brdcol"])
    pr.op('v', lambda e: e.tensor_scalar(T["tokidf"][:, :], T["tokidf"][:, :],
                                         T["brdcol"][:, 0:1], None, ALU.subtract),
          reads=["tokidf", "brdcol"], writes=["tokidf"])
    pr.op('v', lambda e: e.tensor_copy(T["tokid"][:, :], T["tokidf"][:, :]),
          reads=["tokidf"], writes=["tokid"])
    pr.op('v', lambda e: e.tensor_copy(T["slot"][:, :], T["slotf"][:, :]),
          reads=["slotf"], writes=["slot"])
    pr.op('v', lambda e: e.tensor_single_scalar(T["sh"][:, :], T["slot"][:, :], 7,
                                                ALU.arith_shift_right),
          reads=["slot"], writes=["sh"])
    pr.op('v', lambda e: e.tensor_single_scalar(T["rem"][:, :], T["sh"][:, :], 7,
                                                ALU.arith_shift_left),
          reads=["sh"], writes=["rem"])
    pr.op('v', lambda e: e.tensor_sub(T["rem"][:, :], T["slot"][:, :], T["rem"][:, :]),
          reads=["slot", "rem"], writes=["rem"])
    pr.op('v', lambda e: e.tensor_single_scalar(T["rem"][:, :], T["rem"][:, :], 64,
                                                ALU.mult),
          reads=["rem"], writes=["rem"])
    pr.op('v', lambda e: e.tensor_add(T["tgt"][:, :], T["rem"][:, :], T["sh"][:, :]),
          reads=["rem", "sh"], writes=["tgt"])
    pr.op('g', lambda e: e.memset(T["trash"][:, :], NSLOT), writes=["trash"])
    pr.op('v', lambda e: e.select(T["tgt2"][:, :], T["mu8b"][:, :], T["tgt"][:, :],
                                  T["trash"][:, :]),
          reads=["tgt", "mu8b", "trash"], writes=["tgt2"])
    pr.op('g', lambda e: e.memset(T["fill"][:, :], 4096), writes=["fill"])
    pr.dma('g', srcd[0:NSLOT, 0:1].rearrange("(p j) o -> p (j o)", p=128),
           T["fill"][:, :])
    for j in range(256):
        pr.idma_scatter(srcd[:, :], T["tgt2"][:, j:j + 1], T["tokid"][:, j:j + 1])
    pr.dma('g', T["srcidx"][:, :],
           srcd[0:NSLOT, 0:1].rearrange("(p j) o -> p (j o)", p=128))

    for ex in range(8):
        for hf in range(2):
            pr.dma('g', T["w1sb%d" % hf][:, :].rearrange("p (c f) -> p c f", f=2048),
                   w1[ex].rearrange("(c p) (h f) -> p c h f", p=128, h=2)[:, :, hf, :])
        pr.dma('y', T["b1sb"][:, :], b1[ex, :].rearrange("(c p) -> p c", p=128),
               allow_slow_non_contiguous=True)
        pr.dma('y', T["b2sb"][:, :], b2[ex, :].rearrange("(c p) -> p c", p=128),
               allow_slow_non_contiguous=True)
        for g2 in range(2):
            g = ex * 2 + g2
            for tt in range(4):
                jj = g * 4 + tt
                pr.idma_gather(T["gbuf"][:, :], xnf[:, :], T["srcidx"][:, jj:jj + 1])
                pr.op('s', lambda e: e.copy(T["gbf"][:, :], T["gbuf"][:, :]),
                      reads=["gbuf"], writes=["gbf"])
                for c in range(8):
                    tr(T["psT"][0:128, 0:128], "psT",
                       T["gbf"][:, c * 128:(c + 1) * 128], ["gbf"], bf=True)
                    pr.op('v', (lambda e, _c=c, _tt=tt: e.tensor_copy(
                        T["bufT"][:, _c * 512 + _tt * 128:_c * 512 + _tt * 128 + 128],
                        T["psT"][0:128, 0:128])),
                          reads=["psT"], writes=["bufT"])
            for f in range(32):
                w1b = "w1sb%d" % (f // 16)
                fo = f % 16
                for c in range(8):
                    mm(T["psD"][:, 0:512], "psD",
                       T[w1b][:, c * 2048 + fo * 128:c * 2048 + (fo + 1) * 128],
                       T["bufT"][:, c * 512:(c + 1) * 512], c == 0, c == 7,
                       [w1b, "bufT"])
                pr.op('s', (lambda e, _f=f: e.activation(
                    T["hbf"][:, _f * 512:(_f + 1) * 512], T["psD"][:, 0:512], AF.Gelu,
                    bias=T["b1sb"][:, _f:_f + 1])),
                      reads=["psD", "b1sb"], writes=["hbf"])
            for m in range(8):
                w2b = "w2b0" if m % 2 == 0 else "w2b1"
                pr.dma('g', T[w2b][:, :].rearrange("p (c k) -> p c k", k=128),
                       w2[ex].rearrange("(c p) (m k) -> p c m k", p=128, m=8)[:, :, m, :])
                for fc in range(32):
                    mm(T["psE"][:, 0:512], "psE",
                       T[w2b][:, fc * 128:(fc + 1) * 128],
                       T["hbf"][:, fc * 512:(fc + 1) * 512], fc == 0, fc == 31,
                       [w2b, "hbf"])
                pr.op('v', (lambda e, _m=m: e.tensor_scalar(
                    T["eofm"][:, 0:512], T["psE"][:, 0:512], T["b2sb"][:, _m:_m + 1],
                    None, ALU.add)),
                      reads=["psE", "b2sb"], writes=["eofm"])
                for st in range(4):
                    tr(T["psA"][:, 0:128], "psA", T["eofm"][:, st * 128:(st + 1) * 128],
                       ["eofm"])
                    pr.op('v', lambda e: e.tensor_copy(T["estage"][:, :],
                                                       T["psA"][:, 0:128]),
                          reads=["psA"], writes=["estage"])
                    pr.dma('y', eo[g * 512 + st * 128:g * 512 + (st + 1) * 128,
                                   m * 128:(m + 1) * 128], T["estage"][:, :])

    pr.final_waits('y')
    pr.finalize()
    return nc




def build_launchC(upto="all", dbg=False):
    nc = bass.Bass("TRN2", target_bir_lowering=False)
    x = nc.dram_tensor("x", [2, 2048, 1024], F32, kind="ExternalInput").ap()
    s_fm = nc.dram_tensor("s_fm", [2, 1024, 2048], F32, kind="ExternalInput").ap()
    eof = nc.dram_tensor("eof", [NSLOT + 1, 1024], F32, kind="ExternalInput").ap()
    route = nc.dram_tensor("route", [4096, 2], F32, kind="ExternalInput").ap()
    gidxT = nc.dram_tensor("gidxT", [256, 128], F32, kind="ExternalInput").ap()
    myexp = nc.dram_tensor("myexp", [1, 1], F32, kind="ExternalInput").ap()
    Wq = nc.dram_tensor("Wq", [1024, 1024], F32, kind="ExternalInput").ap()
    Wk = nc.dram_tensor("Wk", [1024, 1024], F32, kind="ExternalInput").ap()
    Wv = nc.dram_tensor("Wv", [1024, 1024], F32, kind="ExternalInput").ap()
    Wo = nc.dram_tensor("Wo", [1024, 1024], F32, kind="ExternalInput").ap()
    bo = nc.dram_tensor("bo", [1024], F32, kind="ExternalInput").ap()

    x1_fm = nc.dram_tensor("x1_fm", [2, 1024, 2048], F32, kind="ExternalOutput").ap()
    xn1_fm = nc.dram_tensor("xn1_fm", [2, 1024, 2048], F32, kind="ExternalOutput").ap()
    s1_fm = nc.dram_tensor("s1_fm", [2, 1024, 2048], F32, kind="ExternalOutput").ap()
    qscr = nc.dram_tensor("qscr", [2048, 1024], F32).ap()
    if dbg:
        pr_dump = nc.dram_tensor("dbgdump", [1024, 256], F32, kind="ExternalOutput").ap()
    else:
        pr_dump = None
    if dbg:
        dbg_att = nc.dram_tensor("dbg_att", [2, 16, 16], F32, kind="ExternalOutput").ap()
        dbg_e = nc.dram_tensor("dbg_e", [2, 16, 16], F32, kind="ExternalOutput").ap()
        dbg_q = nc.dram_tensor("dbg_q", [2, 128, 1024], F32, kind="ExternalOutput").ap()
        dbg_out = nc.dram_tensor("dbg_out", [2, 128, 2048], F32, kind="ExternalOutput").ap()
        dbg_qf = nc.dram_tensor("dbg_qf", [2, 128, 1024], F32, kind="ExternalOutput").ap()

    pr = Pr(nc)
    T = pr.T
    sbufs = [
        ("idf", [128, 128], F32), ("idb", [128, 128], BF16), ("ioti", [128, 128], I32),
        ("onescol", [128, 1], BF16), ("ones1f", [1, 128], F32), ("onerow64", [1, 64], F32),
        ("routeb", [128, 64], F32), ("gidx", [128, 32], I32),
        ("gidxf32", [128, 32], F32), ("mycol", [128, 1], F32),
        ("idxf", [128, 1], F32), ("idxi", [128, 1], I32), ("gaT", [128, 128], F32),
        ("xtok", [128, 1024], F32), ("grow", [128, 1024], F32),
        ("xft", [128, 1024], F32), ("gft", [128, 1024], F32), ("stp", [128, 1024], F32),
        ("x1t", [128, 1024], F32), ("x1bf", [128, 1024], BF16),
        ("xn1t", [128, 1024], F32),
        ("srow", [1, 512], F32), ("rsrep", [128, 128], F32), ("mrsrep", [128, 128], F32),
        ("vstg0", [128, 2048], BF16), ("vstg1", [128, 2048], BF16),
        ("wpbh", [128, 8192], BF16), ("wpbl", [128, 8192], BF16),
        ("wstg", [128, 1024], F32),
        ("xf32t", [128, 1024], F32), ("xht", [128, 1024], BF16), ("xlt", [128, 1024], BF16),
        ("q32", [128, 1024], F32), ("k32", [128, 1024], F32),
        ("Racc", [128, 128], F32),
        ("outfm", [128, 16 * 1024], BF16), ("ktok", [128, 16 * 1024], BF16),
        ("maskG", [128, 1024], F32),
        ("Gm", [128, 1024], F32), ("Rm", [128, 16], F32),
        ("bd2", [128, 2], F32), ("ep2", [2, 16], F32), ("esb", [16, 16], F32),
        ("attrow", [1, 256], F32), ("mxr", [16, 1], F32), ("negmax", [16, 1], F32),
        ("esum", [16, 1], F32), ("attsb", [16, 16], F32), ("ipat", [128, 128], F32),
        ("srw", [1, 128], F32), ("Smk", [128, 128], BF16),
        ("wo2", [128, 16 * 1024], BF16), ("bocol", [128, 8], F32),
        ("s1sb", [128, 1024], F32),
    ]
    psums = [
        ("psA", [128, 512], F32), ("psB", [128, 512], F32),
        ("psC", [128, 1024], F32), ("psD", [128, 2048], F32),
    ]
    pr.alloc(sbufs, psums)
    c18 = pr.constcol(0.125)

    def mm(ps_ap, ps_key, lhsT, rhs, start, stop, reads, tp=None):
        pr.op('t', (lambda e, _p=ps_ap, _l=lhsT, _r=rhs, _s=start, _e2=stop, _tp=tp:
                    e.matmul(_p, _l, _r, start=_s, stop=_e2, tile_position=_tp)),
              reads=reads, writes=[ps_key])

    def tr(ps_ap, ps_key, in_ap, reads, pin=128, bf=False):
        ident = (T["idb"] if bf else T["idf"])[0:pin, 0:pin]
        pr.op('t', (lambda e, _p=ps_ap, _i=in_ap, _id=ident: e.transpose(_p, _i, _id)),
              reads=list(reads) + ["idb" if bf else "idf"], writes=[ps_key])

    pr.op('g', lambda e: e.iota(T["ioti"][:, :], pattern=[[-1, 128]], base=0,
                                channel_multiplier=1), writes=["ioti"])
    pr.op('v', lambda e: e.tensor_copy(T["idf"][:, :], T["ioti"][:, :]),
          reads=["ioti"], writes=["idf"])
    pr.op('v', lambda e: e.tensor_single_scalar(T["idf"][:, :], T["idf"][:, :], 0.0,
                                                ALU.is_equal),
          reads=["idf"], writes=["idf"])
    pr.op('v', lambda e: e.tensor_copy(T["idb"][:, :], T["idf"][:, :]),
          reads=["idf"], writes=["idb"])
    pr.op('g', lambda e: e.memset(T["onescol"][:, :], 1.0), writes=["onescol"])
    pr.op('g', lambda e: e.memset(T["ones1f"][:, :], 1.0), writes=["ones1f"])
    pr.op('g', lambda e: e.memset(T["onerow64"][:, :], 1.0), writes=["onerow64"])
    pr.op('v', lambda e: e.tensor_copy(T["Gm"][:, 0:128], T["ioti"][:, :]),
          reads=["ioti"], writes=["Gm"])
    pr.op('v', lambda e: e.tensor_single_scalar(T["ipat"][:, :], T["Gm"][:, 0:128], 0.0,
                                                ALU.is_equal),
          reads=["Gm"], writes=["ipat"])
    pr.op('v', lambda e: e.tensor_single_scalar(T["Gm"][:, 128:256], T["Gm"][:, 0:128],
                                                64.0, ALU.is_equal),
          reads=["Gm"], writes=["Gm"])
    pr.op('v', lambda e: e.tensor_add(T["ipat"][:, :], T["ipat"][:, :],
                                      T["Gm"][:, 128:256]),
          reads=["ipat", "Gm"], writes=["ipat"])
    pr.op('v', lambda e: e.tensor_single_scalar(T["Gm"][:, 128:256], T["Gm"][:, 0:128],
                                                -64.0, ALU.is_equal),
          reads=["Gm"], writes=["Gm"])
    pr.op('v', lambda e: e.tensor_add(T["ipat"][:, :], T["ipat"][:, :],
                                      T["Gm"][:, 128:256]),
          reads=["ipat", "Gm"], writes=["ipat"])
    for bb in range(16):
        pr.op('v', (lambda e, _b=bb: e.tensor_copy(
            T["maskG"][:, _b * 64:(_b + 1) * 64], T["ipat"][:, 0:64])),
              reads=["ipat"], writes=["maskG"])
    pr.op('g', lambda e: e.memset(T["bd2"][:, :], 0.0), writes=["bd2"])
    pr.op('g', lambda e: e.memset(T["bd2"][0:64, 0:1], 1.0), writes=["bd2"])
    pr.op('g', lambda e: e.memset(T["bd2"][64:128, 1:2], 1.0), writes=["bd2"])
    pr.dma('y', T["bocol"][:, :], bo[:].rearrange("(c p) -> p c", p=128),
           allow_slow_non_contiguous=True)
    for par in range(2):
        pr.dma('g', T["wo2"][par * 64:(par + 1) * 64, :]
               .rearrange("p (r k) -> p r k", k=1024),
               Wo[:].rearrange("(r p) k -> p r k", p=64))
    pr.dma('y', T["routeb"][:, :].rearrange("p (j c) -> p j c", c=2),
           route[:].rearrange("(j p) c -> p j c", p=128))
    pr.dma('y', T["mycol"][0:1, 0:1], myexp[0:1, 0:1])
    mm(T["psA"][:, 0:1], "psA", T["ones1f"][0:1, :], T["mycol"][0:1, 0:1], True, True,
       ["mycol", "ones1f"])
    pr.op('v', lambda e: e.tensor_copy(T["mycol"][:, 0:1], T["psA"][:, 0:1]),
          reads=["psA"], writes=["mycol"])
    pr.op('v', lambda e: e.tensor_scalar_mul(T["mycol"][:, 0:1], T["mycol"][:, 0:1],
                                             32.0),
          reads=["mycol"], writes=["mycol"])
    pr.op('g', lambda e: e.iota(T["idxi"][:, 0:1], pattern=[[1, 1]], base=0,
                                channel_multiplier=1), writes=["idxi"])
    pr.op('v', lambda e: e.tensor_copy(T["idxf"][:, 0:1], T["idxi"][:, 0:1]),
          reads=["idxi"], writes=["idxf"])
    pr.op('v', lambda e: e.tensor_add(T["idxf"][:, 0:1], T["idxf"][:, 0:1],
                                      T["mycol"][:, 0:1]),
          reads=["idxf", "mycol"], writes=["idxf"])
    pr.op('v', lambda e: e.tensor_copy(T["idxi"][:, 0:1], T["idxf"][:, 0:1]),
          reads=["idxf"], writes=["idxi"])
    pr.idma_gather(T["gaT"][:, :], gidxT[:, :], T["idxi"][:, 0:1])
    tr(T["psA"][:, 0:128], "psA", T["gaT"][:, :], ["gaT"])
    pr.op('v', lambda e: e.tensor_copy(T["gidx"][:, :], T["psA"][:, 0:32]),
          reads=["psA"], writes=["gidx"])

    for b in range(2):
        for jt in range(16):
            jj = b * 16 + jt
            pr.dma('y', T["xtok"][:, :], x[b, jt * 128:(jt + 1) * 128, :])
            pr.idma_gather(T["grow"][:, :], eof[:, :], T["gidx"][:, jj:jj + 1])
            pr.op('s', (lambda e, _jj=jj: e.activation(
                T["grow"][:, :], T["grow"][:, :], AF.Silu,
                scale=T["routeb"][:, 2 * _jj + 1:2 * _jj + 2])),
                  reads=["grow", "routeb"], writes=["grow"])
            for c in range(8):
                tr(T["psA"][:, 0:128], "psA", T["xtok"][:, c * 128:(c + 1) * 128], ["xtok"])
                pr.op('v', (lambda e, _c=c: e.tensor_copy(
                    T["xft"][:, _c * 128:(_c + 1) * 128], T["psA"][:, 0:128])),
                      reads=["psA"], writes=["xft"])
                tr(T["psB"][:, 0:128], "psB", T["grow"][:, c * 128:(c + 1) * 128], ["grow"])
                pr.op('v', (lambda e, _c=c: e.tensor_copy(
                    T["gft"][:, _c * 128:(_c + 1) * 128], T["psB"][:, 0:128])),
                      reads=["psB"], writes=["gft"])
            pr.dma('y', T["stp"][:, :].rearrange("p (c q) -> p c q", q=128),
                   s_fm[b, :, jt * 128:(jt + 1) * 128]
                   .rearrange("(c p) q -> p c q", p=128))
            pr.op('v', lambda e: e.tensor_mul(T["gft"][:, :], T["gft"][:, :], T["stp"][:, :]),
                  reads=["gft", "stp"], writes=["gft"])
            pr.op('v', lambda e: e.tensor_add(T["x1t"][:, :], T["xft"][:, :], T["gft"][:, :]),
                  reads=["xft", "gft"], writes=["x1t"])
            pr.dma('y', x1_fm[b, :, jt * 128:(jt + 1) * 128]
                   .rearrange("(c p) q -> p c q", p=128),
                   T["x1t"][:, :].rearrange("p (c q) -> p c q", q=128))
            pr.op('s', lambda e: e.copy(T["x1bf"][:, :], T["x1t"][:, :]),
                  reads=["x1t"], writes=["x1bf"])
            for c in range(8):
                mm(T["psA"][0:1, 0:128], "psA", T["onescol"][:, 0:1],
                   T["x1bf"][:, c * 128:(c + 1) * 128], c == 0, c == 7,
                   ["onescol", "x1bf"])
            pr.op('s', lambda e: e.activation(T["x1bf"][:, :], T["x1bf"][:, :], AF.Square),
                  reads=["x1bf"], writes=["x1bf"])
            for c in range(8):
                mm(T["psB"][0:1, 0:128], "psB", T["onescol"][:, 0:1],
                   T["x1bf"][:, c * 128:(c + 1) * 128], c == 0, c == 7,
                   ["onescol", "x1bf"])
            pr.op('v', lambda e: e.tensor_scalar_mul(T["srow"][0:1, 0:128],
                                                     T["psA"][0:1, 0:128], 1.0 / 1024),
                  reads=["psA"], writes=["srow"])
            pr.op('v', lambda e: e.tensor_scalar_mul(T["srow"][0:1, 128:256],
                                                     T["psB"][0:1, 0:128], 1.0 / 1024),
                  reads=["psB"], writes=["srow"])
            pr.op('v', lambda e: e.tensor_mul(T["srow"][0:1, 256:384],
                                              T["srow"][0:1, 0:128], T["srow"][0:1, 0:128]),
                  reads=["srow"], writes=["srow"])
            pr.op('v', lambda e: e.tensor_sub(T["srow"][0:1, 128:256],
                                              T["srow"][0:1, 128:256],
                                              T["srow"][0:1, 256:384]),
                  reads=["srow"], writes=["srow"])
            pr.op('v', lambda e: e.tensor_scalar_add(T["srow"][0:1, 128:256],
                                                     T["srow"][0:1, 128:256], 1e-5),
                  reads=["srow"], writes=["srow"])
            pr.op('s', lambda e: e.activation(T["srow"][0:1, 128:256],
                                              T["srow"][0:1, 128:256], AF.Ln),
                  reads=["srow"], writes=["srow"])
            pr.op('v', lambda e: e.tensor_scalar_mul(T["srow"][0:1, 128:256],
                                                     T["srow"][0:1, 128:256], -0.5),
                  reads=["srow"], writes=["srow"])
            pr.op('s', lambda e: e.activation(T["srow"][0:1, 128:256],
                                              T["srow"][0:1, 128:256], AF.Exp),
                  reads=["srow"], writes=["srow"])
            pr.op('v', lambda e: e.tensor_mul(T["srow"][0:1, 0:128], T["srow"][0:1, 0:128],
                                              T["srow"][0:1, 128:256]),
                  reads=["srow"], writes=["srow"])
            mm(T["psC"][:, 0:128], "psC", T["ones1f"][0:1, :], T["srow"][0:1, 128:256],
               True, True, ["srow", "ones1f"])
            pr.op('v', lambda e: e.tensor_copy(T["rsrep"][:, :], T["psC"][:, 0:128]),
                  reads=["psC"], writes=["rsrep"])
            mm(T["psC"][:, 0:128], "psC", T["ones1f"][0:1, :], T["srow"][0:1, 0:128],
               True, True, ["srow", "ones1f"])
            pr.op('v', lambda e: e.tensor_copy(T["mrsrep"][:, :], T["psC"][:, 0:128]),
                  reads=["psC"], writes=["mrsrep"])
            for c in range(8):
                pr.op('v', (lambda e, _c=c: e.tensor_mul(
                    T["xn1t"][:, _c * 128:(_c + 1) * 128],
                    T["x1t"][:, _c * 128:(_c + 1) * 128], T["rsrep"][:, :])),
                      reads=["x1t", "rsrep"], writes=["xn1t"])
                pr.op('v', (lambda e, _c=c: e.tensor_sub(
                    T["xn1t"][:, _c * 128:(_c + 1) * 128],
                    T["xn1t"][:, _c * 128:(_c + 1) * 128], T["mrsrep"][:, :])),
                      reads=["xn1t", "mrsrep"], writes=["xn1t"])
            pr.dma('y', xn1_fm[b, :, jt * 128:(jt + 1) * 128]
                   .rearrange("(c p) q -> p c q", p=128),
                   T["xn1t"][:, :].rearrange("p (c q) -> p c q", q=128))

        if upto == "combine":
            continue
        pr.op('g', lambda e: e.memset(T["Racc"][:, :], 0.0), writes=["Racc"])
        for (wdram, wkey) in ((Wq, 0), (Wk, 1)):
            for c in range(8):
                pr.dma('y', T["wstg"][:, :], wdram[c * 128:(c + 1) * 128, :])
                pr.op('s', (lambda e, _c=c: e.copy(
                    T["wpbh"][:, _c * 1024:(_c + 1) * 1024], T["wstg"][:, :])),
                      reads=["wstg"], writes=["wpbh"])
                pr.op('v', (lambda e, _c=c: e.tensor_sub(
                    T["xf32t"][:, :], T["wstg"][:, :],
                    T["wpbh"][:, _c * 1024:(_c + 1) * 1024])),
                      reads=["wstg", "wpbh"], writes=["xf32t"])
                pr.op('v', (lambda e, _c=c: e.tensor_copy(
                    T["wpbl"][:, _c * 1024:(_c + 1) * 1024], T["xf32t"][:, :])),
                      reads=["xf32t"], writes=["wpbl"])
            dstps = "psC" if wkey == 0 else "psD"
            dst32 = "q32" if wkey == 0 else "k32"
            for jt in range(16):
                pr.dma('y', T["xf32t"][:, :].rearrange("p (c q) -> p c q", q=128),
                       xn1_fm[b, :, jt * 128:(jt + 1) * 128]
                       .rearrange("(c p) q -> p c q", p=128))
                pr.op('s', lambda e: e.copy(T["xht"][:, :], T["xf32t"][:, :]),
                      reads=["xf32t"], writes=["xht"])
                pr.op('v', lambda e: e.tensor_sub(T["grow"][:, :], T["xf32t"][:, :],
                                                  T["xht"][:, :]),
                      reads=["xf32t", "xht"], writes=["grow"])
                pr.op('v', lambda e: e.tensor_copy(T["xlt"][:, :], T["grow"][:, :]),
                      reads=["grow"], writes=["xlt"])
                for c in range(8):
                    for h2 in range(2):
                        for (sx, sw) in (("xht", "wpbh"), ("xht", "wpbl"),
                                         ("xlt", "wpbh")):
                            mm(T[dstps][:, h2 * 512:(h2 + 1) * 512], dstps,
                               T[sx][:, c * 128:(c + 1) * 128],
                               T[sw][:, c * 1024 + h2 * 512:c * 1024 + (h2 + 1) * 512],
                               (c == 0 and sx == "xht" and sw == "wpbh"), c == 7,
                               [sx, sw])
                pr.op('v', (lambda e, _d=dst32, _p=dstps: e.tensor_copy(
                    T[_d][:, :], T[_p][:, 0:1024])),
                      reads=[dstps], writes=[dst32])
                if wkey == 0:
                    pr.dma('y', qscr[jt * 128:(jt + 1) * 128, :], T["q32"][:, :])
                if wkey == 1:
                    pr.dma('y', T["q32"][:, :], qscr[jt * 128:(jt + 1) * 128, :])
                    for m in range(8):
                        for h2 in range(2):
                            mm(T["psB"][:, 0:512], "psB",
                               T["k32"][:, m * 128:(m + 1) * 128],
                               T["q32"][:, h2 * 512:(h2 + 1) * 512], h2 == 0, False,
                               ["k32", "q32"]) if False else None
                        for h2 in range(2):
                            mm(T["psC"][:, h2 * 512:(h2 + 1) * 512], "psC",
                               T["q32"][:, m * 128:(m + 1) * 128],
                               T["k32"][:, h2 * 512:(h2 + 1) * 512], True, True,
                               ["q32", "k32"])
                        pr.op('v', lambda e: e.tensor_mul(T["Gm"][:, :],
                                                          T["psC"][:, 0:1024],
                                                          T["maskG"][:, :]),
                              reads=["psC", "maskG"], writes=["Gm"])
                        pr.op('v', (lambda e, _m=m: e.tensor_reduce(
                            T["Rm"][:, :],
                            T["Gm"][:, :].rearrange("p (z k) -> p z k", k=64),
                            AX.X, ALU.add)),
                              reads=["Gm"], writes=["Rm"])
                        pr.op('v', (lambda e, _m=m: e.tensor_add(
                            T["Racc"][:, _m * 16:(_m + 1) * 16], 
                            T["Racc"][:, _m * 16:(_m + 1) * 16], T["Rm"][:, :])),
                              reads=["Racc", "Rm"], writes=["Racc"])
        if upto == "qk":
            continue
        for m in range(8):
            if upto == "noattn":
                break
            mm(T["psB"][0:2, 0:16], "psB", T["bd2"][:, :],
               T["Racc"][:, m * 16:(m + 1) * 16], True, True, ["bd2", "Racc"])
            pr.op('v', lambda e: e.tensor_copy(T["ep2"][:, :], T["psB"][0:2, 0:16]),
                  reads=["psB"], writes=["ep2"])
            pr.dma('y', T["esb"][2 * m:2 * m + 2, :], T["ep2"][:, :])
        pr.op('v', lambda e: e.reduce_max(T["mxr"][:, :], T["esb"][:, :], AX.X),
              reads=["esb"], writes=["mxr"])
        pr.op('v', lambda e: e.tensor_scalar_mul(T["negmax"][:, :], T["mxr"][:, :], -0.125),
              reads=["mxr"], writes=["negmax"])
        pr.op('s', lambda e: e.activation(T["attsb"][:, :], T["esb"][:, :], AF.Exp,
                                          bias=T["negmax"][0:16, 0:1],
                                          scale=T[c18][0:16, 0:1],
                                          accum_out=T["esum"][0:16, 0:1]),
              reads=["esb", "negmax", c18], writes=["attsb", "esum"])
        pr.op('v', lambda e: e.reciprocal(T["esum"][:, :], T["esum"][:, :]),
              reads=["esum"], writes=["esum"])
        pr.op('v', lambda e: e.tensor_scalar(T["attsb"][:, :], T["attsb"][:, :],
                                             T["esum"][0:16, 0:1], None, ALU.mult),
              reads=["attsb", "esum"], writes=["attsb"])
        pr.dma('y', T["attrow"][0:1, :].rearrange("o (a z) -> o a z", z=16),
               T["attsb"][0:16, :])
        if dbg:
            pr.dma('y', dbg_att[b, :, :], T["attsb"][0:16, :])
            pr.dma('y', dbg_e[b, :, :], T["esb"][0:16, :])
            pr.dma('g', dbg_q[b, :, :], T["outfm"][:, 0:1024])
        for m in range(8):
            wb = "wpbh"
            pr.dma('g', T[wb][:, 0:1024].rearrange("p (c k) -> p c k", k=128),
                   Wv[:].rearrange("(c p) (m k) -> p c m k", p=128, m=8)[:, :, m, :])
            for c in range(8):
                vs = "vstg0" if c % 2 == 0 else "vstg1"
                pr.dma('g', T[vs][:, :], xn1_fm[b, c * 128:(c + 1) * 128, :])
                for q4 in range(4):
                    mm(T["psD"][:, q4 * 512:(q4 + 1) * 512], "psD",
                       T[wb][:, c * 128:(c + 1) * 128],
                       T[vs][:, q4 * 512:(q4 + 1) * 512],
                       c == 0, c == 7, [vs, wb])
            pr.op('v', (lambda e, _m=m: e.tensor_copy(
                T["ktok"][:, _m * 2048:(_m + 1) * 2048], T["psD"][:, 0:2048])),
                  reads=["psD"], writes=["ktok"])
        for m in range(8):
            if upto in ("noattn", "nogram"):
                break
            for k8 in range(8):
                for rh in range(2):
                    for ch in range(2):
                        idxv = 16 * (2 * m + ch) + 2 * k8 + rh
                        pr.op('v', (lambda e, _rh=rh, _ch=ch, _ix=idxv: e.tensor_scalar(
                            T["srw"][0:1, _ch * 64:(_ch + 1) * 64], T["onerow64"][0:1, :],
                            T["attrow"][0:1, _ix:_ix + 1], None, ALU.mult)),
                              reads=["onerow64", "attrow"], writes=["srw"])
                    mm(T["psB"][rh * 64:(rh + 1) * 64, 0:128], "psB",
                       T["ones1f"][0:1, 0:64], T["srw"][0:1, :], True, True,
                       ["ones1f", "srw"], tp=(0, 64) if rh == 1 else None)
                pr.op('v', lambda e: e.tensor_mul(T["Smk"][:, :], T["ipat"][:, :],
                                                  T["psB"][:, 0:128]),
                      reads=["ipat", "psB"], writes=["Smk"])
                for q4 in range(4):
                    mm(T["psD"][:, q4 * 512:(q4 + 1) * 512], "psD", T["Smk"][:, :],
                       T["ktok"][:, k8 * 2048 + q4 * 512:k8 * 2048 + (q4 + 1) * 512],
                       k8 == 0, k8 == 7, ["Smk", "ktok"])
            pr.op('v', (lambda e, _m=m: e.tensor_copy(
                T["outfm"][:, _m * 2048:(_m + 1) * 2048], T["psD"][:, 0:2048])),
                  reads=["psD"], writes=["outfm"])
        if dbg:
            pr.dma('g', dbg_out[b, :, :], T["outfm"][:, 0:2048])
        for m8 in range(8):
            pr.op('v', (lambda e, _m=m8: e.tensor_copy(
                T["ktok"][:, _m * 2048:(_m + 1) * 2048]
                .rearrange("p (r q) -> p r q", q=128),
                T["outfm"][:, _m * 2048:(_m + 1) * 2048]
                .rearrange("p (q r) -> p q r", r=16).transpose([0, 2, 1]))),
                  reads=["outfm"], writes=["ktok"])
        for dm in range(8):
            if upto in ("noattn", "nogram", "nokron"):
                break
            for a in range(16):
                par = a % 2
                mslab = a // 2
                apos = (a // 2) + par * 8
                for r in range(16):
                    mov = T["ktok"][par * 64:(par + 1) * 64,
                                    mslab * 2048 + r * 128:mslab * 2048 + (r + 1) * 128]
                    sta = T["wo2"][par * 64:(par + 1) * 64,
                                   r * 1024 + dm * 128:r * 1024 + (dm + 1) * 128]
                    mm(T["psD"][:, apos * 128:(apos + 1) * 128], "psD", sta, mov,
                       r == 0, r == 15, ["wo2", "ktok"],
                       tp=(64, 0) if par == 1 else None)
            for half in range(2):
                for a8 in range(8):
                    a = half * 8 + a8
                    apos = (a // 2) + (a % 2) * 8
                    pr.op('v', (lambda e, _dm=dm, _a8=a8, _ap=apos: e.tensor_scalar(
                        T["s1sb"][:, _a8 * 128:(_a8 + 1) * 128],
                        T["psD"][:, _ap * 128:(_ap + 1) * 128],
                        T["bocol"][:, _dm:_dm + 1], None, ALU.add)),
                          reads=["psD", "bocol"], writes=["s1sb"])
                pr.dma('y', s1_fm[b, dm * 128:(dm + 1) * 128,
                                  half * 1024:(half + 1) * 1024], T["s1sb"][:, :])

    pr.final_waits('y')
    pr.finalize()
    return nc




def build_launchD():
    nc = bass.Bass("TRN2", target_bir_lowering=False)
    x1_fm = nc.dram_tensor("x1_fm", [2, 1024, 2048], F32, kind="ExternalInput").ap()
    xn1_fm = nc.dram_tensor("xn1_fm", [2, 1024, 2048], F32, kind="ExternalInput").ap()
    s1_fm = nc.dram_tensor("s1_fm", [2, 1024, 2048], F32, kind="ExternalInput").ap()
    rW1 = nc.dram_tensor("rW1", [512, 1024], F32, kind="ExternalInput").ap()
    rb1 = nc.dram_tensor("rb1", [512], F32, kind="ExternalInput").ap()
    lng = nc.dram_tensor("lng", [512], F32, kind="ExternalInput").ap()
    lnb = nc.dram_tensor("lnb", [512], F32, kind="ExternalInput").ap()
    rW2 = nc.dram_tensor("rW2", [8, 512], F32, kind="ExternalInput").ap()
    rb2 = nc.dram_tensor("rb2", [8], F32, kind="ExternalInput").ap()
    qW = nc.dram_tensor("qW", [8, 1024, 1024], F32, kind="ExternalInput").ap()
    qb = nc.dram_tensor("qb", [8, 1024], F32, kind="ExternalInput").ap()
    kW = nc.dram_tensor("kW", [8, 1024, 1024], F32, kind="ExternalInput").ap()
    kb = nc.dram_tensor("kb", [8, 1024], F32, kind="ExternalInput").ap()
    vW = nc.dram_tensor("vW", [8, 1024, 1024], F32, kind="ExternalInput").ap()
    vb = nc.dram_tensor("vb", [8, 1024], F32, kind="ExternalInput").ap()
    oW = nc.dram_tensor("oW", [8, 1024, 1024], F32, kind="ExternalInput").ap()
    ob = nc.dram_tensor("ob", [8, 1024], F32, kind="ExternalInput").ap()
    out = nc.dram_tensor("out", [2, 2048, 1024], F32, kind="ExternalOutput").ap()
    gscr = nc.dram_tensor("gscr", [2, 8, 2048], F32).ap()
    oacc = nc.dram_tensor("oacc", [2, 1024, 2048], F32, **DK).ap()

    pr = Pr(nc)
    T = pr.T
    sbufs = [
        ("idf", [128, 128], F32), ("ioti", [128, 128], I32),
        ("onescol", [128, 1], BF16), ("ones1f", [1, 128], F32),
        ("xn1bf", [128, 8 * 2048], BF16),
        ("q16", [128, 4 * 2048], BF16), ("k16", [128, 4 * 2048], BF16),
        ("vtok2", [128, 16 * 1040], BF16),
        ("expT", [128, 16 * 512], BF16), ("og", [128, 8192], BF16),
        ("wst0", [128, 4096], BF16),
        ("xstg", [128, 1024], F32), ("xh", [128, 1024], BF16), ("xl", [128, 1024], BF16),
        ("h32", [128, 512], F32), ("hr", [128, 512], F32), ("hrT", [128, 512], F32),
        ("stats", [128, 8], F32), ("probs", [128, 8], F32), ("mx8", [128, 8], F32),
        ("mi8", [128, 8], U32), ("tif", [128, 2], F32), ("iotaf", [128, 8], F32),
        ("eq1", [128, 8], F32), ("eq2", [128, 8], F32), ("gts", [128, 8], F32),
        ("rb1rep", [128, 512], F32), ("grep", [128, 512], F32), ("brep", [128, 512], F32),
        ("rb2rep", [128, 8], F32), ("rw2T", [128, 32], F32),
        ("gflat", [1, 2048], F32), ("obrow", [1, 1024], F32),
        ("divrep", [128, 512], F32), ("zr", [1, 512], F32), ("dvr", [1, 512], F32),
        ("ostg", [128, 512], F32), ("x2st", [128, 2048], F32),
        ("x1c", [128, 2048], F32), ("s1c", [128, 2048], F32),
        ("gt8", [8, 128], F32),
    ]
    psums = [
        ("psC", [128, 512], F32), ("psD", [128, 2048], F32),
        ("psE", [128, 512], F32), ("psF", [128, 512], F32),
        ("psG", [128, 512], F32),
    ]
    pr.alloc(sbufs, psums)
    c18 = pr.constcol(0.125)

    def mm(ps_ap, ps_key, lhsT, rhs, start, stop, reads, tp=None):
        pr.op('t', (lambda e, _p=ps_ap, _l=lhsT, _r=rhs, _s=start, _e2=stop, _tp=tp:
                    e.matmul(_p, _l, _r, start=_s, stop=_e2, tile_position=_tp)),
              reads=reads, writes=[ps_key])

    def tr(ps_ap, ps_key, in_ap, reads, pin=128):
        ident = T["idf"][0:pin, 0:pin]
        pr.op('t', (lambda e, _p=ps_ap, _i=in_ap, _id=ident: e.transpose(_p, _i, _id)),
              reads=list(reads) + ["idf"], writes=[ps_key])

    pr.op('g', lambda e: e.iota(T["ioti"][:, :], pattern=[[-1, 128]], base=0,
                                channel_multiplier=1), writes=["ioti"])
    pr.op('v', lambda e: e.tensor_copy(T["idf"][:, :], T["ioti"][:, :]),
          reads=["ioti"], writes=["idf"])
    pr.op('v', lambda e: e.tensor_single_scalar(T["idf"][:, :], T["idf"][:, :], 0.0,
                                                ALU.is_equal),
          reads=["idf"], writes=["idf"])
    pr.op('g', lambda e: e.memset(T["onescol"][:, :], 1.0), writes=["onescol"])
    pr.op('g', lambda e: e.memset(T["ones1f"][:, :], 1.0), writes=["ones1f"])
    pr.op('g', lambda e: e.memset(T["vtok2"][:, :], 1.0), writes=["vtok2"])
    pr.op('g', lambda e: e.iota(T["ioti"][:, 0:8], pattern=[[1, 8]], base=0,
                                channel_multiplier=0), writes=["ioti"])
    pr.op('v', lambda e: e.tensor_copy(T["iotaf"][:, :], T["ioti"][:, 0:8]),
          reads=["ioti"], writes=["iotaf"])
    pr.dma('y', T["hr"][0:1, 0:512], rb1[:][None, :])
    mm(T["psC"][:, 0:512], "psC", T["ones1f"][0:1, :], T["hr"][0:1, 0:512], True, True,
       ["hr", "ones1f"])
    pr.op('v', lambda e: e.tensor_copy(T["rb1rep"][:, :], T["psC"][:, 0:512]),
          reads=["psC"], writes=["rb1rep"])
    pr.dma('y', T["hr"][0:1, 0:512], lng[:][None, :])
    mm(T["psC"][:, 0:512], "psC", T["ones1f"][0:1, :], T["hr"][0:1, 0:512], True, True,
       ["hr", "ones1f"])
    pr.op('v', lambda e: e.tensor_copy(T["grep"][:, :], T["psC"][:, 0:512]),
          reads=["psC"], writes=["grep"])
    pr.dma('y', T["hr"][0:1, 0:512], lnb[:][None, :])
    mm(T["psC"][:, 0:512], "psC", T["ones1f"][0:1, :], T["hr"][0:1, 0:512], True, True,
       ["hr", "ones1f"])
    pr.op('v', lambda e: e.tensor_copy(T["brep"][:, :], T["psC"][:, 0:512]),
          reads=["psC"], writes=["brep"])
    pr.dma('y', T["stats"][0:1, 0:8], rb2[:][None, :])
    mm(T["psC"][:, 0:8], "psC", T["ones1f"][0:1, :], T["stats"][0:1, 0:8], True, True,
       ["stats", "ones1f"])
    pr.op('v', lambda e: e.tensor_copy(T["rb2rep"][:, :], T["psC"][:, 0:8]),
          reads=["psC"], writes=["rb2rep"])
    pr.dma('y', T["hrT"][0:8, 0:512], rW2[:, :])
    for c4 in range(4):
        tr(T["psC"][:, 0:8], "psC", T["hrT"][0:8, c4 * 128:(c4 + 1) * 128], ["hrT"], pin=8)
        pr.op('v', (lambda e, _c=c4: e.tensor_copy(T["rw2T"][:, _c * 8:(_c + 1) * 8],
                                                   T["psC"][:, 0:8])),
              reads=["psC"], writes=["rw2T"])

    for b in range(2):
        for c2 in range(4):
            pr.dma('y', T["xstg"][:, :], rW1[c2 * 128:(c2 + 1) * 128, :])
            for c in range(8):
                tr(T["psC"][:, 0:128], "psC", T["xstg"][:, c * 128:(c + 1) * 128], ["xstg"])
                pr.op('v', (lambda e, _c=c, _c2=c2: e.tensor_copy(
                    T["hr"][:, 0:128], T["psC"][:, 0:128])),
                      reads=["psC"], writes=["hr"])
                pr.op('s', (lambda e, _c=c, _c2=c2: e.copy(
                    T["og"][:, _c * 512 + _c2 * 128:_c * 512 + _c2 * 128 + 128],
                    T["hr"][:, 0:128])),
                      reads=["hr"], writes=["og"])
                pr.op('v', (lambda e, _c=c, _c2=c2: e.tensor_sub(
                    T["h32"][:, 0:128], T["hr"][:, 0:128],
                    T["og"][:, _c * 512 + _c2 * 128:_c * 512 + _c2 * 128 + 128])),
                      reads=["hr", "og"], writes=["h32"])
                pr.op('v', (lambda e, _c=c, _c2=c2: e.tensor_copy(
                    T["og"][:, 4096 + _c * 512 + _c2 * 128:4096 + _c * 512 + _c2 * 128 + 128],
                    T["h32"][:, 0:128])),
                      reads=["h32"], writes=["og"])
        for jt in range(16):
            pr.dma('y', T["xstg"][:, :].rearrange("p (c q) -> p c q", q=128),
                   xn1_fm[b, :, jt * 128:(jt + 1) * 128]
                   .rearrange("(c p) q -> p c q", p=128))
            pr.op('s', lambda e: e.copy(T["xh"][:, :], T["xstg"][:, :]),
                  reads=["xstg"], writes=["xh"])
            pr.op('v', lambda e: e.tensor_sub(T["x1c"][:, 0:1024], T["xstg"][:, :],
                                              T["xh"][:, :]),
                  reads=["xstg", "xh"], writes=["x1c"])
            pr.op('v', lambda e: e.tensor_copy(T["xl"][:, :], T["x1c"][:, 0:1024]),
                  reads=["x1c"], writes=["xl"])
            for c in range(8):
                for (sx, off) in (("xh", 0), ("xh", 4096), ("xl", 0)):
                    mm(T["psC"][:, 0:512], "psC", T[sx][:, c * 128:(c + 1) * 128],
                       T["og"][:, off + c * 512:off + (c + 1) * 512],
                       (c == 0 and sx == "xh" and off == 0), (c == 7 and sx == "xl"),
                       [sx, "og"])
            pr.op('v', lambda e: e.tensor_add(T["h32"][:, :], T["psC"][:, 0:512],
                                              T["rb1rep"][:, :]),
                  reads=["psC", "rb1rep"], writes=["h32"])
            pr.op('v', lambda e: e.reduce_sum(T["stats"][:, 0:1], T["h32"][:, :], AX.X),
                  reads=["h32"], writes=["stats"])
            pr.op('v', lambda e: e.tensor_scalar_mul(T["stats"][:, 0:1], T["stats"][:, 0:1],
                                                     1.0 / 512),
                  reads=["stats"], writes=["stats"])
            pr.op('s', lambda e: e.activation(T["hr"][:, 0:512], T["h32"][:, :], AF.Square),
                  reads=["h32"], writes=["hr"])
            pr.op('v', lambda e: e.reduce_sum(T["stats"][:, 1:2], T["hr"][:, 0:512], AX.X),
                  reads=["hr"], writes=["stats"])
            pr.op('v', lambda e: e.tensor_scalar_mul(T["stats"][:, 1:2], T["stats"][:, 1:2],
                                                     1.0 / 512),
                  reads=["stats"], writes=["stats"])
            pr.op('v', lambda e: e.tensor_mul(T["stats"][:, 2:3], T["stats"][:, 0:1],
                                              T["stats"][:, 0:1]),
                  reads=["stats"], writes=["stats"])
            pr.op('v', lambda e: e.tensor_sub(T["stats"][:, 1:2], T["stats"][:, 1:2],
                                              T["stats"][:, 2:3]),
                  reads=["stats"], writes=["stats"])
            pr.op('v', lambda e: e.tensor_scalar_add(T["stats"][:, 1:2], T["stats"][:, 1:2],
                                                     1e-5),
                  reads=["stats"], writes=["stats"])
            pr.op('s', lambda e: e.activation(T["stats"][:, 1:2], T["stats"][:, 1:2], AF.Ln),
                  reads=["stats"], writes=["stats"])
            pr.op('v', lambda e: e.tensor_scalar_mul(T["stats"][:, 1:2], T["stats"][:, 1:2],
                                                     -0.5),
                  reads=["stats"], writes=["stats"])
            pr.op('s', lambda e: e.activation(T["stats"][:, 1:2], T["stats"][:, 1:2], AF.Exp),
                  reads=["stats"], writes=["stats"])
            pr.op('v', lambda e: e.tensor_scalar(T["hr"][:, 0:512], T["h32"][:, :],
                                                 T["stats"][:, 0:1], T["stats"][:, 1:2],
                                                 ALU.subtract, ALU.mult),
                  reads=["h32", "stats"], writes=["hr"])
            pr.op('v', lambda e: e.tensor_mul(T["hr"][:, 0:512], T["hr"][:, 0:512],
                                              T["grep"][:, :]),
                  reads=["hr", "grep"], writes=["hr"])
            pr.op('v', lambda e: e.tensor_add(T["hr"][:, 0:512], T["hr"][:, 0:512],
                                              T["brep"][:, :]),
                  reads=["hr", "brep"], writes=["hr"])
            pr.op('v', lambda e: e.tensor_relu(T["hr"][:, 0:512], T["hr"][:, 0:512]),
                  reads=["hr"], writes=["hr"])
            for c4 in range(4):
                tr(T["psC"][:, 0:128], "psC", T["hr"][:, c4 * 128:(c4 + 1) * 128], ["hr"])
                pr.op('v', (lambda e, _c=c4: e.tensor_copy(
                    T["hrT"][:, _c * 128:(_c + 1) * 128], T["psC"][:, 0:128])),
                      reads=["psC"], writes=["hrT"])
            for c4 in range(4):
                mm(T["psE"][0:128, 0:8], "psE", T["hrT"][:, c4 * 128:(c4 + 1) * 128],
                   T["rw2T"][:, c4 * 8:(c4 + 1) * 8], c4 == 0, c4 == 3, ["hrT", "rw2T"])
            pr.op('v', lambda e: e.tensor_add(T["probs"][:, :], T["psE"][:, 0:8],
                                              T["rb2rep"][:, :]),
                  reads=["psE", "rb2rep"], writes=["probs"])
            pr.op('v', lambda e: e.reduce_max(T["stats"][:, 3:4], T["probs"][:, :], AX.X),
                  reads=["probs"], writes=["stats"])
            pr.op('v', lambda e: e.tensor_scalar_mul(T["stats"][:, 3:4], T["stats"][:, 3:4],
                                                     -1.0),
                  reads=["stats"], writes=["stats"])
            pr.op('s', lambda e: e.activation(T["probs"][:, :], T["probs"][:, :], AF.Exp,
                                              bias=T["stats"][:, 3:4],
                                              accum_out=T["stats"][:, 4:5]),
                  reads=["probs", "stats"], writes=["probs", "stats4"])
            pr.op('v', lambda e: e.reciprocal(T["stats"][:, 4:5], T["stats"][:, 4:5]),
                  reads=["stats4"], writes=["stats4"])
            pr.op('v', lambda e: e.tensor_scalar(T["probs"][:, :], T["probs"][:, :],
                                                 T["stats"][:, 4:5], None, ALU.mult),
                  reads=["probs", "stats4"], writes=["probs"])
            pr.op('v', lambda e: e.max(T["mx8"][:, :], T["probs"][:, :]),
                  reads=["probs"], writes=["mx8"])
            pr.op('v', lambda e: e.max_index(T["mi8"][:, :], T["mx8"][:, :],
                                             T["probs"][:, :]),
                  reads=["mx8", "probs"], writes=["mi8"])
            pr.op('v', lambda e: e.tensor_copy(T["tif"][:, 0:2], T["mi8"][:, 0:2]),
                  reads=["mi8"], writes=["tif"])
            pr.op('v', lambda e: e.tensor_add(T["stats"][:, 5:6], T["mx8"][:, 0:1],
                                              T["mx8"][:, 1:2]),
                  reads=["mx8"], writes=["stats"])
            pr.op('v', lambda e: e.reciprocal(T["stats"][:, 5:6], T["stats"][:, 5:6]),
                  reads=["stats"], writes=["stats"])
            pr.op('v', lambda e: e.tensor_scalar(T["eq1"][:, :], T["iotaf"][:, :],
                                                 T["tif"][:, 0:1], None, ALU.is_equal),
                  reads=["iotaf", "tif"], writes=["eq1"])
            pr.op('v', lambda e: e.tensor_scalar(T["eq2"][:, :], T["iotaf"][:, :],
                                                 T["tif"][:, 1:2], None, ALU.is_equal),
                  reads=["iotaf", "tif"], writes=["eq2"])
            pr.op('v', lambda e: e.tensor_scalar(T["eq1"][:, :], T["eq1"][:, :],
                                                 T["mx8"][:, 0:1], None, ALU.mult),
                  reads=["eq1", "mx8"], writes=["eq1"])
            pr.op('v', lambda e: e.tensor_scalar(T["eq2"][:, :], T["eq2"][:, :],
                                                 T["mx8"][:, 1:2], None, ALU.mult),
                  reads=["eq2", "mx8"], writes=["eq2"])
            pr.op('v', lambda e: e.tensor_add(T["gts"][:, :], T["eq1"][:, :], T["eq2"][:, :]),
                  reads=["eq1", "eq2"], writes=["gts"])
            pr.op('v', lambda e: e.tensor_scalar(T["gts"][:, :], T["gts"][:, :],
                                                 T["stats"][:, 5:6], None, ALU.mult),
                  reads=["gts", "stats"], writes=["gts"])
            tr(T["psC"][0:8, 0:128], "psC", T["gts"][:, :], ["gts"])
            pr.op('v', lambda e: e.tensor_copy(T["gt8"][0:8, :], T["psC"][0:8, 0:128]),
                  reads=["psC"], writes=["gt8"])
            pr.dma('y', gscr[b, :, jt * 128:(jt + 1) * 128], T["gt8"][0:8, :])

        for c in range(8):
            pr.dma('g', T["xn1bf"][:, c * 2048:(c + 1) * 2048],
                   xn1_fm[b, c * 128:(c + 1) * 128, :])

        for ex in range(8):
            pr.dma('y', T["gflat"][0:1, :], gscr[b, ex, :][None, :])
            pr.dma('y', T["obrow"][0:1, :], ob[ex, :][None, :])
            for hg in range(2):
                for (wd, dstb, tok_major) in ((qW, "q16", False), (kW, "k16", False),
                                              (vW, None, True)):
                    pr.dma('g', T["wst0"][:, :].rearrange("p (c f) -> p c f", f=512),
                           wd[ex].rearrange("(c p) (g f) -> p c g f", p=128, g=2)
                           [:, :, hg, :])
                    if not tok_major:
                        for hpc in range(4):
                            for c in range(8):
                                for f4 in range(4):
                                    mm(T["psD"][:, f4 * 512:(f4 + 1) * 512], "psD",
                                       T["wst0"][:, c * 512 + hpc * 128:
                                                  c * 512 + (hpc + 1) * 128],
                                       T["xn1bf"][:, c * 2048 + f4 * 512:
                                                  c * 2048 + (f4 + 1) * 512],
                                       c == 0, c == 7, ["wst0", "xn1bf"])
                            bvec = qb if dstb == "q16" else kb
                            pr.dma('y', T["zr"][0:1, 0:128],
                                   bvec[ex, hg * 512 + hpc * 128:
                                        hg * 512 + (hpc + 1) * 128][None, :])
                            tr(T["psC"][:, 0:1], "psC", T["zr"][0:1, 0:128], ["zr"], pin=1)
                            pr.op('v', lambda e: e.tensor_copy(T["divrep"][:, 0:1],
                                                               T["psC"][:, 0:1]),
                                  reads=["psC"], writes=["divrep"])
                            pr.op('v', (lambda e, _d=dstb, _hpc=hpc: e.tensor_scalar(
                                T[_d][:, _hpc * 2048:(_hpc + 1) * 2048],
                                T["psD"][:, 0:2048], T["divrep"][:, 0:1], None, ALU.add)),
                                  reads=["psD", "divrep"], writes=[dstb])
                    else:
                        pr.dma('y', T["zr"][0:1, 0:512],
                               vb[ex, hg * 512:(hg + 1) * 512][None, :])
                        mm(T["psC"][:, 0:512], "psC", T["ones1f"][0:1, :],
                           T["zr"][0:1, 0:512], True, True, ["zr", "ones1f"])
                        pr.op('v', lambda e: e.tensor_copy(T["divrep"][:, :],
                                                           T["psC"][:, 0:512]),
                              reads=["psC"], writes=["divrep"])
                        for tt in range(16):
                            for c in range(8):
                                mm(T["psC"][:, 0:512], "psC",
                                   T["xn1bf"][:, c * 2048 + tt * 128:
                                              c * 2048 + (tt + 1) * 128],
                                   T["wst0"][:, c * 512:(c + 1) * 512],
                                   c == 0, c == 7, ["wst0", "xn1bf"])
                            pr.op('v', (lambda e, _tt=tt, _hg=hg: e.tensor_add(
                                T["vtok2"][:, _tt * 1040 + _hg * 520:
                                           _tt * 1040 + _hg * 520 + 520]
                                .rearrange("p (h z) -> p h z", z=65)[:, :, 0:64],
                                T["psC"][:, 0:512]
                                .rearrange("p (h z) -> p h z", z=64),
                                T["divrep"][:, 0:512]
                                .rearrange("p (h z) -> p h z", z=64))),
                                  reads=["psC", "divrep"], writes=["vtok2"])
                for hp in range(8):
                    hglob = hg * 8 + hp
                    par = hp % 2
                    hpc = hp // 2
                    pse = "psE" if par == 0 else "psF"
                    for tqc in range(4):
                        for tkt in range(16):
                            if tkt % 2 == 0:
                                sdst = T[pse][:, 0:512]
                                skey = pse
                            else:
                                sdst = T["psD"][:, par * 512:(par + 1) * 512]
                                skey = "psD"
                            mm(sdst, skey,
                               T["k16"][par * 64:(par + 1) * 64,
                                        hpc * 2048 + tkt * 128:hpc * 2048 + (tkt + 1) * 128],
                               T["q16"][par * 64:(par + 1) * 64,
                                        hpc * 2048 + tqc * 512:hpc * 2048 + (tqc + 1) * 512],
                               True, True, ["k16", "q16"],
                               tp=(64, 0) if par == 1 else None)
                            pr.op('s', (lambda e, _t=tkt, _d=sdst: e.activation(
                                T["expT"][:, _t * 512:(_t + 1) * 512], _d,
                                AF.Exp, scale=T[c18][:, 0:1])),
                                  reads=[skey, c18], writes=["expT"])
                        for tkt in range(16):
                            mm(T["psG"][0:65, 0:512], "psG",
                               T["vtok2"][:, tkt * 1040 + hglob * 65:
                                          tkt * 1040 + (hglob + 1) * 65],
                               T["expT"][:, tkt * 512:(tkt + 1) * 512],
                               tkt == 0, tkt == 15, ["vtok2", "expT"])
                        pr.op('v', lambda e: e.reciprocal(T["zr"][0:1, 0:512],
                                                          T["psG"][64:65, 0:512]),
                              reads=["psG"], writes=["zr"])
                        pr.op('v', (lambda e, _tqc=tqc: e.tensor_mul(
                            T["dvr"][0:1, 0:512], T["zr"][0:1, 0:512],
                            T["gflat"][0:1, _tqc * 512:(_tqc + 1) * 512])),
                              reads=["zr", "gflat"], writes=["dvr"])
                        mm(T["psC"][:, 0:512], "psC", T["ones1f"][0:1, :],
                           T["dvr"][0:1, 0:512], True, True, ["dvr", "ones1f"])
                        pr.op('v', lambda e: e.tensor_copy(T["divrep"][:, :],
                                                           T["psC"][:, 0:512]),
                              reads=["psC"], writes=["divrep"])
                        pr.op('v', (lambda e, _par=par, _hpc=hpc, _tqc=tqc: e.tensor_mul(
                            T["og"][_par * 64:(_par + 1) * 64,
                                    _hpc * 2048 + _tqc * 512:_hpc * 2048 + (_tqc + 1) * 512],
                            T["psG"][0:64, 0:512], T["divrep"][0:64, 0:512])),
                              reads=["psG", "divrep"], writes=["og"])
                pr.dma('g', T["wst0"][:, :].rearrange("p (c f) -> p c f", f=1024),
                       oW[ex].rearrange("(g c p) f -> p g c f", p=128, g=2)[:, hg, :, :])
                for m in range(8):
                    for tqc in range(4):
                        for c in range(4):
                            mm(T["psC"][:, 0:512], "psC",
                               T["wst0"][:, c * 1024 + m * 128:c * 1024 + (m + 1) * 128],
                               T["og"][:, c * 2048 + tqc * 512:c * 2048 + (tqc + 1) * 512],
                               c == 0, (c == 3 and hg == 1), ["wst0", "og"])
                        if hg == 0:
                            mm(T["psC"][:, 0:512], "psC",
                               T["obrow"][0:1, m * 128:(m + 1) * 128],
                               T["gflat"][0:1, tqc * 512:(tqc + 1) * 512],
                               False, True, ["obrow", "gflat"])
                        else:
                            pr.op('t', (lambda e: e.nop()), reads=[], writes=[]) if False                                 else None
                        pr.op('v', lambda e: e.tensor_copy(T["ostg"][:, :],
                                                           T["psC"][:, 0:512]),
                              reads=["psC"], writes=["ostg"])
                        first = (ex == 0 and hg == 0)
                        pr.dma('g', oacc[b, m * 128:(m + 1) * 128,
                                         tqc * 512:(tqc + 1) * 512],
                               T["ostg"][:, :],
                               **({} if first else {"accum_op": ALU.add}))

        for m in range(8):
            pr.dma('y', T["x1c"][:, :], x1_fm[b, m * 128:(m + 1) * 128, :])
            pr.dma('y', T["s1c"][:, :], s1_fm[b, m * 128:(m + 1) * 128, :])
            pr.dma('y', T["x2st"][:, :], oacc[b, m * 128:(m + 1) * 128, :])
            pr.op('s', lambda e: e.activation(T["x2st"][:, :], T["x2st"][:, :], AF.Silu),
                  reads=["x2st"], writes=["x2st"])
            pr.op('v', lambda e: e.tensor_mul(T["s1c"][:, :], T["s1c"][:, :], T["x2st"][:, :]),
                  reads=["s1c", "x2st"], writes=["s1c"])
            pr.op('v', lambda e: e.tensor_add(T["x2st"][:, :], T["x1c"][:, :], T["s1c"][:, :]),
                  reads=["x1c", "s1c"], writes=["x2st"])
            for tt in range(16):
                tr(T["psC"][:, 0:128], "psC", T["x2st"][:, tt * 128:(tt + 1) * 128],
                   ["x2st"])
                pr.op('v', (lambda e, _tt=tt: e.tensor_copy(
                    T["x1c"][:, _tt * 128:(_tt + 1) * 128], T["psC"][:, 0:128])),
                      reads=["psC"], writes=["x1c"])
            pr.dma('y', out[b, :, m * 128:(m + 1) * 128]
                   .rearrange("(tt p) q -> p tt q", p=128),
                   T["x1c"][:, :].rearrange("p (tt q) -> p tt q", q=128))

    pr.final_waits('y')
    pr.final_waits('g')
    pr.finalize()
    return nc






SHARED_CONSTS = {"ioti", "idf", "idb", "onescol", "ones1f"}
CONST_SBUFS = [
    ("ioti", [128, 128], I32), ("idf", [128, 128], F32), ("idb", [128, 128], BF16),
    ("onescol", [128, 1], BF16), ("ones1f", [1, 128], F32), ("zrow", [1, 128], F32),
]


def emit_consts(pr, xn_tok, eo):
    T = pr.T
    pr.op('g', lambda e: e.memset(T["zrow"][:, :], 0.0), writes=["zrow"])
    for i in range(8):
        pr.dma('y', xn_tok[4096:4097, i * 128:(i + 1) * 128], T["zrow"][0:1, :])
        pr.dma('y', eo[NSLOT:NSLOT + 1, i * 128:(i + 1) * 128], T["zrow"][0:1, :])

SBUFS_A = [
    ("cw", [128, 64], F32), ("cb", [128, 16], F32), ("bdt", [128, 16], F32),
    ("dsk", [128, 16], F32), ("alog", [128, 256], F32), ("aneg", [128, 256], F32),
    ("lng", [128, 8], F32), ("lnb", [128, 8], F32),
    ("rwsb", [8, 1024], F32), ("rwT", [128, 64], F32),
    ("wdt", [64, 2048], BF16), ("wxp", [128, 16 * 96], BF16),
    ("idf", [128, 128], F32), ("ioti", [128, 128], I32),
    ("onescol", [128, 1], BF16), ("ones1f", [1, 128], F32),
    ("wbuf0", [128, 8 * 128], BF16), ("wbuf1", [128, 8 * 128], BF16),
    ("obuf0", [128, 16 * 128], BF16), ("obuf1", [128, 16 * 128], BF16),
    ("xtok0", [128, 1024], F32), ("xtok1", [128, 1024], F32),
    ("x_fm", [128, 2048], F32), ("x_bf", [128, 2048], BF16),
    ("xn_f", [128, 2048], F32), ("xn_bf", [128, 2048], BF16),
    ("tmp2", [128, 256], F32), ("xn2_bf", [128, 2048], BF16),
    ("srow", [1, 1024], F32), ("u_s", [128, 16 * 260], F32),
    ("zs", [128, 4096], F32), ("us", [128, 4096], F32), ("usbf", [128, 4096], BF16),
    ("du", [128, 4096], F32), ("delta", [128, 4096], F32),
    ("y_acc", [128, 4096], F32), ("padt", [128, 48], F32), ("hstate", [128, 256], F32),
    ("dtbf", [64, 256], BF16), ("bcsb", [16, 512], F32), ("bcrow", [1, 512], F32),
    ("brep", [128, 256], F32), ("crep", [128, 256], F32),
    ("dAt", [128, 256], F32), ("duBt", [128, 256], F32), ("ht", [128, 256], F32),
    ("ytmp", [128, 256], F32), ("ctmp", [128, 256], F32),
    ("rsrep", [128, 256], F32), ("mrsrep", [128, 256], F32),
    ("spabs", [128, 256], F32), ("spexp", [128, 256], F32), ("spln", [128, 256], F32),
    ("yg_bf", [128, 4096], BF16), ("s_sb", [128, 256], F32),
    ("rsb", [8, 256], F32), ("rtok", [128, 16], F32), ("mx8", [128, 8], F32),
    ("mi8", [128, 8], U32), ("negmax", [128, 1], F32), ("etok", [128, 8], F32),
    ("esum", [128, 1], F32), ("routesb", [128, 4], F32),
]
PSUMS_A = [
    ("psA", [128, 512], F32), ("psB", [128, 512], F32),
    ("psC", [128, 512], F32), ("psD", [128, 512], F32),
    ("psE", [128, 512], F32), ("psF", [128, 512], F32),
    ("psG", [128, 512], F32), ("psH", [128, 512], F32),
]


def emit_A(pr, x, W_in, conv_w, conv_b, W_xproj, W_dt, b_dt, A_log, D_skip,
           W_out, ln_g, ln_b, rw, s_fm, xn_tok, route):
    T = pr.T

    def mm(ps_ap, ps_key, lhsT, rhs, start, stop, reads):
        pr.op('t', (lambda e, _p=ps_ap, _l=lhsT, _r=rhs, _s=start, _e2=stop:
                    e.matmul(_p, _l, _r, start=_s, stop=_e2)),
              reads=reads, writes=[ps_key])

    def tr(ps_ap, ps_key, in_ap, reads, pin=128):
        ident = T["idf"][0:pin, 0:pin]
        pr.op('t', (lambda e, _p=ps_ap, _i=in_ap, _id=ident: e.transpose(_p, _i, _id)),
              reads=list(reads) + ["idf"], writes=[ps_key])

    c_one = pr.constcol(1.0)
    c_neg1 = pr.constcol(-1.0)

    pr.dma('y', T["cw"][:, :].rearrange("p (c j) -> p c j", j=4),
           conv_w[:].rearrange("(c p) j -> p c j", p=128))
    pr.dma('y', T["cb"][:, :], conv_b[:].rearrange("(c p) -> p c", p=128),
           allow_slow_non_contiguous=True)
    pr.dma('y', T["bdt"][:, :], b_dt[:].rearrange("(c p) -> p c", p=128),
           allow_slow_non_contiguous=True)
    pr.dma('y', T["dsk"][:, :], D_skip[:].rearrange("(c p) -> p c", p=128),
           allow_slow_non_contiguous=True)
    pr.dma('y', T["alog"][:, :].rearrange("p (c n) -> p c n", n=16),
           A_log[:].rearrange("(c p) n -> p c n", p=128))
    pr.dma('y', T["lng"][:, :], ln_g[:].rearrange("(c p) -> p c", p=128),
           allow_slow_non_contiguous=True)
    pr.dma('y', T["lnb"][:, :], ln_b[:].rearrange("(c p) -> p c", p=128),
           allow_slow_non_contiguous=True)
    pr.dma('y', T["rwsb"][:, :], rw[:, :])
    pr.dma('g', T["wdt"][:, :], W_dt[:, :])
    pr.dma('g', T["wxp"][:, :].rearrange("p (c f) -> p c f", f=96),
           W_xproj[:].rearrange("(c p) f -> p c f", p=128))
    pr.op('g', lambda e: e.iota(T["ioti"][:, :], pattern=[[-1, 128]], base=0,
                                channel_multiplier=1), writes=["ioti"])
    pr.op('v', lambda e: e.tensor_copy(T["idf"][:, :], T["ioti"][:, :]),
          reads=["ioti"], writes=["idf"])
    pr.op('v', lambda e: e.tensor_single_scalar(T["idf"][:, :], T["idf"][:, :], 0.0,
                                                ALU.is_equal),
          reads=["idf"], writes=["idf"])
    pr.op('g', lambda e: e.memset(T["onescol"][:, :], 1.0), writes=["onescol"])
    pr.op('g', lambda e: e.memset(T["ones1f"][:, :], 1.0), writes=["ones1f"])
    pr.op('s', lambda e: e.activation(T["aneg"][:, :], T["alog"][:, :], AF.Exp),
          reads=["alog"], writes=["aneg"])
    pr.op('v', lambda e: e.tensor_scalar_mul(T["aneg"][:, :], T["aneg"][:, :], -1.0),
          reads=["aneg"], writes=["aneg"])
    for c in range(8):
        tr(T["psA"][:, 0:8], "psA", T["rwsb"][0:8, c * 128:(c + 1) * 128], ["rwsb"], pin=8)
        pr.op('v', (lambda e, _c=c: e.tensor_copy(T["rwT"][:, _c * 8:(_c + 1) * 8],
                                                  T["psA"][:, 0:8])),
              reads=["psA"], writes=["rwT"])

    def stats(src_bf):
        for c in range(8):
            mm(T["psA"][0:1, 0:TC], "psA", T["onescol"][:, 0:1],
               src_bf[:, c * TC:(c + 1) * TC], c == 0, c == 7, ["onescol", src_bf])
        pr.op('s', (lambda e, _s=src_bf: e.activation(_s[:, :], _s[:, :], AF.Square)),
              reads=[src_bf], writes=[src_bf])
        for c in range(8):
            mm(T["psB"][0:1, 0:TC], "psB", T["onescol"][:, 0:1],
               src_bf[:, c * TC:(c + 1) * TC], c == 0, c == 7, ["onescol", src_bf])
        pr.op('v', lambda e: e.tensor_scalar_mul(T["srow"][0:1, 0:TC],
                                                 T["psA"][0:1, 0:TC], 1.0 / 1024),
              reads=["psA"], writes=["srow"])
        pr.op('v', lambda e: e.tensor_scalar_mul(T["srow"][0:1, 256:256 + TC],
                                                 T["psB"][0:1, 0:TC], 1.0 / 1024),
              reads=["psB"], writes=["srow"])
        pr.op('v', lambda e: e.tensor_mul(T["srow"][0:1, 512:512 + TC],
                                          T["srow"][0:1, 0:TC], T["srow"][0:1, 0:TC]),
              reads=["srow"], writes=["srow"])
        pr.op('v', lambda e: e.tensor_sub(T["srow"][0:1, 256:256 + TC],
                                          T["srow"][0:1, 256:256 + TC],
                                          T["srow"][0:1, 512:512 + TC]),
              reads=["srow"], writes=["srow"])
        pr.op('v', lambda e: e.tensor_scalar_add(T["srow"][0:1, 256:256 + TC],
                                                 T["srow"][0:1, 256:256 + TC], 1e-5),
              reads=["srow"], writes=["srow"])
        pr.op('s', lambda e: e.activation(T["srow"][0:1, 256:256 + TC],
                                          T["srow"][0:1, 256:256 + TC], AF.Ln),
              reads=["srow"], writes=["srow"])
        pr.op('v', lambda e: e.tensor_scalar_mul(T["srow"][0:1, 256:256 + TC],
                                                 T["srow"][0:1, 256:256 + TC], -0.5),
              reads=["srow"], writes=["srow"])
        pr.op('s', lambda e: e.activation(T["srow"][0:1, 256:256 + TC],
                                          T["srow"][0:1, 256:256 + TC], AF.Exp),
              reads=["srow"], writes=["srow"])
        pr.op('v', lambda e: e.tensor_mul(T["srow"][0:1, 0:TC], T["srow"][0:1, 0:TC],
                                          T["srow"][0:1, 256:256 + TC]),
              reads=["srow"], writes=["srow"])

    def bcast(row_ap, dst_name):
        mm(T["psC"][:, 0:TC], "psC", T["ones1f"][0:1, :], row_ap, True, True,
           ["srow", "ones1f"])
        pr.op('v', (lambda e, _d=dst_name: e.tensor_copy(T[_d][:, 0:TC], T["psC"][:, 0:TC])),
              reads=["psC"], writes=[dst_name])

    for b in range(2):
        pr.op('g', lambda e: e.memset(T["hstate"][:, :], 0.0), writes=["hstate"])
        pr.op('g', lambda e: e.memset(T["padt"][:, :], 0.0), writes=["padt"])
        for tci in range(NTC):
            t0 = tci * TC
            xt = ["xtok0", "xtok1"]
            for i in range(2):
                pr.dma('y', T[xt[i]][:, :], x[b, t0 + i * 128: t0 + (i + 1) * 128, :])
            for c in range(8):
                for i in range(2):
                    tr(T["psH"][:, 0:128], "psH",
                       T[xt[i]][:, c * 128:(c + 1) * 128], [xt[i]])
                    col = c * 256 + i * 128
                    pr.op('v', (lambda e, _col=col: e.tensor_copy(
                        T["x_fm"][:, _col:_col + 128], T["psH"][:, 0:128])),
                          reads=["psH"], writes=["x_fm"])
            pr.op('s', lambda e: e.copy(T["x_bf"][:, :], T["x_fm"][:, :]),
                  reads=["x_fm"], writes=["x_bf"])

            stats(T["x_bf"])
            bcast(T["srow"][0:1, 256:256 + TC], "rsrep")
            bcast(T["srow"][0:1, 0:TC], "mrsrep")
            for c in range(8):
                pr.op('v', (lambda e, _c=c: e.tensor_mul(
                    T["xn_f"][:, _c * TC:(_c + 1) * TC],
                    T["x_fm"][:, _c * TC:(_c + 1) * TC], T["rsrep"][:, 0:TC])),
                      reads=["x_fm", "rsrep"], writes=["xn_f"])
                pr.op('v', (lambda e, _c=c: e.tensor_sub(
                    T["xn_f"][:, _c * TC:(_c + 1) * TC],
                    T["xn_f"][:, _c * TC:(_c + 1) * TC], T["mrsrep"][:, 0:TC])),
                      reads=["xn_f", "mrsrep"], writes=["xn_f"])
            pr.op('s', lambda e: e.copy(T["xn_bf"][:, :], T["xn_f"][:, :]),
                  reads=["xn_f"], writes=["xn_bf"])

            stats(T["xn_bf"])
            bcast(T["srow"][0:1, 256:256 + TC], "rsrep")
            bcast(T["srow"][0:1, 0:TC], "mrsrep")
            for c in range(8):
                pr.op('v', (lambda e, _c=c: e.tensor_mul(
                    T["tmp2"][:, 0:TC],
                    T["xn_f"][:, _c * TC:(_c + 1) * TC], T["rsrep"][:, 0:TC])),
                      reads=["xn_f", "rsrep"], writes=["tmp2"])
                pr.op('v', lambda e: e.tensor_sub(
                    T["tmp2"][:, 0:TC], T["tmp2"][:, 0:TC], T["mrsrep"][:, 0:TC]),
                      reads=["tmp2", "mrsrep"], writes=["tmp2"])
                pr.op('v', (lambda e, _c=c: e.tensor_scalar(
                    T["xn2_bf"][:, _c * TC:(_c + 1) * TC], T["tmp2"][:, 0:TC],
                    T["lng"][:, _c:_c + 1], T["lnb"][:, _c:_c + 1],
                    ALU.mult, ALU.add)),
                      reads=["tmp2", "lnb", "lng"], writes=["xn2_bf"])

            for f in range(32):
                wb = "wbuf0" if f % 2 == 0 else "wbuf1"
                pr.dma('g', T[wb][:, :].rearrange("p (c k) -> p c k", k=128),
                       W_in[:].rearrange("(c p) (f k) -> p c f k", p=128, f=32)[:, :, f, :])
                for c in range(8):
                    mm(T["psD"][:, 0:TC], "psD", T[wb][:, c * 128:(c + 1) * 128],
                       T["xn2_bf"][:, c * TC:(c + 1) * TC], c == 0, c == 7, [wb, "xn2_bf"])
                if f < 16:
                    pr.op('v', (lambda e, _f=f: e.tensor_copy(
                        T["u_s"][:, _f * 260 + 3:_f * 260 + 3 + TC], T["psD"][:, 0:TC])),
                          reads=["psD"], writes=["u_s"])
                else:
                    pr.op('s', (lambda e, _z=f - 16: e.activation(
                        T["zs"][:, _z * TC:(_z + 1) * TC], T["psD"][:, 0:TC], AF.Silu)),
                          reads=["psD"], writes=["zs"])

            for c in range(16):
                pr.op('v', (lambda e, _c=c: e.tensor_copy(
                    T["u_s"][:, _c * 260:_c * 260 + 3], T["padt"][:, _c * 3:_c * 3 + 3])),
                      reads=["padt"], writes=["u_s"])
            for c in range(16):
                base = c * 260
                pr.op('v', (lambda e, _b=base, _c=c: e.tensor_scalar(
                    T["ytmp"][:, 0:TC], T["u_s"][:, _b:_b + TC],
                    T["cw"][:, 4 * _c:4 * _c + 1], None, ALU.mult)),
                      reads=["u_s", "cw"], writes=["ytmp"])
                for j in range(1, 4):
                    pr.op('v', (lambda e, _b=base, _c=c, _j=j: e.scalar_tensor_tensor(
                        T["ytmp"][:, 0:TC], T["u_s"][:, _b + _j:_b + _j + TC],
                        T["cw"][:, 4 * _c + _j:4 * _c + _j + 1], T["ytmp"][:, 0:TC],
                        ALU.mult, ALU.add)),
                          reads=["u_s", "cw", "ytmp"], writes=["ytmp"])
                pr.op('v', (lambda e, _b=base, _c=c: e.tensor_copy(
                    T["padt"][:, _c * 3:_c * 3 + 3], T["u_s"][:, _b + TC:_b + TC + 3])),
                      reads=["u_s"], writes=["padt"])
                pr.op('s', (lambda e, _c=c: e.activation(
                    T["us"][:, _c * TC:(_c + 1) * TC], T["ytmp"][:, 0:TC], AF.Silu,
                    bias=T["cb"][:, _c:_c + 1])),
                      reads=["ytmp", "cb"], writes=["us"])
            pr.op('s', lambda e: e.copy(T["usbf"][:, :], T["us"][:, :]),
                  reads=["us"], writes=["usbf"])

            for c in range(16):
                mm(T["psE"][0:64, 0:TC], "psE", T["wxp"][:, c * 96:c * 96 + 64],
                   T["usbf"][:, c * TC:(c + 1) * TC], c == 0, c == 15, ["wxp", "usbf"])
            for c in range(16):
                mm(T["psF"][0:16, 0:TC], "psF", T["wxp"][:, c * 96 + 64:c * 96 + 80],
                   T["usbf"][:, c * TC:(c + 1) * TC], c == 0, c == 15, ["wxp", "usbf"])
            for c in range(16):
                mm(T["psG"][0:16, 0:TC], "psG", T["wxp"][:, c * 96 + 80:c * 96 + 96],
                   T["usbf"][:, c * TC:(c + 1) * TC], c == 0, c == 15, ["wxp", "usbf"])
            pr.op('s', lambda e: e.copy(T["dtbf"][0:64, 0:TC], T["psE"][0:64, 0:TC]),
                  reads=["psE"], writes=["dtbf"])
            pr.op('v', lambda e: e.tensor_copy(T["bcsb"][0:16, 0:TC], T["psF"][0:16, 0:TC]),
                  reads=["psF"], writes=["bcsb"])
            pr.op('v', lambda e: e.tensor_copy(T["bcsb"][0:16, 256:256 + TC],
                                               T["psG"][0:16, 0:TC]),
                  reads=["psG"], writes=["bcsb"])

            for c in range(16):
                mm(T["psE"][:, 256:256 + TC], "psE", T["wdt"][0:64, c * 128:(c + 1) * 128],
                   T["dtbf"][0:64, 0:TC], True, True, ["wdt", "dtbf"])
                pr.op('s', (lambda e, _c=c: e.activation(
                    T["spabs"][:, 0:TC], T["psE"][:, 256:256 + TC], AF.Abs,
                    bias=T["bdt"][:, _c:_c + 1])),
                      reads=["psE", "bdt"], writes=["spabs"])
                pr.op('s', (lambda e, _c=c: e.activation(
                    T["delta"][:, _c * TC:(_c + 1) * TC], T["psE"][:, 256:256 + TC],
                    AF.Relu, bias=T["bdt"][:, _c:_c + 1])),
                      reads=["psE", "bdt"], writes=["delta"])
                pr.op('s', lambda e: e.activation(
                    T["spexp"][:, 0:TC], T["spabs"][:, 0:TC], AF.Exp,
                    scale=T[c_neg1][:, 0:1]),
                      reads=["spabs", c_neg1], writes=["spexp"])
                pr.op('s', lambda e: e.activation(
                    T["spln"][:, 0:TC], T["spexp"][:, 0:TC], AF.Ln,
                    bias=T[c_one][:, 0:1]),
                      reads=["spexp", c_one], writes=["spln"])
                pr.op('v', (lambda e, _c=c: e.tensor_add(
                    T["delta"][:, _c * TC:(_c + 1) * TC],
                    T["delta"][:, _c * TC:(_c + 1) * TC], T["spln"][:, 0:TC])),
                      reads=["delta", "spln"], writes=["delta"])

            for c in range(16):
                pr.op('v', (lambda e, _c=c: e.tensor_mul(
                    T["du"][:, _c * TC:(_c + 1) * TC],
                    T["delta"][:, _c * TC:(_c + 1) * TC],
                    T["us"][:, _c * TC:(_c + 1) * TC])),
                      reads=["delta", "us"], writes=["du"])

            for n in range(16):
                pr.dma('y', T["bcrow"][0:1, 0:512], T["bcsb"][n:n + 1, 0:512])
                mm(T["psC"][:, 0:TC], "psC", T["ones1f"][0:1, :],
                   T["bcrow"][0:1, 0:TC], True, True, ["bcrow", "ones1f"])
                pr.op('v', lambda e: e.tensor_copy(T["brep"][:, 0:TC], T["psC"][:, 0:TC]),
                      reads=["psC"], writes=["brep"])
                mm(T["psC"][:, 0:TC], "psC", T["ones1f"][0:1, :],
                   T["bcrow"][0:1, 256:256 + TC], True, True, ["bcrow", "ones1f"])
                pr.op('v', lambda e: e.tensor_copy(T["crep"][:, 0:TC], T["psC"][:, 0:TC]),
                      reads=["psC"], writes=["crep"])
                for c in range(16):
                    pr.op('s', (lambda e, _c=c, _n=n: e.activation(
                        T["dAt"][:, 0:TC], T["delta"][:, _c * TC:(_c + 1) * TC], AF.Exp,
                        scale=T["aneg"][:, 16 * _c + _n:16 * _c + _n + 1])),
                          reads=["delta", "aneg"], writes=["dAt"])
                    pr.op('v', (lambda e, _c=c: e.tensor_mul(
                        T["duBt"][:, 0:TC], T["du"][:, _c * TC:(_c + 1) * TC],
                        T["brep"][:, 0:TC])),
                          reads=["du", "brep"], writes=["duBt"])
                    pr.op('v', (lambda e, _c=c, _n=n: e.tensor_tensor_scan(
                        T["ht"][:, 0:TC], T["dAt"][:, 0:TC], T["duBt"][:, 0:TC],
                        T["hstate"][:, 16 * _c + _n:16 * _c + _n + 1],
                        ALU.mult, ALU.add)),
                          reads=["dAt", "duBt", "hstate"], writes=["ht"])
                    pr.op('v', (lambda e, _c=c, _n=n: e.tensor_copy(
                        T["hstate"][:, 16 * _c + _n:16 * _c + _n + 1],
                        T["ht"][:, TC - 1:TC])),
                          reads=["ht"], writes=["hstate"])
                    pr.op('v', lambda e: e.tensor_mul(
                        T["ctmp"][:, 0:TC], T["ht"][:, 0:TC], T["crep"][:, 0:TC]),
                          reads=["ht", "crep"], writes=["ctmp"])
                    if n == 0:
                        pr.op('v', (lambda e, _c=c: e.scalar_tensor_tensor(
                            T["y_acc"][:, _c * TC:(_c + 1) * TC],
                            T["us"][:, _c * TC:(_c + 1) * TC],
                            T["dsk"][:, _c:_c + 1], T["ctmp"][:, 0:TC],
                            ALU.mult, ALU.add)),
                              reads=["us", "dsk", "ctmp"], writes=["y_acc"])
                    else:
                        pr.op('v', (lambda e, _c=c: e.tensor_add(
                            T["y_acc"][:, _c * TC:(_c + 1) * TC],
                            T["y_acc"][:, _c * TC:(_c + 1) * TC], T["ctmp"][:, 0:TC])),
                              reads=["y_acc", "ctmp"], writes=["y_acc"])

            for c in range(16):
                pr.op('v', (lambda e, _c=c: e.tensor_mul(
                    T["yg_bf"][:, _c * TC:(_c + 1) * TC],
                    T["y_acc"][:, _c * TC:(_c + 1) * TC],
                    T["zs"][:, _c * TC:(_c + 1) * TC])),
                      reads=["y_acc", "zs"], writes=["yg_bf"])

            for m in range(8):
                ob = "obuf0" if m % 2 == 0 else "obuf1"
                pr.dma('g', T[ob][:, :].rearrange("p (c k) -> p c k", k=128),
                       W_out[:].rearrange("(c p) (m k) -> p c m k", p=128, m=8)[:, :, m, :])
                for c in range(16):
                    mm(T["psD"][:, 256:256 + TC], "psD", T[ob][:, c * 128:(c + 1) * 128],
                       T["yg_bf"][:, c * TC:(c + 1) * TC], c == 0, c == 15, [ob, "yg_bf"])
                pr.op('v', (lambda e, _m=m: e.tensor_add(
                    T["s_sb"][:, 0:TC], T["psD"][:, 256:256 + TC],
                    T["xn_f"][:, _m * TC:(_m + 1) * TC])),
                      reads=["psD", "xn_f"], writes=["s_sb"])
                pr.dma('y', s_fm[b, m * 128:(m + 1) * 128, t0:t0 + TC], T["s_sb"][:, 0:TC])

            for i in range(2):
                for c in range(8):
                    tr(T["psH"][:, 128:256], "psH",
                       T["xn_f"][:, c * 256 + i * 128:c * 256 + i * 128 + 128], ["xn_f"])
                    pr.op('v', (lambda e, _c=c, _i=i: e.tensor_copy(
                        T[xt[_i]][:, _c * 128:(_c + 1) * 128], T["psH"][:, 128:256])),
                          reads=["psH"], writes=[xt[i]])
                pr.dma('y', xn_tok[b * 2048 + t0 + i * 128: b * 2048 + t0 + (i + 1) * 128, :],
                       T[xt[i]][:, :])

            for c in range(8):
                mm(T["psA"][0:8, 256:256 + TC], "psA", T["rwT"][:, c * 8:(c + 1) * 8],
                   T["xn_f"][:, c * TC:(c + 1) * TC], c == 0, c == 7, ["rwT", "xn_f"])
            pr.op('v', lambda e: e.tensor_copy(T["rsb"][0:8, 0:TC],
                                               T["psA"][0:8, 256:256 + TC]),
                  reads=["psA"], writes=["rsb"])
            for i in range(2):
                tr(T["psB"][:, 256:264], "psB", T["rsb"][0:8, i * 128:(i + 1) * 128], ["rsb"], pin=8)
                pr.op('v', lambda e: e.tensor_copy(T["rtok"][:, 0:8], T["psB"][:, 256:264]),
                      reads=["psB"], writes=["rtok"])
                pr.op('v', lambda e: e.max(T["mx8"][:, :], T["rtok"][:, 0:8]),
                      reads=["rtok"], writes=["mx8"])
                pr.op('v', lambda e: e.max_index(T["mi8"][:, :], T["mx8"][:, :],
                                                 T["rtok"][:, 0:8]),
                      reads=["mx8", "rtok"], writes=["mi8"])
                pr.op('v', lambda e: e.tensor_scalar_mul(T["negmax"][:, 0:1],
                                                         T["mx8"][:, 0:1], -1.0),
                      reads=["mx8"], writes=["negmax"])
                pr.op('s', lambda e: e.activation(T["etok"][:, :], T["rtok"][:, 0:8],
                                                  AF.Exp, bias=T["negmax"][:, 0:1],
                                                  accum_out=T["esum"][:, 0:1]),
                      reads=["rtok", "negmax"], writes=["etok", "esum"])
                pr.op('v', lambda e: e.reciprocal(T["esum"][:, 0:1], T["esum"][:, 0:1]),
                      reads=["esum"], writes=["esum"])
                pr.op('v', lambda e: e.tensor_copy(T["routesb"][:, 0:1], T["mi8"][:, 0:1]),
                      reads=["mi8"], writes=["routesb"])
                pr.op('v', lambda e: e.tensor_copy(T["routesb"][:, 1:2], T["esum"][:, 0:1]),
                      reads=["esum"], writes=["routesb"])
                pr.dma('y', route[b * 2048 + t0 + i * 128: b * 2048 + t0 + (i + 1) * 128, :],
                       T["routesb"][:, 0:2])



SBUFS_B = [
    ("evsb", [128, 256], F32), ("EQ", [128, 2048], BF16),
    ("rank", [128, 2048], F32), ("cnt8", [8, 256], F32), ("pfx", [8, 256], F32),
    ("cntrow", [1, 2048], F32), ("pfxrow", [1, 2048], F32), ("ones8", [8, 256], F32),
    ("base", [128, 256], F32), ("rtk", [128, 256], F32), ("postk", [128, 256], F32),
    ("brd", [128, 256], F32), ("tmpb", [128, 256], F32),
    ("tri", [128, 128], BF16), ("ioti", [128, 128], I32),
    ("onescol", [128, 1], BF16), ("ones1f", [1, 128], F32), ("idf", [128, 128], F32),
    ("idb", [128, 128], BF16), ("mycol", [128, 1], F32), ("myc32", [128, 1], F32),
    ("xt0", [128, 1024], F32),
    ("xfmT", [128, 1024], F32),
    ("rwsb", [8, 1024], F32), ("s8", [8, 1], F32), ("rwc", [8, 1024], F32),
    ("rwcT", [128, 64], F32),
    ("rtok", [128, 8], F32), ("mx8", [128, 8], F32), ("mi8", [128, 8], U32),
    ("iotf", [128, 256], F32),
    ("colsel", [8, 256], F32), ("tmp8", [8, 256], F32), ("base8", [8, 1], F32),
    ("base8row", [1, 8], F32), ("brdcol", [128, 1], F32),
    ("baseTok", [128, 256], F32), ("lr", [128, 256], F32),
    ("slotf", [128, 256], F32), ("kgm", [128, 256], F32), ("klm", [128, 256], F32),
    ("keepA", [128, 256], F32), ("mOwn", [128, 256], F32), ("dcol", [128, 256], F32),
    ("gidxf", [128, 256], F32),
    ("tokidf", [128, 256], F32), ("gstg", [128, 128], F32),
    ("mu8b", [128, 256], U8),
    ("tokid", [128, 256], I32), ("slot", [128, 256], I32), ("sh", [128, 256], I32),
    ("rem", [128, 256], I32), ("tgt", [128, 256], I32), ("tgt2", [128, 256], I32),
    ("trash", [128, 256], I32),
    ("fill", [128, 64], I32), ("srcidx", [128, 64], I32),
    ("w1sb0", [128, 8 * 2048], BF16), ("w1sb1", [128, 8 * 2048], BF16),
    ("b1sb", [128, 32], F32), ("b2sb", [128, 8], F32),
    ("w2b0", [128, 32 * 128], BF16), ("w2b1", [128, 32 * 128], BF16),
    ("gbuf", [128, 1024], F32), ("gbf", [128, 1024], BF16),
    ("bufT", [128, 4096], BF16), ("hbf", [128, 16384], BF16),
    ("eofm", [128, 512], F32), ("estage", [128, 128], F32),
]
PSUMS_B = [
    ("psA", [128, 512], F32), ("psB", [128, 512], F32),
    ("psC", [128, 512], F32), ("psD", [128, 512], F32),
    ("psT", [128, 512], BF16), ("psE", [128, 512], F32),
    ("psR", [128, 8], F32),
]


def emit_B2(pr, x, xnf, myexp, rw, w1, b1, w2, b2, eo, gidxT, srcd):
    T = pr.T

    def mm(ps_ap, ps_key, lhsT, rhs, start, stop, reads):
        pr.op('t', (lambda e, _p=ps_ap, _l=lhsT, _r=rhs, _s=start, _e2=stop:
                    e.matmul(_p, _l, _r, start=_s, stop=_e2)),
              reads=reads, writes=[ps_key])

    def tr(ps_ap, ps_key, in_ap, reads, pin=128, bf=False):
        ident = (T["idb"] if bf else T["idf"])[0:pin, 0:pin]
        pr.op('t', (lambda e, _p=ps_ap, _i=in_ap, _id=ident: e.transpose(_p, _i, _id)),
              reads=list(reads) + ["idb" if bf else "idf"], writes=[ps_key])

    pr.op('g', lambda e: e.iota(T["ioti"][:, :], pattern=[[-1, 128]], base=0,
                                channel_multiplier=1), writes=["ioti"])
    pr.op('v', lambda e: e.tensor_copy(T["idf"][:, :], T["ioti"][:, :]),
          reads=["ioti"], writes=["idf"])
    pr.op('v', lambda e: e.tensor_single_scalar(T["idf"][:, :], T["idf"][:, :], 0.0,
                                                ALU.is_equal),
          reads=["idf"], writes=["idf"])
    pr.op('v', lambda e: e.tensor_copy(T["idb"][:, :], T["idf"][:, :]),
          reads=["idf"], writes=["idb"])
    pr.op('v', lambda e: e.tensor_copy(T["rank"][:, 0:128], T["ioti"][:, :]),
          reads=["ioti"], writes=["rank"])
    pr.op('v', lambda e: e.tensor_single_scalar(T["tri"][:, :], T["rank"][:, 0:128], 0.0,
                                                ALU.is_lt),
          reads=["rank"], writes=["tri"])
    pr.op('g', lambda e: e.memset(T["onescol"][:, :], 1.0), writes=["onescol"])
    pr.op('g', lambda e: e.memset(T["ones1f"][:, :], 1.0), writes=["ones1f"])
    pr.dma('y', T["mycol"][0:1, 0:1], myexp[0:1, 0:1])
    mm(T["psA"][:, 0:1], "psA", T["ones1f"][0:1, :], T["mycol"][0:1, 0:1], True, True,
       ["mycol", "ones1f"])
    pr.op('v', lambda e: e.tensor_copy(T["mycol"][:, 0:1], T["psA"][:, 0:1]),
          reads=["psA"], writes=["mycol"])
    pr.op('v', lambda e: e.tensor_scalar_mul(T["myc32"][:, 0:1], T["mycol"][:, 0:1],
                                             32.0),
          reads=["mycol"], writes=["myc32"])

    pr.dma('y', T["rwsb"][0:8, :], rw[:, :])
    pr.op('v', lambda e: e.reduce_sum(T["s8"][0:8, 0:1], T["rwsb"][0:8, :], AX.X),
          reads=["rwsb"], writes=["s8"])
    pr.op('v', lambda e: e.tensor_scalar_mul(T["s8"][0:8, 0:1], T["s8"][0:8, 0:1],
                                             1.0 / 1024),
          reads=["s8"], writes=["s8"])
    pr.op('v', lambda e: e.tensor_scalar(T["rwc"][0:8, :], T["rwsb"][0:8, :],
                                         T["s8"][0:8, 0:1], None, ALU.subtract),
          reads=["rwsb", "s8"], writes=["rwc"])
    for c in range(8):
        tr(T["psA"][:, 0:8], "psA", T["rwc"][0:8, c * 128:(c + 1) * 128], ["rwc"],
           pin=8)
        pr.op('v', (lambda e, _c=c: e.tensor_copy(T["rwcT"][:, _c * 8:(_c + 1) * 8],
                                                  T["psA"][:, 0:8])),
              reads=["psA"], writes=["rwcT"])
    for bb in range(16):
        for jt in range(16):
            col = bb * 16 + jt
            xt = "xt0"
            pr.dma('y', T[xt][:, :], x[bb, jt * 128:(jt + 1) * 128, :])
            for h in range(2):
                for c4 in range(4):
                    c = h * 4 + c4
                    tr(T["psB"][:, c4 * 128:(c4 + 1) * 128], "psB",
                       T[xt][:, c * 128:(c + 1) * 128], [xt])
                pr.op('v', (lambda e, _h=h: e.tensor_copy(
                    T["xfmT"][:, _h * 512:(_h + 1) * 512], T["psB"][:, 0:512])),
                      reads=["psB"], writes=["xfmT"])
            for c in range(8):
                mm(T["psR"][:, 0:8], "psR", T["xfmT"][:, c * 128:(c + 1) * 128],
                   T["rwcT"][:, c * 8:(c + 1) * 8], c == 0, c == 7, ["xfmT", "rwcT"])
            pr.op('v', lambda e: e.tensor_copy(T["rtok"][:, :], T["psR"][:, 0:8]),
                  reads=["psR"], writes=["rtok"])
            pr.op('v', lambda e: e.max(T["mx8"][:, :], T["rtok"][:, :]),
                  reads=["rtok"], writes=["mx8"])
            pr.op('v', lambda e: e.max_index(T["mi8"][:, :], T["mx8"][:, :],
                                             T["rtok"][:, :]),
                  reads=["mx8", "rtok"], writes=["mi8"])
            pr.op('v', (lambda e, _col=col: e.tensor_copy(
                T["evsb"][:, _col:_col + 1], T["mi8"][:, 0:1])),
                  reads=["mi8"], writes=["evsb"])

    for e8 in range(8):
        pr.op('v', (lambda e, _e=e8: e.tensor_single_scalar(
            T["EQ"][:, _e * 256:(_e + 1) * 256], T["evsb"][:, :], float(_e),
            ALU.is_equal)),
              reads=["evsb"], writes=["EQ"])
    for q in range(4):
        mm(T["psA"][0:1, 0:512], "psA", T["onescol"][:, 0:1],
           T["EQ"][:, q * 512:(q + 1) * 512], True, True, ["onescol", "EQ"])
        pr.op('v', (lambda e, _q=q: e.tensor_copy(
            T["cntrow"][0:1, _q * 512:(_q + 1) * 512], T["psA"][0:1, 0:512])),
              reads=["psA"], writes=["cntrow"])
        mm(T["psB"][:, 0:512], "psB", T["tri"][:, :],
           T["EQ"][:, q * 512:(q + 1) * 512], True, True, ["tri", "EQ"])
        pr.op('v', (lambda e, _q=q: e.tensor_copy(
            T["rank"][:, _q * 512:(_q + 1) * 512], T["psB"][:, 0:512])),
              reads=["psB"], writes=["rank"])
    pr.op('g', lambda e: e.memset(T["ones8"][:, :], 1.0), writes=["ones8"])
    pr.dma('y', T["cnt8"][0:8, 0:256], T["cntrow"][0:1, :].rearrange("o (n t) -> o n t", t=256))
    pr.op('v', lambda e: e.tensor_tensor_scan(T["pfx"][0:8, 0:256], T["ones8"][0:8, 0:256],
                                              T["cnt8"][0:8, 0:256], 0.0, ALU.mult, ALU.add),
          reads=["cnt8", "ones8"], writes=["pfx"])
    pr.op('v', lambda e: e.tensor_sub(T["pfx"][0:8, 0:256], T["pfx"][0:8, 0:256],
                                      T["cnt8"][0:8, 0:256]),
          reads=["pfx", "cnt8"], writes=["pfx"])
    pr.dma('y', T["pfxrow"][0:1, :].rearrange("o (n t) -> o n t", t=256), T["pfx"][0:8, 0:256])
    for e8 in range(8):
        mm(T["psC"][:, 0:256], "psC", T["ones1f"][0:1, :],
           T["pfxrow"][0:1, e8 * 256:(e8 + 1) * 256], True, True, ["pfxrow", "ones1f"])
        pr.op('v', lambda e: e.tensor_copy(T["brd"][:, 0:256], T["psC"][:, 0:256]),
              reads=["psC"], writes=["brd"])
        if e8 == 0:
            pr.op('v', lambda e: e.tensor_mul(T["base"][:, :], T["EQ"][:, 0:256],
                                              T["brd"][:, :]),
                  reads=["EQ", "brd"], writes=["base"])
            pr.op('v', lambda e: e.tensor_mul(T["rtk"][:, :], T["EQ"][:, 0:256],
                                              T["rank"][:, 0:256]),
                  reads=["EQ", "rank"], writes=["rtk"])
        else:
            pr.op('v', (lambda e, _e=e8: e.tensor_mul(T["tmpb"][:, :],
                                                      T["EQ"][:, _e * 256:(_e + 1) * 256],
                                                      T["brd"][:, :])),
                  reads=["EQ", "brd"], writes=["tmpb"])
            pr.op('v', lambda e: e.tensor_add(T["base"][:, :], T["base"][:, :],
                                              T["tmpb"][:, :]),
                  reads=["base", "tmpb"], writes=["base"])
            pr.op('v', (lambda e, _e=e8: e.tensor_mul(T["tmpb"][:, :],
                                                      T["EQ"][:, _e * 256:(_e + 1) * 256],
                                                      T["rank"][:, _e * 256:(_e + 1) * 256])),
                  reads=["EQ", "rank"], writes=["tmpb"])
            pr.op('v', lambda e: e.tensor_add(T["rtk"][:, :], T["rtk"][:, :], T["tmpb"][:, :]),
                  reads=["rtk", "tmpb"], writes=["rtk"])
    pr.op('v', lambda e: e.tensor_add(T["postk"][:, :], T["base"][:, :], T["rtk"][:, :]),
          reads=["base", "rtk"], writes=["postk"])
    pr.op('g', lambda e: e.iota(T["tokid"][:, :], pattern=[[1, 256]], base=0,
                                channel_multiplier=0), writes=["tokid"])
    pr.op('v', lambda e: e.tensor_copy(T["iotf"][:, :], T["tokid"][:, :]),
          reads=["tokid"], writes=["iotf"])
    pr.op('v', lambda e: e.tensor_scalar(T["colsel"][0:8, :], T["iotf"][0:8, :],
                                         T["myc32"][0:8, 0:1], None, ALU.is_equal),
          reads=["iotf", "myc32"], writes=["colsel"])
    pr.op('v', lambda e: e.tensor_mul(T["tmp8"][0:8, :], T["pfx"][0:8, :],
                                      T["colsel"][0:8, :]),
          reads=["pfx", "colsel"], writes=["tmp8"])
    pr.op('v', lambda e: e.reduce_sum(T["base8"][0:8, 0:1], T["tmp8"][0:8, :], AX.X),
          reads=["tmp8"], writes=["base8"])
    pr.dma('y', T["base8row"][0:1, :].rearrange("o (n t) -> o n t", t=1),
           T["base8"][0:8, 0:1])
    for e8 in range(8):
        mm(T["psC"][:, 0:1], "psC", T["ones1f"][0:1, :],
           T["base8row"][0:1, e8:e8 + 1], True, True, ["base8row", "ones1f"])
        pr.op('v', lambda e: e.tensor_copy(T["brdcol"][:, 0:1], T["psC"][:, 0:1]),
              reads=["psC"], writes=["brdcol"])
        if e8 == 0:
            pr.op('v', lambda e: e.tensor_scalar(T["baseTok"][:, :], T["EQ"][:, 0:256],
                                                 T["brdcol"][:, 0:1], None, ALU.mult),
                  reads=["EQ", "brdcol"], writes=["baseTok"])
        else:
            pr.op('v', (lambda e, _e=e8: e.tensor_scalar(
                T["tmpb"][:, :], T["EQ"][:, _e * 256:(_e + 1) * 256],
                T["brdcol"][:, 0:1], None, ALU.mult)),
                  reads=["EQ", "brdcol"], writes=["tmpb"])
            pr.op('v', lambda e: e.tensor_add(T["baseTok"][:, :], T["baseTok"][:, :],
                                              T["tmpb"][:, :]),
                  reads=["baseTok", "tmpb"], writes=["baseTok"])
    pr.op('v', lambda e: e.tensor_sub(T["lr"][:, :], T["postk"][:, :],
                                      T["baseTok"][:, :]),
          reads=["postk", "baseTok"], writes=["lr"])
    pr.op('v', lambda e: e.tensor_single_scalar(T["slotf"][:, :], T["evsb"][:, :],
                                                float(CLOC), ALU.mult),
          reads=["evsb"], writes=["slotf"])
    pr.op('v', lambda e: e.tensor_add(T["slotf"][:, :], T["slotf"][:, :], T["lr"][:, :]),
          reads=["slotf", "lr"], writes=["slotf"])
    pr.op('v', lambda e: e.tensor_single_scalar(T["kgm"][:, :], T["postk"][:, :],
                                                6144.0, ALU.is_lt),
          reads=["postk"], writes=["kgm"])
    pr.op('v', lambda e: e.tensor_single_scalar(T["klm"][:, :], T["lr"][:, :],
                                                float(CLOC) - 0.5, ALU.is_lt),
          reads=["lr"], writes=["klm"])
    pr.op('v', lambda e: e.tensor_mul(T["keepA"][:, :], T["kgm"][:, :], T["klm"][:, :]),
          reads=["kgm", "klm"], writes=["keepA"])
    pr.op('v', lambda e: e.tensor_single_scalar(T["gidxf"][:, :], T["slotf"][:, :],
                                                float(NSLOT), ALU.subtract),
          reads=["slotf"], writes=["gidxf"])
    pr.op('v', lambda e: e.tensor_mul(T["gidxf"][:, :], T["gidxf"][:, :],
                                      T["keepA"][:, :]),
          reads=["gidxf", "keepA"], writes=["gidxf"])
    pr.op('v', lambda e: e.tensor_single_scalar(T["gidxf"][:, :], T["gidxf"][:, :],
                                                float(NSLOT), ALU.add),
          reads=["gidxf"], writes=["gidxf"])
    dd = getattr(pr, "dbgdump", None)
    if dd is not None:
        for i, nm in enumerate(("evsb", "postk", "baseTok", "lr", "kgm", "klm",
                                "slotf", "gidxf")):
            pr.dma('y', dd[i * 128:(i + 1) * 128, :], T[nm][:, :])
    for h in range(2):
        tr(T["psA"][:, 0:128], "psA", T["gidxf"][:, h * 128:(h + 1) * 128], ["gidxf"])
        pr.op('v', lambda e: e.tensor_copy(T["gstg"][:, :], T["psA"][:, 0:128]),
              reads=["psA"], writes=["gstg"])
        pr.dma('y', gidxT[h * 128:(h + 1) * 128, :], T["gstg"][:, :])
    pr.op('v', lambda e: e.tensor_scalar(T["dcol"][:, :], T["iotf"][:, :],
                                         T["myc32"][:, 0:1], None, ALU.subtract),
          reads=["iotf", "myc32"], writes=["dcol"])
    pr.op('v', lambda e: e.tensor_single_scalar(T["mOwn"][:, :], T["dcol"][:, :], -0.5,
                                                ALU.is_gt),
          reads=["dcol"], writes=["mOwn"])
    pr.op('v', lambda e: e.tensor_single_scalar(T["tmpb"][:, :], T["dcol"][:, :], 31.5,
                                                ALU.is_lt),
          reads=["dcol"], writes=["tmpb"])
    pr.op('v', lambda e: e.tensor_mul(T["mOwn"][:, :], T["mOwn"][:, :], T["tmpb"][:, :]),
          reads=["mOwn", "tmpb"], writes=["mOwn"])
    pr.op('v', lambda e: e.tensor_mul(T["keepA"][:, :], T["keepA"][:, :],
                                      T["mOwn"][:, :]),
          reads=["keepA", "mOwn"], writes=["keepA"])
    pr.op('v', lambda e: e.tensor_single_scalar(T["mu8b"][:, :], T["keepA"][:, :], 0.5,
                                                ALU.is_gt),
          reads=["keepA"], writes=["mu8b"])
    pr.op('g', lambda e: e.iota(T["tokid"][:, :], pattern=[[128, 256]], base=0,
                                channel_multiplier=1), writes=["tokid"])
    pr.op('v', lambda e: e.tensor_copy(T["tokidf"][:, :], T["tokid"][:, :]),
          reads=["tokid"], writes=["tokidf"])
    pr.op('v', lambda e: e.tensor_scalar_mul(T["brdcol"][:, 0:1], T["mycol"][:, 0:1],
                                             4096.0),
          reads=["mycol"], writes=["brdcol"])
    pr.op('v', lambda e: e.tensor_scalar(T["tokidf"][:, :], T["tokidf"][:, :],
                                         T["brdcol"][:, 0:1], None, ALU.subtract),
          reads=["tokidf", "brdcol"], writes=["tokidf"])
    pr.op('v', lambda e: e.tensor_copy(T["tokid"][:, :], T["tokidf"][:, :]),
          reads=["tokidf"], writes=["tokid"])
    pr.op('v', lambda e: e.tensor_copy(T["slot"][:, :], T["slotf"][:, :]),
          reads=["slotf"], writes=["slot"])
    pr.op('v', lambda e: e.tensor_single_scalar(T["sh"][:, :], T["slot"][:, :], 7,
                                                ALU.arith_shift_right),
          reads=["slot"], writes=["sh"])
    pr.op('v', lambda e: e.tensor_single_scalar(T["rem"][:, :], T["sh"][:, :], 7,
                                                ALU.arith_shift_left),
          reads=["sh"], writes=["rem"])
    pr.op('v', lambda e: e.tensor_sub(T["rem"][:, :], T["slot"][:, :], T["rem"][:, :]),
          reads=["slot", "rem"], writes=["rem"])
    pr.op('v', lambda e: e.tensor_single_scalar(T["rem"][:, :], T["rem"][:, :], 64,
                                                ALU.mult),
          reads=["rem"], writes=["rem"])
    pr.op('v', lambda e: e.tensor_add(T["tgt"][:, :], T["rem"][:, :], T["sh"][:, :]),
          reads=["rem", "sh"], writes=["tgt"])
    pr.op('g', lambda e: e.memset(T["trash"][:, :], NSLOT), writes=["trash"])
    pr.op('v', lambda e: e.select(T["tgt2"][:, :], T["mu8b"][:, :], T["tgt"][:, :],
                                  T["trash"][:, :]),
          reads=["tgt", "mu8b", "trash"], writes=["tgt2"])
    pr.op('g', lambda e: e.memset(T["fill"][:, :], 4096), writes=["fill"])
    pr.dma('g', srcd[0:NSLOT, 0:1].rearrange("(p j) o -> p (j o)", p=128),
           T["fill"][:, :])
    for j in range(256):
        pr.idma_scatter(srcd[:, :], T["tgt2"][:, j:j + 1], T["tokid"][:, j:j + 1])
    pr.dma('g', T["srcidx"][:, :],
           srcd[0:NSLOT, 0:1].rearrange("(p j) o -> p (j o)", p=128))

    for ex in range(8):
        for hf in range(2):
            pr.dma('g', T["w1sb%d" % hf][:, :].rearrange("p (c f) -> p c f", f=2048),
                   w1[ex].rearrange("(c p) (h f) -> p c h f", p=128, h=2)[:, :, hf, :])
        pr.dma('y', T["b1sb"][:, :], b1[ex, :].rearrange("(c p) -> p c", p=128),
               allow_slow_non_contiguous=True)
        pr.dma('y', T["b2sb"][:, :], b2[ex, :].rearrange("(c p) -> p c", p=128),
               allow_slow_non_contiguous=True)
        for g2 in range(2):
            g = ex * 2 + g2
            for tt in range(4):
                jj = g * 4 + tt
                pr.idma_gather(T["gbuf"][:, :], xnf[:, :], T["srcidx"][:, jj:jj + 1])
                pr.op('s', lambda e: e.copy(T["gbf"][:, :], T["gbuf"][:, :]),
                      reads=["gbuf"], writes=["gbf"])
                for c in range(8):
                    tr(T["psT"][0:128, 0:128], "psT",
                       T["gbf"][:, c * 128:(c + 1) * 128], ["gbf"], bf=True)
                    pr.op('v', (lambda e, _c=c, _tt=tt: e.tensor_copy(
                        T["bufT"][:, _c * 512 + _tt * 128:_c * 512 + _tt * 128 + 128],
                        T["psT"][0:128, 0:128])),
                          reads=["psT"], writes=["bufT"])
            for f in range(32):
                w1b = "w1sb%d" % (f // 16)
                fo = f % 16
                for c in range(8):
                    mm(T["psD"][:, 0:512], "psD",
                       T[w1b][:, c * 2048 + fo * 128:c * 2048 + (fo + 1) * 128],
                       T["bufT"][:, c * 512:(c + 1) * 512], c == 0, c == 7,
                       [w1b, "bufT"])
                pr.op('s', (lambda e, _f=f: e.activation(
                    T["hbf"][:, _f * 512:(_f + 1) * 512], T["psD"][:, 0:512], AF.Gelu,
                    bias=T["b1sb"][:, _f:_f + 1])),
                      reads=["psD", "b1sb"], writes=["hbf"])
            for m in range(8):
                w2b = "w2b0" if m % 2 == 0 else "w2b1"
                pr.dma('g', T[w2b][:, :].rearrange("p (c k) -> p c k", k=128),
                       w2[ex].rearrange("(c p) (m k) -> p c m k", p=128, m=8)[:, :, m, :])
                for fc in range(32):
                    mm(T["psE"][:, 0:512], "psE",
                       T[w2b][:, fc * 128:(fc + 1) * 128],
                       T["hbf"][:, fc * 512:(fc + 1) * 512], fc == 0, fc == 31,
                       [w2b, "hbf"])
                pr.op('v', (lambda e, _m=m: e.tensor_scalar(
                    T["eofm"][:, 0:512], T["psE"][:, 0:512], T["b2sb"][:, _m:_m + 1],
                    None, ALU.add)),
                      reads=["psE", "b2sb"], writes=["eofm"])
                for st in range(4):
                    tr(T["psA"][:, 0:128], "psA", T["eofm"][:, st * 128:(st + 1) * 128],
                       ["eofm"])
                    pr.op('v', lambda e: e.tensor_copy(T["estage"][:, :],
                                                       T["psA"][:, 0:128]),
                          reads=["psA"], writes=["estage"])
                    pr.dma('y', eo[g * 512 + st * 128:g * 512 + (st + 1) * 128,
                                   m * 128:(m + 1) * 128], T["estage"][:, :])



SBUFS_C = [
    ("idf", [128, 128], F32), ("idb", [128, 128], BF16), ("ioti", [128, 128], I32),
    ("onescol", [128, 1], BF16), ("ones1f", [1, 128], F32), ("onerow64", [1, 64], F32),
    ("routeb", [128, 64], F32), ("gidx", [128, 32], I32),
    ("mycol", [128, 1], F32),
    ("idxf", [128, 1], F32), ("idxi", [128, 1], I32), ("gaT", [128, 128], F32),
    ("xtok", [128, 1024], F32), ("grow", [128, 1024], F32),
    ("xft", [128, 1024], F32), ("gft", [128, 1024], F32), ("stp", [128, 1024], F32),
    ("x1t", [128, 1024], F32), ("x1bf", [128, 1024], BF16),
    ("xn1t", [128, 1024], F32),
    ("srow", [1, 512], F32), ("rsrep", [128, 128], F32), ("mrsrep", [128, 128], F32),
    ("vstg0", [128, 2048], BF16), ("vstg1", [128, 2048], BF16),
    ("wpbh", [128, 8192], BF16), ("wpbl", [128, 8192], BF16),
    ("wstg", [128, 1024], F32),
    ("xf32t", [128, 1024], F32), ("xht", [128, 1024], BF16), ("xlt", [128, 1024], BF16),
    ("q32", [128, 1024], F32), ("k32", [128, 1024], F32),
    ("Racc", [128, 128], F32),
    ("outfm", [128, 16 * 1024], BF16), ("ktok", [128, 16 * 1024], BF16),
    ("maskG", [128, 1024], F32),
    ("Gm", [128, 1024], F32), ("Rm", [128, 16], F32),
    ("bd2", [128, 2], F32), ("ep2", [2, 16], F32), ("esb", [16, 16], F32),
    ("attrow", [1, 256], F32), ("mxr", [16, 1], F32), ("negmax", [16, 1], F32),
    ("esum", [16, 1], F32), ("attsb", [16, 16], F32), ("ipat", [128, 128], F32),
    ("srw", [1, 128], F32), ("Smk", [128, 128], BF16),
    ("wo2", [128, 16 * 1024], BF16), ("bocol", [128, 8], F32),
    ("s1sb", [128, 1024], F32),
]
PSUMS_C = [
    ("psA", [128, 512], F32), ("psB", [128, 512], F32),
    ("psC", [128, 1024], F32), ("psD", [128, 2048], F32),
]


def emit_C(pr, x, s_fm, eof, route, gidxT, myexp, Wq, Wk, Wv, Wo, bo,
           x1_fm, xn1_fm, s1_fm, qscr, upto="all", dbg=False):
    T = pr.T
    c18 = pr.constcol(0.125)

    def mm(ps_ap, ps_key, lhsT, rhs, start, stop, reads, tp=None):
        pr.op('t', (lambda e, _p=ps_ap, _l=lhsT, _r=rhs, _s=start, _e2=stop, _tp=tp:
                    e.matmul(_p, _l, _r, start=_s, stop=_e2, tile_position=_tp)),
              reads=reads, writes=[ps_key])

    def tr(ps_ap, ps_key, in_ap, reads, pin=128, bf=False):
        ident = (T["idb"] if bf else T["idf"])[0:pin, 0:pin]
        pr.op('t', (lambda e, _p=ps_ap, _i=in_ap, _id=ident: e.transpose(_p, _i, _id)),
              reads=list(reads) + ["idb" if bf else "idf"], writes=[ps_key])

    pr.op('g', lambda e: e.iota(T["ioti"][:, :], pattern=[[-1, 128]], base=0,
                                channel_multiplier=1), writes=["ioti"])
    pr.op('v', lambda e: e.tensor_copy(T["idf"][:, :], T["ioti"][:, :]),
          reads=["ioti"], writes=["idf"])
    pr.op('v', lambda e: e.tensor_single_scalar(T["idf"][:, :], T["idf"][:, :], 0.0,
                                                ALU.is_equal),
          reads=["idf"], writes=["idf"])
    pr.op('v', lambda e: e.tensor_copy(T["idb"][:, :], T["idf"][:, :]),
          reads=["idf"], writes=["idb"])
    pr.op('g', lambda e: e.memset(T["onescol"][:, :], 1.0), writes=["onescol"])
    pr.op('g', lambda e: e.memset(T["ones1f"][:, :], 1.0), writes=["ones1f"])
    pr.op('g', lambda e: e.memset(T["onerow64"][:, :], 1.0), writes=["onerow64"])
    pr.op('v', lambda e: e.tensor_copy(T["Gm"][:, 0:128], T["ioti"][:, :]),
          reads=["ioti"], writes=["Gm"])
    pr.op('v', lambda e: e.tensor_single_scalar(T["ipat"][:, :], T["Gm"][:, 0:128], 0.0,
                                                ALU.is_equal),
          reads=["Gm"], writes=["ipat"])
    pr.op('v', lambda e: e.tensor_single_scalar(T["Gm"][:, 128:256], T["Gm"][:, 0:128],
                                                64.0, ALU.is_equal),
          reads=["Gm"], writes=["Gm"])
    pr.op('v', lambda e: e.tensor_add(T["ipat"][:, :], T["ipat"][:, :],
                                      T["Gm"][:, 128:256]),
          reads=["ipat", "Gm"], writes=["ipat"])
    pr.op('v', lambda e: e.tensor_single_scalar(T["Gm"][:, 128:256], T["Gm"][:, 0:128],
                                                -64.0, ALU.is_equal),
          reads=["Gm"], writes=["Gm"])
    pr.op('v', lambda e: e.tensor_add(T["ipat"][:, :], T["ipat"][:, :],
                                      T["Gm"][:, 128:256]),
          reads=["ipat", "Gm"], writes=["ipat"])
    for bb in range(16):
        pr.op('v', (lambda e, _b=bb: e.tensor_copy(
            T["maskG"][:, _b * 64:(_b + 1) * 64], T["ipat"][:, 0:64])),
              reads=["ipat"], writes=["maskG"])
    pr.op('g', lambda e: e.memset(T["bd2"][:, :], 0.0), writes=["bd2"])
    pr.op('g', lambda e: e.memset(T["bd2"][0:64, 0:1], 1.0), writes=["bd2"])
    pr.op('g', lambda e: e.memset(T["bd2"][64:128, 1:2], 1.0), writes=["bd2"])
    pr.dma('y', T["bocol"][:, :], bo[:].rearrange("(c p) -> p c", p=128),
           allow_slow_non_contiguous=True)
    for par in range(2):
        pr.dma('g', T["wo2"][par * 64:(par + 1) * 64, :]
               .rearrange("p (r k) -> p r k", k=1024),
               Wo[:].rearrange("(r p) k -> p r k", p=64))
    pr.dma('y', T["routeb"][:, :].rearrange("p (j c) -> p j c", c=2),
           route[:].rearrange("(j p) c -> p j c", p=128))
    pr.dma('y', T["mycol"][0:1, 0:1], myexp[0:1, 0:1])
    mm(T["psA"][:, 0:1], "psA", T["ones1f"][0:1, :], T["mycol"][0:1, 0:1], True, True,
       ["mycol", "ones1f"])
    pr.op('v', lambda e: e.tensor_copy(T["mycol"][:, 0:1], T["psA"][:, 0:1]),
          reads=["psA"], writes=["mycol"])
    pr.op('v', lambda e: e.tensor_scalar_mul(T["mycol"][:, 0:1], T["mycol"][:, 0:1],
                                             32.0),
          reads=["mycol"], writes=["mycol"])
    pr.op('g', lambda e: e.iota(T["idxi"][:, 0:1], pattern=[[1, 1]], base=0,
                                channel_multiplier=1), writes=["idxi"])
    pr.op('v', lambda e: e.tensor_copy(T["idxf"][:, 0:1], T["idxi"][:, 0:1]),
          reads=["idxi"], writes=["idxf"])
    pr.op('v', lambda e: e.tensor_add(T["idxf"][:, 0:1], T["idxf"][:, 0:1],
                                      T["mycol"][:, 0:1]),
          reads=["idxf", "mycol"], writes=["idxf"])
    pr.op('v', lambda e: e.tensor_copy(T["idxi"][:, 0:1], T["idxf"][:, 0:1]),
          reads=["idxf"], writes=["idxi"])
    pr.idma_gather(T["gaT"][:, :], gidxT[:, :], T["idxi"][:, 0:1])
    tr(T["psA"][:, 0:128], "psA", T["gaT"][:, :], ["gaT"])
    pr.op('v', lambda e: e.tensor_copy(T["gidx"][:, :], T["psA"][:, 0:32]),
          reads=["psA"], writes=["gidx"])

    for b in range(2):
        for jt in range(16):
            jj = b * 16 + jt
            pr.dma('y', T["xtok"][:, :], x[b, jt * 128:(jt + 1) * 128, :])
            pr.idma_gather(T["grow"][:, :], eof[:, :], T["gidx"][:, jj:jj + 1])
            pr.op('s', (lambda e, _jj=jj: e.activation(
                T["grow"][:, :], T["grow"][:, :], AF.Silu,
                scale=T["routeb"][:, 2 * _jj + 1:2 * _jj + 2])),
                  reads=["grow", "routeb"], writes=["grow"])
            for c in range(8):
                tr(T["psA"][:, 0:128], "psA", T["xtok"][:, c * 128:(c + 1) * 128], ["xtok"])
                pr.op('v', (lambda e, _c=c: e.tensor_copy(
                    T["xft"][:, _c * 128:(_c + 1) * 128], T["psA"][:, 0:128])),
                      reads=["psA"], writes=["xft"])
                tr(T["psB"][:, 0:128], "psB", T["grow"][:, c * 128:(c + 1) * 128], ["grow"])
                pr.op('v', (lambda e, _c=c: e.tensor_copy(
                    T["gft"][:, _c * 128:(_c + 1) * 128], T["psB"][:, 0:128])),
                      reads=["psB"], writes=["gft"])
            pr.dma('y', T["stp"][:, :].rearrange("p (c q) -> p c q", q=128),
                   s_fm[b, :, jt * 128:(jt + 1) * 128]
                   .rearrange("(c p) q -> p c q", p=128))
            pr.op('v', lambda e: e.tensor_mul(T["gft"][:, :], T["gft"][:, :], T["stp"][:, :]),
                  reads=["gft", "stp"], writes=["gft"])
            pr.op('v', lambda e: e.tensor_add(T["x1t"][:, :], T["xft"][:, :], T["gft"][:, :]),
                  reads=["xft", "gft"], writes=["x1t"])
            pr.dma('y', x1_fm[b, :, jt * 128:(jt + 1) * 128]
                   .rearrange("(c p) q -> p c q", p=128),
                   T["x1t"][:, :].rearrange("p (c q) -> p c q", q=128))
            pr.op('s', lambda e: e.copy(T["x1bf"][:, :], T["x1t"][:, :]),
                  reads=["x1t"], writes=["x1bf"])
            for c in range(8):
                mm(T["psA"][0:1, 0:128], "psA", T["onescol"][:, 0:1],
                   T["x1bf"][:, c * 128:(c + 1) * 128], c == 0, c == 7,
                   ["onescol", "x1bf"])
            pr.op('s', lambda e: e.activation(T["x1bf"][:, :], T["x1bf"][:, :], AF.Square),
                  reads=["x1bf"], writes=["x1bf"])
            for c in range(8):
                mm(T["psB"][0:1, 0:128], "psB", T["onescol"][:, 0:1],
                   T["x1bf"][:, c * 128:(c + 1) * 128], c == 0, c == 7,
                   ["onescol", "x1bf"])
            pr.op('v', lambda e: e.tensor_scalar_mul(T["srow"][0:1, 0:128],
                                                     T["psA"][0:1, 0:128], 1.0 / 1024),
                  reads=["psA"], writes=["srow"])
            pr.op('v', lambda e: e.tensor_scalar_mul(T["srow"][0:1, 128:256],
                                                     T["psB"][0:1, 0:128], 1.0 / 1024),
                  reads=["psB"], writes=["srow"])
            pr.op('v', lambda e: e.tensor_mul(T["srow"][0:1, 256:384],
                                              T["srow"][0:1, 0:128], T["srow"][0:1, 0:128]),
                  reads=["srow"], writes=["srow"])
            pr.op('v', lambda e: e.tensor_sub(T["srow"][0:1, 128:256],
                                              T["srow"][0:1, 128:256],
                                              T["srow"][0:1, 256:384]),
                  reads=["srow"], writes=["srow"])
            pr.op('v', lambda e: e.tensor_scalar_add(T["srow"][0:1, 128:256],
                                                     T["srow"][0:1, 128:256], 1e-5),
                  reads=["srow"], writes=["srow"])
            pr.op('s', lambda e: e.activation(T["srow"][0:1, 128:256],
                                              T["srow"][0:1, 128:256], AF.Ln),
                  reads=["srow"], writes=["srow"])
            pr.op('v', lambda e: e.tensor_scalar_mul(T["srow"][0:1, 128:256],
                                                     T["srow"][0:1, 128:256], -0.5),
                  reads=["srow"], writes=["srow"])
            pr.op('s', lambda e: e.activation(T["srow"][0:1, 128:256],
                                              T["srow"][0:1, 128:256], AF.Exp),
                  reads=["srow"], writes=["srow"])
            pr.op('v', lambda e: e.tensor_mul(T["srow"][0:1, 0:128], T["srow"][0:1, 0:128],
                                              T["srow"][0:1, 128:256]),
                  reads=["srow"], writes=["srow"])
            mm(T["psC"][:, 0:128], "psC", T["ones1f"][0:1, :], T["srow"][0:1, 128:256],
               True, True, ["srow", "ones1f"])
            pr.op('v', lambda e: e.tensor_copy(T["rsrep"][:, :], T["psC"][:, 0:128]),
                  reads=["psC"], writes=["rsrep"])
            mm(T["psC"][:, 0:128], "psC", T["ones1f"][0:1, :], T["srow"][0:1, 0:128],
               True, True, ["srow", "ones1f"])
            pr.op('v', lambda e: e.tensor_copy(T["mrsrep"][:, :], T["psC"][:, 0:128]),
                  reads=["psC"], writes=["mrsrep"])
            for c in range(8):
                pr.op('v', (lambda e, _c=c: e.tensor_mul(
                    T["xn1t"][:, _c * 128:(_c + 1) * 128],
                    T["x1t"][:, _c * 128:(_c + 1) * 128], T["rsrep"][:, :])),
                      reads=["x1t", "rsrep"], writes=["xn1t"])
                pr.op('v', (lambda e, _c=c: e.tensor_sub(
                    T["xn1t"][:, _c * 128:(_c + 1) * 128],
                    T["xn1t"][:, _c * 128:(_c + 1) * 128], T["mrsrep"][:, :])),
                      reads=["xn1t", "mrsrep"], writes=["xn1t"])
            pr.dma('y', xn1_fm[b, :, jt * 128:(jt + 1) * 128]
                   .rearrange("(c p) q -> p c q", p=128),
                   T["xn1t"][:, :].rearrange("p (c q) -> p c q", q=128))

        if upto == "combine":
            continue
        pr.op('g', lambda e: e.memset(T["Racc"][:, :], 0.0), writes=["Racc"])
        for (wdram, wkey) in ((Wq, 0), (Wk, 1)):
            for c in range(8):
                pr.dma('y', T["wstg"][:, :], wdram[c * 128:(c + 1) * 128, :])
                pr.op('s', (lambda e, _c=c: e.copy(
                    T["wpbh"][:, _c * 1024:(_c + 1) * 1024], T["wstg"][:, :])),
                      reads=["wstg"], writes=["wpbh"])
                pr.op('v', (lambda e, _c=c: e.tensor_sub(
                    T["xf32t"][:, :], T["wstg"][:, :],
                    T["wpbh"][:, _c * 1024:(_c + 1) * 1024])),
                      reads=["wstg", "wpbh"], writes=["xf32t"])
                pr.op('v', (lambda e, _c=c: e.tensor_copy(
                    T["wpbl"][:, _c * 1024:(_c + 1) * 1024], T["xf32t"][:, :])),
                      reads=["xf32t"], writes=["wpbl"])
            dstps = "psC" if wkey == 0 else "psD"
            dst32 = "q32" if wkey == 0 else "k32"
            for jt in range(16):
                pr.dma('y', T["xf32t"][:, :].rearrange("p (c q) -> p c q", q=128),
                       xn1_fm[b, :, jt * 128:(jt + 1) * 128]
                       .rearrange("(c p) q -> p c q", p=128))
                pr.op('s', lambda e: e.copy(T["xht"][:, :], T["xf32t"][:, :]),
                      reads=["xf32t"], writes=["xht"])
                pr.op('v', lambda e: e.tensor_sub(T["grow"][:, :], T["xf32t"][:, :],
                                                  T["xht"][:, :]),
                      reads=["xf32t", "xht"], writes=["grow"])
                pr.op('v', lambda e: e.tensor_copy(T["xlt"][:, :], T["grow"][:, :]),
                      reads=["grow"], writes=["xlt"])
                for c in range(8):
                    for h2 in range(2):
                        for (sx, sw) in (("xht", "wpbh"), ("xht", "wpbl"),
                                         ("xlt", "wpbh")):
                            mm(T[dstps][:, h2 * 512:(h2 + 1) * 512], dstps,
                               T[sx][:, c * 128:(c + 1) * 128],
                               T[sw][:, c * 1024 + h2 * 512:c * 1024 + (h2 + 1) * 512],
                               (c == 0 and sx == "xht" and sw == "wpbh"), c == 7,
                               [sx, sw])
                pr.op('v', (lambda e, _d=dst32, _p=dstps: e.tensor_copy(
                    T[_d][:, :], T[_p][:, 0:1024])),
                      reads=[dstps], writes=[dst32])
                if wkey == 0:
                    pr.dma('y', qscr[jt * 128:(jt + 1) * 128, :], T["q32"][:, :])
                if wkey == 1:
                    pr.dma('y', T["q32"][:, :], qscr[jt * 128:(jt + 1) * 128, :])
                    for m in range(8):
                        for h2 in range(2):
                            mm(T["psB"][:, 0:512], "psB",
                               T["k32"][:, m * 128:(m + 1) * 128],
                               T["q32"][:, h2 * 512:(h2 + 1) * 512], h2 == 0, False,
                               ["k32", "q32"]) if False else None
                        for h2 in range(2):
                            mm(T["psC"][:, h2 * 512:(h2 + 1) * 512], "psC",
                               T["q32"][:, m * 128:(m + 1) * 128],
                               T["k32"][:, h2 * 512:(h2 + 1) * 512], True, True,
                               ["q32", "k32"])
                        pr.op('v', lambda e: e.tensor_mul(T["Gm"][:, :],
                                                          T["psC"][:, 0:1024],
                                                          T["maskG"][:, :]),
                              reads=["psC", "maskG"], writes=["Gm"])
                        pr.op('v', (lambda e, _m=m: e.tensor_reduce(
                            T["Rm"][:, :],
                            T["Gm"][:, :].rearrange("p (z k) -> p z k", k=64),
                            AX.X, ALU.add)),
                              reads=["Gm"], writes=["Rm"])
                        pr.op('v', (lambda e, _m=m: e.tensor_add(
                            T["Racc"][:, _m * 16:(_m + 1) * 16], 
                            T["Racc"][:, _m * 16:(_m + 1) * 16], T["Rm"][:, :])),
                              reads=["Racc", "Rm"], writes=["Racc"])
        if upto == "qk":
            continue
        for m in range(8):
            if upto == "noattn":
                break
            mm(T["psB"][0:2, 0:16], "psB", T["bd2"][:, :],
               T["Racc"][:, m * 16:(m + 1) * 16], True, True, ["bd2", "Racc"])
            pr.op('v', lambda e: e.tensor_copy(T["ep2"][:, :], T["psB"][0:2, 0:16]),
                  reads=["psB"], writes=["ep2"])
            pr.dma('y', T["esb"][2 * m:2 * m + 2, :], T["ep2"][:, :])
        pr.op('v', lambda e: e.reduce_max(T["mxr"][:, :], T["esb"][:, :], AX.X),
              reads=["esb"], writes=["mxr"])
        pr.op('v', lambda e: e.tensor_scalar_mul(T["negmax"][:, :], T["mxr"][:, :], -0.125),
              reads=["mxr"], writes=["negmax"])
        pr.op('s', lambda e: e.activation(T["attsb"][:, :], T["esb"][:, :], AF.Exp,
                                          bias=T["negmax"][0:16, 0:1],
                                          scale=T[c18][0:16, 0:1],
                                          accum_out=T["esum"][0:16, 0:1]),
              reads=["esb", "negmax", c18], writes=["attsb", "esum"])
        pr.op('v', lambda e: e.reciprocal(T["esum"][:, :], T["esum"][:, :]),
              reads=["esum"], writes=["esum"])
        pr.op('v', lambda e: e.tensor_scalar(T["attsb"][:, :], T["attsb"][:, :],
                                             T["esum"][0:16, 0:1], None, ALU.mult),
              reads=["attsb", "esum"], writes=["attsb"])
        pr.dma('y', T["attrow"][0:1, :].rearrange("o (a z) -> o a z", z=16),
               T["attsb"][0:16, :])
        if dbg:
            pr.dma('y', dbg_att[b, :, :], T["attsb"][0:16, :])
            pr.dma('y', dbg_e[b, :, :], T["esb"][0:16, :])
            pr.dma('g', dbg_q[b, :, :], T["outfm"][:, 0:1024])
        for m in range(8):
            wb = "wpbh"
            pr.dma('g', T[wb][:, 0:1024].rearrange("p (c k) -> p c k", k=128),
                   Wv[:].rearrange("(c p) (m k) -> p c m k", p=128, m=8)[:, :, m, :])
            for c in range(8):
                vs = "vstg0" if c % 2 == 0 else "vstg1"
                pr.dma('g', T[vs][:, :], xn1_fm[b, c * 128:(c + 1) * 128, :])
                for q4 in range(4):
                    mm(T["psD"][:, q4 * 512:(q4 + 1) * 512], "psD",
                       T[wb][:, c * 128:(c + 1) * 128],
                       T[vs][:, q4 * 512:(q4 + 1) * 512],
                       c == 0, c == 7, [vs, wb])
            pr.op('v', (lambda e, _m=m: e.tensor_copy(
                T["ktok"][:, _m * 2048:(_m + 1) * 2048], T["psD"][:, 0:2048])),
                  reads=["psD"], writes=["ktok"])
        for m in range(8):
            if upto in ("noattn", "nogram"):
                break
            for k8 in range(8):
                for rh in range(2):
                    for ch in range(2):
                        idxv = 16 * (2 * m + ch) + 2 * k8 + rh
                        pr.op('v', (lambda e, _rh=rh, _ch=ch, _ix=idxv: e.tensor_scalar(
                            T["srw"][0:1, _ch * 64:(_ch + 1) * 64], T["onerow64"][0:1, :],
                            T["attrow"][0:1, _ix:_ix + 1], None, ALU.mult)),
                              reads=["onerow64", "attrow"], writes=["srw"])
                    mm(T["psB"][rh * 64:(rh + 1) * 64, 0:128], "psB",
                       T["ones1f"][0:1, 0:64], T["srw"][0:1, :], True, True,
                       ["ones1f", "srw"], tp=(0, 64) if rh == 1 else None)
                pr.op('v', lambda e: e.tensor_mul(T["Smk"][:, :], T["ipat"][:, :],
                                                  T["psB"][:, 0:128]),
                      reads=["ipat", "psB"], writes=["Smk"])
                for q4 in range(4):
                    mm(T["psD"][:, q4 * 512:(q4 + 1) * 512], "psD", T["Smk"][:, :],
                       T["ktok"][:, k8 * 2048 + q4 * 512:k8 * 2048 + (q4 + 1) * 512],
                       k8 == 0, k8 == 7, ["Smk", "ktok"])
            pr.op('v', (lambda e, _m=m: e.tensor_copy(
                T["outfm"][:, _m * 2048:(_m + 1) * 2048], T["psD"][:, 0:2048])),
                  reads=["psD"], writes=["outfm"])
        if dbg:
            pr.dma('g', dbg_out[b, :, :], T["outfm"][:, 0:2048])
        for m8 in range(8):
            pr.op('v', (lambda e, _m=m8: e.tensor_copy(
                T["ktok"][:, _m * 2048:(_m + 1) * 2048]
                .rearrange("p (r q) -> p r q", q=128),
                T["outfm"][:, _m * 2048:(_m + 1) * 2048]
                .rearrange("p (q r) -> p q r", r=16).transpose([0, 2, 1]))),
                  reads=["outfm"], writes=["ktok"])
        for dm in range(8):
            if upto in ("noattn", "nogram", "nokron"):
                break
            for a in range(16):
                par = a % 2
                mslab = a // 2
                apos = (a // 2) + par * 8
                for r in range(16):
                    mov = T["ktok"][par * 64:(par + 1) * 64,
                                    mslab * 2048 + r * 128:mslab * 2048 + (r + 1) * 128]
                    sta = T["wo2"][par * 64:(par + 1) * 64,
                                   r * 1024 + dm * 128:r * 1024 + (dm + 1) * 128]
                    mm(T["psD"][:, apos * 128:(apos + 1) * 128], "psD", sta, mov,
                       r == 0, r == 15, ["wo2", "ktok"],
                       tp=(64, 0) if par == 1 else None)
            for half in range(2):
                for a8 in range(8):
                    a = half * 8 + a8
                    apos = (a // 2) + (a % 2) * 8
                    pr.op('v', (lambda e, _dm=dm, _a8=a8, _ap=apos: e.tensor_scalar(
                        T["s1sb"][:, _a8 * 128:(_a8 + 1) * 128],
                        T["psD"][:, _ap * 128:(_ap + 1) * 128],
                        T["bocol"][:, _dm:_dm + 1], None, ALU.add)),
                          reads=["psD", "bocol"], writes=["s1sb"])
                pr.dma('y', s1_fm[b, dm * 128:(dm + 1) * 128,
                                  half * 1024:(half + 1) * 1024], T["s1sb"][:, :])



SBUFS_D = [
    ("idf", [128, 128], F32), ("ioti", [128, 128], I32),
    ("onescol", [128, 1], BF16), ("ones1f", [1, 128], F32),
    ("xn1bf", [128, 8 * 2048], BF16),
    ("q16", [128, 4 * 2048], BF16), ("k16", [128, 4 * 2048], BF16),
    ("vtok2", [128, 16 * 1040], BF16),
    ("expT", [128, 16 * 512], BF16), ("og", [128, 8192], BF16),
    ("wst0", [128, 4096], BF16),
    ("xstg", [128, 1024], F32), ("xh", [128, 1024], BF16), ("xl", [128, 1024], BF16),
    ("h32", [128, 512], F32), ("hr", [128, 512], F32), ("hrT", [128, 512], F32),
    ("stats", [128, 8], F32), ("probs", [128, 8], F32), ("mx8", [128, 8], F32),
    ("mi8", [128, 8], U32), ("tif", [128, 2], F32), ("iotaf", [128, 8], F32),
    ("eq1", [128, 8], F32), ("eq2", [128, 8], F32), ("gts", [128, 8], F32),
    ("rb1rep", [128, 512], F32), ("grep", [128, 512], F32), ("brep", [128, 512], F32),
    ("rb2rep", [128, 8], F32), ("rw2T", [128, 32], F32),
    ("gflat", [1, 2048], F32), ("obrow", [1, 1024], F32),
    ("divrep", [128, 512], F32), ("zr", [1, 512], F32), ("dvr", [1, 512], F32),
    ("ostg", [128, 512], F32), ("x2st", [128, 2048], F32),
    ("x1c", [128, 2048], F32), ("s1c", [128, 2048], F32),
    ("gt8", [8, 128], F32),
]
PSUMS_D = [
    ("psC", [128, 512], F32), ("psD", [128, 2048], F32),
    ("psE", [128, 512], F32), ("psF", [128, 512], F32),
    ("psG", [128, 512], F32),
]


def emit_D(pr, x1_fm, xn1_fm, s1_fm, rW1, rb1, lng, lnb, rW2, rb2, qW, qb,
           kW, kb, vW, vb, oW, ob, out, gscr, oacc):
    T = pr.T
    c18 = pr.constcol(0.125)

    def mm(ps_ap, ps_key, lhsT, rhs, start, stop, reads, tp=None):
        pr.op('t', (lambda e, _p=ps_ap, _l=lhsT, _r=rhs, _s=start, _e2=stop, _tp=tp:
                    e.matmul(_p, _l, _r, start=_s, stop=_e2, tile_position=_tp)),
              reads=reads, writes=[ps_key])

    def tr(ps_ap, ps_key, in_ap, reads, pin=128):
        ident = T["idf"][0:pin, 0:pin]
        pr.op('t', (lambda e, _p=ps_ap, _i=in_ap, _id=ident: e.transpose(_p, _i, _id)),
              reads=list(reads) + ["idf"], writes=[ps_key])

    pr.op('g', lambda e: e.iota(T["ioti"][:, :], pattern=[[-1, 128]], base=0,
                                channel_multiplier=1), writes=["ioti"])
    pr.op('v', lambda e: e.tensor_copy(T["idf"][:, :], T["ioti"][:, :]),
          reads=["ioti"], writes=["idf"])
    pr.op('v', lambda e: e.tensor_single_scalar(T["idf"][:, :], T["idf"][:, :], 0.0,
                                                ALU.is_equal),
          reads=["idf"], writes=["idf"])
    pr.op('g', lambda e: e.memset(T["onescol"][:, :], 1.0), writes=["onescol"])
    pr.op('g', lambda e: e.memset(T["ones1f"][:, :], 1.0), writes=["ones1f"])
    pr.op('g', lambda e: e.memset(T["vtok2"][:, :], 1.0), writes=["vtok2"])
    pr.op('g', lambda e: e.iota(T["ioti"][:, 0:8], pattern=[[1, 8]], base=0,
                                channel_multiplier=0), writes=["ioti"])
    pr.op('v', lambda e: e.tensor_copy(T["iotaf"][:, :], T["ioti"][:, 0:8]),
          reads=["ioti"], writes=["iotaf"])
    pr.dma('y', T["hr"][0:1, 0:512], rb1[:][None, :])
    mm(T["psC"][:, 0:512], "psC", T["ones1f"][0:1, :], T["hr"][0:1, 0:512], True, True,
       ["hr", "ones1f"])
    pr.op('v', lambda e: e.tensor_copy(T["rb1rep"][:, :], T["psC"][:, 0:512]),
          reads=["psC"], writes=["rb1rep"])
    pr.dma('y', T["hr"][0:1, 0:512], lng[:][None, :])
    mm(T["psC"][:, 0:512], "psC", T["ones1f"][0:1, :], T["hr"][0:1, 0:512], True, True,
       ["hr", "ones1f"])
    pr.op('v', lambda e: e.tensor_copy(T["grep"][:, :], T["psC"][:, 0:512]),
          reads=["psC"], writes=["grep"])
    pr.dma('y', T["hr"][0:1, 0:512], lnb[:][None, :])
    mm(T["psC"][:, 0:512], "psC", T["ones1f"][0:1, :], T["hr"][0:1, 0:512], True, True,
       ["hr", "ones1f"])
    pr.op('v', lambda e: e.tensor_copy(T["brep"][:, :], T["psC"][:, 0:512]),
          reads=["psC"], writes=["brep"])
    pr.dma('y', T["stats"][0:1, 0:8], rb2[:][None, :])
    mm(T["psC"][:, 0:8], "psC", T["ones1f"][0:1, :], T["stats"][0:1, 0:8], True, True,
       ["stats", "ones1f"])
    pr.op('v', lambda e: e.tensor_copy(T["rb2rep"][:, :], T["psC"][:, 0:8]),
          reads=["psC"], writes=["rb2rep"])
    pr.dma('y', T["hrT"][0:8, 0:512], rW2[:, :])
    for c4 in range(4):
        tr(T["psC"][:, 0:8], "psC", T["hrT"][0:8, c4 * 128:(c4 + 1) * 128], ["hrT"], pin=8)
        pr.op('v', (lambda e, _c=c4: e.tensor_copy(T["rw2T"][:, _c * 8:(_c + 1) * 8],
                                                   T["psC"][:, 0:8])),
              reads=["psC"], writes=["rw2T"])

    for b in range(2):
        for c2 in range(4):
            pr.dma('y', T["xstg"][:, :], rW1[c2 * 128:(c2 + 1) * 128, :])
            for c in range(8):
                tr(T["psC"][:, 0:128], "psC", T["xstg"][:, c * 128:(c + 1) * 128], ["xstg"])
                pr.op('v', (lambda e, _c=c, _c2=c2: e.tensor_copy(
                    T["hr"][:, 0:128], T["psC"][:, 0:128])),
                      reads=["psC"], writes=["hr"])
                pr.op('s', (lambda e, _c=c, _c2=c2: e.copy(
                    T["og"][:, _c * 512 + _c2 * 128:_c * 512 + _c2 * 128 + 128],
                    T["hr"][:, 0:128])),
                      reads=["hr"], writes=["og"])
                pr.op('v', (lambda e, _c=c, _c2=c2: e.tensor_sub(
                    T["h32"][:, 0:128], T["hr"][:, 0:128],
                    T["og"][:, _c * 512 + _c2 * 128:_c * 512 + _c2 * 128 + 128])),
                      reads=["hr", "og"], writes=["h32"])
                pr.op('v', (lambda e, _c=c, _c2=c2: e.tensor_copy(
                    T["og"][:, 4096 + _c * 512 + _c2 * 128:4096 + _c * 512 + _c2 * 128 + 128],
                    T["h32"][:, 0:128])),
                      reads=["h32"], writes=["og"])
        for jt in range(16):
            pr.dma('y', T["xstg"][:, :].rearrange("p (c q) -> p c q", q=128),
                   xn1_fm[b, :, jt * 128:(jt + 1) * 128]
                   .rearrange("(c p) q -> p c q", p=128))
            pr.op('s', lambda e: e.copy(T["xh"][:, :], T["xstg"][:, :]),
                  reads=["xstg"], writes=["xh"])
            pr.op('v', lambda e: e.tensor_sub(T["x1c"][:, 0:1024], T["xstg"][:, :],
                                              T["xh"][:, :]),
                  reads=["xstg", "xh"], writes=["x1c"])
            pr.op('v', lambda e: e.tensor_copy(T["xl"][:, :], T["x1c"][:, 0:1024]),
                  reads=["x1c"], writes=["xl"])
            for c in range(8):
                for (sx, off) in (("xh", 0), ("xh", 4096), ("xl", 0)):
                    mm(T["psC"][:, 0:512], "psC", T[sx][:, c * 128:(c + 1) * 128],
                       T["og"][:, off + c * 512:off + (c + 1) * 512],
                       (c == 0 and sx == "xh" and off == 0), (c == 7 and sx == "xl"),
                       [sx, "og"])
            pr.op('v', lambda e: e.tensor_add(T["h32"][:, :], T["psC"][:, 0:512],
                                              T["rb1rep"][:, :]),
                  reads=["psC", "rb1rep"], writes=["h32"])
            pr.op('v', lambda e: e.reduce_sum(T["stats"][:, 0:1], T["h32"][:, :], AX.X),
                  reads=["h32"], writes=["stats"])
            pr.op('v', lambda e: e.tensor_scalar_mul(T["stats"][:, 0:1], T["stats"][:, 0:1],
                                                     1.0 / 512),
                  reads=["stats"], writes=["stats"])
            pr.op('s', lambda e: e.activation(T["hr"][:, 0:512], T["h32"][:, :], AF.Square),
                  reads=["h32"], writes=["hr"])
            pr.op('v', lambda e: e.reduce_sum(T["stats"][:, 1:2], T["hr"][:, 0:512], AX.X),
                  reads=["hr"], writes=["stats"])
            pr.op('v', lambda e: e.tensor_scalar_mul(T["stats"][:, 1:2], T["stats"][:, 1:2],
                                                     1.0 / 512),
                  reads=["stats"], writes=["stats"])
            pr.op('v', lambda e: e.tensor_mul(T["stats"][:, 2:3], T["stats"][:, 0:1],
                                              T["stats"][:, 0:1]),
                  reads=["stats"], writes=["stats"])
            pr.op('v', lambda e: e.tensor_sub(T["stats"][:, 1:2], T["stats"][:, 1:2],
                                              T["stats"][:, 2:3]),
                  reads=["stats"], writes=["stats"])
            pr.op('v', lambda e: e.tensor_scalar_add(T["stats"][:, 1:2], T["stats"][:, 1:2],
                                                     1e-5),
                  reads=["stats"], writes=["stats"])
            pr.op('s', lambda e: e.activation(T["stats"][:, 1:2], T["stats"][:, 1:2], AF.Ln),
                  reads=["stats"], writes=["stats"])
            pr.op('v', lambda e: e.tensor_scalar_mul(T["stats"][:, 1:2], T["stats"][:, 1:2],
                                                     -0.5),
                  reads=["stats"], writes=["stats"])
            pr.op('s', lambda e: e.activation(T["stats"][:, 1:2], T["stats"][:, 1:2], AF.Exp),
                  reads=["stats"], writes=["stats"])
            pr.op('v', lambda e: e.tensor_scalar(T["hr"][:, 0:512], T["h32"][:, :],
                                                 T["stats"][:, 0:1], T["stats"][:, 1:2],
                                                 ALU.subtract, ALU.mult),
                  reads=["h32", "stats"], writes=["hr"])
            pr.op('v', lambda e: e.tensor_mul(T["hr"][:, 0:512], T["hr"][:, 0:512],
                                              T["grep"][:, :]),
                  reads=["hr", "grep"], writes=["hr"])
            pr.op('v', lambda e: e.tensor_add(T["hr"][:, 0:512], T["hr"][:, 0:512],
                                              T["brep"][:, :]),
                  reads=["hr", "brep"], writes=["hr"])
            pr.op('v', lambda e: e.tensor_relu(T["hr"][:, 0:512], T["hr"][:, 0:512]),
                  reads=["hr"], writes=["hr"])
            for c4 in range(4):
                tr(T["psC"][:, 0:128], "psC", T["hr"][:, c4 * 128:(c4 + 1) * 128], ["hr"])
                pr.op('v', (lambda e, _c=c4: e.tensor_copy(
                    T["hrT"][:, _c * 128:(_c + 1) * 128], T["psC"][:, 0:128])),
                      reads=["psC"], writes=["hrT"])
            for c4 in range(4):
                mm(T["psE"][0:128, 0:8], "psE", T["hrT"][:, c4 * 128:(c4 + 1) * 128],
                   T["rw2T"][:, c4 * 8:(c4 + 1) * 8], c4 == 0, c4 == 3, ["hrT", "rw2T"])
            pr.op('v', lambda e: e.tensor_add(T["probs"][:, :], T["psE"][:, 0:8],
                                              T["rb2rep"][:, :]),
                  reads=["psE", "rb2rep"], writes=["probs"])
            pr.op('v', lambda e: e.reduce_max(T["stats"][:, 3:4], T["probs"][:, :], AX.X),
                  reads=["probs"], writes=["stats"])
            pr.op('v', lambda e: e.tensor_scalar_mul(T["stats"][:, 3:4], T["stats"][:, 3:4],
                                                     -1.0),
                  reads=["stats"], writes=["stats"])
            pr.op('s', lambda e: e.activation(T["probs"][:, :], T["probs"][:, :], AF.Exp,
                                              bias=T["stats"][:, 3:4],
                                              accum_out=T["stats"][:, 4:5]),
                  reads=["probs", "stats"], writes=["probs", "stats4"])
            pr.op('v', lambda e: e.reciprocal(T["stats"][:, 4:5], T["stats"][:, 4:5]),
                  reads=["stats4"], writes=["stats4"])
            pr.op('v', lambda e: e.tensor_scalar(T["probs"][:, :], T["probs"][:, :],
                                                 T["stats"][:, 4:5], None, ALU.mult),
                  reads=["probs", "stats4"], writes=["probs"])
            pr.op('v', lambda e: e.max(T["mx8"][:, :], T["probs"][:, :]),
                  reads=["probs"], writes=["mx8"])
            pr.op('v', lambda e: e.max_index(T["mi8"][:, :], T["mx8"][:, :],
                                             T["probs"][:, :]),
                  reads=["mx8", "probs"], writes=["mi8"])
            pr.op('v', lambda e: e.tensor_copy(T["tif"][:, 0:2], T["mi8"][:, 0:2]),
                  reads=["mi8"], writes=["tif"])
            pr.op('v', lambda e: e.tensor_add(T["stats"][:, 5:6], T["mx8"][:, 0:1],
                                              T["mx8"][:, 1:2]),
                  reads=["mx8"], writes=["stats"])
            pr.op('v', lambda e: e.reciprocal(T["stats"][:, 5:6], T["stats"][:, 5:6]),
                  reads=["stats"], writes=["stats"])
            pr.op('v', lambda e: e.tensor_scalar(T["eq1"][:, :], T["iotaf"][:, :],
                                                 T["tif"][:, 0:1], None, ALU.is_equal),
                  reads=["iotaf", "tif"], writes=["eq1"])
            pr.op('v', lambda e: e.tensor_scalar(T["eq2"][:, :], T["iotaf"][:, :],
                                                 T["tif"][:, 1:2], None, ALU.is_equal),
                  reads=["iotaf", "tif"], writes=["eq2"])
            pr.op('v', lambda e: e.tensor_scalar(T["eq1"][:, :], T["eq1"][:, :],
                                                 T["mx8"][:, 0:1], None, ALU.mult),
                  reads=["eq1", "mx8"], writes=["eq1"])
            pr.op('v', lambda e: e.tensor_scalar(T["eq2"][:, :], T["eq2"][:, :],
                                                 T["mx8"][:, 1:2], None, ALU.mult),
                  reads=["eq2", "mx8"], writes=["eq2"])
            pr.op('v', lambda e: e.tensor_add(T["gts"][:, :], T["eq1"][:, :], T["eq2"][:, :]),
                  reads=["eq1", "eq2"], writes=["gts"])
            pr.op('v', lambda e: e.tensor_scalar(T["gts"][:, :], T["gts"][:, :],
                                                 T["stats"][:, 5:6], None, ALU.mult),
                  reads=["gts", "stats"], writes=["gts"])
            tr(T["psC"][0:8, 0:128], "psC", T["gts"][:, :], ["gts"])
            pr.op('v', lambda e: e.tensor_copy(T["gt8"][0:8, :], T["psC"][0:8, 0:128]),
                  reads=["psC"], writes=["gt8"])
            pr.dma('y', gscr[b, :, jt * 128:(jt + 1) * 128], T["gt8"][0:8, :])

        for c in range(8):
            pr.dma('g', T["xn1bf"][:, c * 2048:(c + 1) * 2048],
                   xn1_fm[b, c * 128:(c + 1) * 128, :])

        for ex in range(8):
            pr.dma('y', T["gflat"][0:1, :], gscr[b, ex, :][None, :])
            pr.dma('y', T["obrow"][0:1, :], ob[ex, :][None, :])
            for hg in range(2):
                for (wd, dstb, tok_major) in ((qW, "q16", False), (kW, "k16", False),
                                              (vW, None, True)):
                    pr.dma('g', T["wst0"][:, :].rearrange("p (c f) -> p c f", f=512),
                           wd[ex].rearrange("(c p) (g f) -> p c g f", p=128, g=2)
                           [:, :, hg, :])
                    if not tok_major:
                        for hpc in range(4):
                            for c in range(8):
                                for f4 in range(4):
                                    mm(T["psD"][:, f4 * 512:(f4 + 1) * 512], "psD",
                                       T["wst0"][:, c * 512 + hpc * 128:
                                                  c * 512 + (hpc + 1) * 128],
                                       T["xn1bf"][:, c * 2048 + f4 * 512:
                                                  c * 2048 + (f4 + 1) * 512],
                                       c == 0, c == 7, ["wst0", "xn1bf"])
                            bvec = qb if dstb == "q16" else kb
                            pr.dma('y', T["zr"][0:1, 0:128],
                                   bvec[ex, hg * 512 + hpc * 128:
                                        hg * 512 + (hpc + 1) * 128][None, :])
                            tr(T["psC"][:, 0:1], "psC", T["zr"][0:1, 0:128], ["zr"], pin=1)
                            pr.op('v', lambda e: e.tensor_copy(T["divrep"][:, 0:1],
                                                               T["psC"][:, 0:1]),
                                  reads=["psC"], writes=["divrep"])
                            pr.op('v', (lambda e, _d=dstb, _hpc=hpc: e.tensor_scalar(
                                T[_d][:, _hpc * 2048:(_hpc + 1) * 2048],
                                T["psD"][:, 0:2048], T["divrep"][:, 0:1], None, ALU.add)),
                                  reads=["psD", "divrep"], writes=[dstb])
                    else:
                        pr.dma('y', T["zr"][0:1, 0:512],
                               vb[ex, hg * 512:(hg + 1) * 512][None, :])
                        mm(T["psC"][:, 0:512], "psC", T["ones1f"][0:1, :],
                           T["zr"][0:1, 0:512], True, True, ["zr", "ones1f"])
                        pr.op('v', lambda e: e.tensor_copy(T["divrep"][:, :],
                                                           T["psC"][:, 0:512]),
                              reads=["psC"], writes=["divrep"])
                        for tt in range(16):
                            for c in range(8):
                                mm(T["psC"][:, 0:512], "psC",
                                   T["xn1bf"][:, c * 2048 + tt * 128:
                                              c * 2048 + (tt + 1) * 128],
                                   T["wst0"][:, c * 512:(c + 1) * 512],
                                   c == 0, c == 7, ["wst0", "xn1bf"])
                            pr.op('v', (lambda e, _tt=tt, _hg=hg: e.tensor_add(
                                T["vtok2"][:, _tt * 1040 + _hg * 520:
                                           _tt * 1040 + _hg * 520 + 520]
                                .rearrange("p (h z) -> p h z", z=65)[:, :, 0:64],
                                T["psC"][:, 0:512]
                                .rearrange("p (h z) -> p h z", z=64),
                                T["divrep"][:, 0:512]
                                .rearrange("p (h z) -> p h z", z=64))),
                                  reads=["psC", "divrep"], writes=["vtok2"])
                for hp in range(8):
                    hglob = hg * 8 + hp
                    par = hp % 2
                    hpc = hp // 2
                    pse = "psE" if par == 0 else "psF"
                    for tqc in range(4):
                        for tkt in range(16):
                            if tkt % 2 == 0:
                                sdst = T[pse][:, 0:512]
                                skey = pse
                            else:
                                sdst = T["psD"][:, par * 512:(par + 1) * 512]
                                skey = "psD"
                            mm(sdst, skey,
                               T["k16"][par * 64:(par + 1) * 64,
                                        hpc * 2048 + tkt * 128:hpc * 2048 + (tkt + 1) * 128],
                               T["q16"][par * 64:(par + 1) * 64,
                                        hpc * 2048 + tqc * 512:hpc * 2048 + (tqc + 1) * 512],
                               True, True, ["k16", "q16"],
                               tp=(64, 0) if par == 1 else None)
                            pr.op('s', (lambda e, _t=tkt, _d=sdst: e.activation(
                                T["expT"][:, _t * 512:(_t + 1) * 512], _d,
                                AF.Exp, scale=T[c18][:, 0:1])),
                                  reads=[skey, c18], writes=["expT"])
                        for tkt in range(16):
                            mm(T["psG"][0:65, 0:512], "psG",
                               T["vtok2"][:, tkt * 1040 + hglob * 65:
                                          tkt * 1040 + (hglob + 1) * 65],
                               T["expT"][:, tkt * 512:(tkt + 1) * 512],
                               tkt == 0, tkt == 15, ["vtok2", "expT"])
                        pr.op('v', lambda e: e.reciprocal(T["zr"][0:1, 0:512],
                                                          T["psG"][64:65, 0:512]),
                              reads=["psG"], writes=["zr"])
                        pr.op('v', (lambda e, _tqc=tqc: e.tensor_mul(
                            T["dvr"][0:1, 0:512], T["zr"][0:1, 0:512],
                            T["gflat"][0:1, _tqc * 512:(_tqc + 1) * 512])),
                              reads=["zr", "gflat"], writes=["dvr"])
                        mm(T["psC"][:, 0:512], "psC", T["ones1f"][0:1, :],
                           T["dvr"][0:1, 0:512], True, True, ["dvr", "ones1f"])
                        pr.op('v', lambda e: e.tensor_copy(T["divrep"][:, :],
                                                           T["psC"][:, 0:512]),
                              reads=["psC"], writes=["divrep"])
                        pr.op('v', (lambda e, _par=par, _hpc=hpc, _tqc=tqc: e.tensor_mul(
                            T["og"][_par * 64:(_par + 1) * 64,
                                    _hpc * 2048 + _tqc * 512:_hpc * 2048 + (_tqc + 1) * 512],
                            T["psG"][0:64, 0:512], T["divrep"][0:64, 0:512])),
                              reads=["psG", "divrep"], writes=["og"])
                pr.dma('g', T["wst0"][:, :].rearrange("p (c f) -> p c f", f=1024),
                       oW[ex].rearrange("(g c p) f -> p g c f", p=128, g=2)[:, hg, :, :])
                for m in range(8):
                    for tqc in range(4):
                        for c in range(4):
                            mm(T["psC"][:, 0:512], "psC",
                               T["wst0"][:, c * 1024 + m * 128:c * 1024 + (m + 1) * 128],
                               T["og"][:, c * 2048 + tqc * 512:c * 2048 + (tqc + 1) * 512],
                               c == 0, (c == 3 and hg == 1), ["wst0", "og"])
                        if hg == 0:
                            mm(T["psC"][:, 0:512], "psC",
                               T["obrow"][0:1, m * 128:(m + 1) * 128],
                               T["gflat"][0:1, tqc * 512:(tqc + 1) * 512],
                               False, True, ["obrow", "gflat"])
                        else:
                            pr.op('t', (lambda e: e.nop()), reads=[], writes=[]) if False                                 else None
                        pr.op('v', lambda e: e.tensor_copy(T["ostg"][:, :],
                                                           T["psC"][:, 0:512]),
                              reads=["psC"], writes=["ostg"])
                        first = (ex == 0 and hg == 0)
                        pr.dma('g', oacc[b, m * 128:(m + 1) * 128,
                                         tqc * 512:(tqc + 1) * 512],
                               T["ostg"][:, :],
                               **({} if first else {"accum_op": ALU.add}))

        for m in range(8):
            pr.dma('y', T["x1c"][:, :], x1_fm[b, m * 128:(m + 1) * 128, :])
            pr.dma('y', T["s1c"][:, :], s1_fm[b, m * 128:(m + 1) * 128, :])
            pr.dma('y', T["x2st"][:, :], oacc[b, m * 128:(m + 1) * 128, :])
            pr.op('s', lambda e: e.activation(T["x2st"][:, :], T["x2st"][:, :], AF.Silu),
                  reads=["x2st"], writes=["x2st"])
            pr.op('v', lambda e: e.tensor_mul(T["s1c"][:, :], T["s1c"][:, :], T["x2st"][:, :]),
                  reads=["s1c", "x2st"], writes=["s1c"])
            pr.op('v', lambda e: e.tensor_add(T["x2st"][:, :], T["x1c"][:, :], T["s1c"][:, :]),
                  reads=["x1c", "s1c"], writes=["x2st"])
            for tt in range(16):
                tr(T["psC"][:, 0:128], "psC", T["x2st"][:, tt * 128:(tt + 1) * 128],
                   ["x2st"])
                pr.op('v', (lambda e, _tt=tt: e.tensor_copy(
                    T["x1c"][:, _tt * 128:(_tt + 1) * 128], T["psC"][:, 0:128])),
                      reads=["psC"], writes=["x1c"])
            pr.dma('y', out[b, :, m * 128:(m + 1) * 128]
                   .rearrange("(tt p) q -> p tt q", p=128),
                   T["x1c"][:, :].rearrange("p (tt q) -> p tt q", q=128))



def build_fused(dbg=False):
    nc = bass.Bass("TRN2", target_bir_lowering=False)
    EI = lambda n, s: nc.dram_tensor(n, s, F32, kind="ExternalInput").ap()
    DK = dict(kind="ExternalOutput") if dbg else {}
    x_own = EI("x_own", [2, 2048, 1024])
    x_full = EI("x_full", [16, 2048, 1024])
    myexp = EI("myexp", [1, 1])
    W_in = EI("W_in", [1024, 4096]); conv_w = EI("conv_w", [2048, 4])
    conv_b = EI("conv_b", [2048]); W_xproj = EI("W_xproj", [2048, 96])
    W_dt = EI("W_dt", [64, 2048]); b_dt = EI("b_dt", [2048])
    A_log = EI("A_log", [2048, 16]); D_skip = EI("D_skip", [2048])
    W_out = EI("W_out", [2048, 1024]); ln_g = EI("ln_g", [1024])
    ln_b = EI("ln_b", [1024]); rw = EI("rw", [8, 1024])
    w1 = EI("w1", [8, 1024, 4096]); b1 = EI("b1", [8, 4096])
    w2 = EI("w2", [8, 4096, 1024]); b2 = EI("b2", [8, 1024])
    Wq = EI("Wq", [1024, 1024]); Wk = EI("Wk", [1024, 1024])
    Wv = EI("Wv", [1024, 1024]); Wo = EI("Wo", [1024, 1024]); bo = EI("bo", [1024])
    rW1 = EI("rW1", [512, 1024]); rb1 = EI("rb1", [512])
    lng = EI("lng_d", [512]); lnb = EI("lnb_d", [512])
    rW2 = EI("rW2", [8, 512]); rb2 = EI("rb2", [8])
    qW = EI("qW", [8, 1024, 1024]); qb = EI("qb", [8, 1024])
    kW = EI("kW", [8, 1024, 1024]); kb = EI("kb", [8, 1024])
    vW = EI("vW", [8, 1024, 1024]); vb = EI("vb", [8, 1024])
    oW = EI("oW", [8, 1024, 1024]); ob = EI("ob", [8, 1024])
    out = nc.dram_tensor("out", [2, 2048, 1024], F32, kind="ExternalOutput").ap()
    s_fm = nc.dram_tensor("s_fm", [2, 1024, 2048], F32, **DK).ap()
    xn_tok = nc.dram_tensor("xn_tok", [4097, 1024], F32, **DK).ap()
    route = nc.dram_tensor("route", [4096, 2], F32, **DK).ap()
    eo = nc.dram_tensor("eo", [NSLOT + 1, 1024], F32, **DK).ap()
    gidxT = nc.dram_tensor("gidxT", [256, 128], F32, **DK).ap()
    srcd = nc.dram_tensor("srcd", [NSLOT + 2, 1], I32).ap()
    x1_fm = nc.dram_tensor("x1_fm", [2, 1024, 2048], F32, **DK).ap()
    xn1_fm = nc.dram_tensor("xn1_fm", [2, 1024, 2048], F32, **DK).ap()
    s1_fm = nc.dram_tensor("s1_fm", [2, 1024, 2048], F32, **DK).ap()
    qscr = nc.dram_tensor("qscr", [2048, 1024], F32).ap()
    if dbg:
        pr_dump = nc.dram_tensor("dbgdump", [1024, 256], F32, kind="ExternalOutput").ap()
    else:
        pr_dump = None
    gscr = nc.dram_tensor("gscr", [2, 8, 2048], F32).ap()
    oacc = nc.dram_tensor("oacc", [2, 1024, 2048], F32, **DK).ap()

    pr = Pr(nc)
    pr.dbgdump = pr_dump
    pr.alloc(CONST_SBUFS, [])
    emit_consts(pr, xn_tok, eo)
    phases = getattr(build_fused, "PHASES", "ABCD")
    pr.constcol(1.0)
    pr.constcol(-1.0)
    pr.constcol(0.125)
    pr.base_T = dict(pr.T)
    flt = lambda specs: [s for s in specs if s[0] not in SHARED_CONSTS]

    if 'A' in phases:
        pr.push_scope()
        pr.T = dict(pr.base_T)
        pr.alloc(flt(SBUFS_A), PSUMS_A, prefix='pA_')
        emit_A(pr, x_own, W_in, conv_w, conv_b, W_xproj, W_dt, b_dt, A_log, D_skip,
               W_out, ln_g, ln_b, rw, s_fm, xn_tok, route)
        pr.barrier()
        pr.pop_scope()

    if 'B' in phases:
        noscope = getattr(build_fused, "NOSCOPE", False)
        if not noscope:
            pr.push_scope()
        pr.T = dict(pr.base_T)
        pr.alloc(flt(SBUFS_B), PSUMS_B, prefix='' if noscope else 'pB_')
        emit_B2(pr, x_full, xn_tok, myexp, rw, w1, b1, w2, b2, eo, gidxT, srcd)
        pr.barrier()
        if not noscope:
            pr.pop_scope()

    if 'C' in phases:
        pr.push_scope()
        pr.T = dict(pr.base_T)
        pr.alloc(flt(SBUFS_C), PSUMS_C, prefix='pC_')
        emit_C(pr, x_own, s_fm, eo, route, gidxT, myexp, Wq, Wk, Wv, Wo, bo,
               x1_fm, xn1_fm, s1_fm, qscr)
        pr.barrier()
        pr.pop_scope()

    if 'D' in phases:
        pr.push_scope()
        pr.T = dict(pr.base_T)
        pr.alloc(flt(SBUFS_D), PSUMS_D, prefix='pD_')
        emit_D(pr, x1_fm, xn1_fm, s1_fm, rW1, rb1, lng, lnb, rW2, rb2, qW, qb,
               kW, kb, vW, vb, oW, ob, out, gscr, oacc)
        pr.final_waits('y')
        pr.final_waits('g')
        pr.pop_scope()
    pr.final_waits('y')
    pr.final_waits('g')
    pr.finalize()
    return nc


def kernel(**inputs):
    nc = build_fused()
    in_maps = []
    for k in range(NCORES):
        in_maps.append(dict(
            x_own=np.ascontiguousarray(inputs["L0_attn_x"][2 * k:2 * k + 2]),
            x_full=inputs["L0_attn_x"],
            myexp=np.array([[float(k)]], np.float32),
            W_in=inputs["L0_attn_W_in"], conv_w=inputs["L0_attn_conv_w"],
            conv_b=inputs["L0_attn_conv_b"], W_xproj=inputs["L0_attn_W_xproj"],
            W_dt=inputs["L0_attn_W_dt"], b_dt=inputs["L0_attn_b_dt"],
            A_log=inputs["L0_attn_A_log"], D_skip=inputs["L0_attn_D_skip"],
            W_out=inputs["L0_attn_W_out"], ln_g=inputs["L0_attn_ln_gamma"],
            ln_b=inputs["L0_attn_ln_beta"], rw=inputs["L0_ffn_router_w"],
            w1=inputs["L0_ffn_w1"], b1=inputs["L0_ffn_b1"],
            w2=inputs["L0_ffn_w2"], b2=inputs["L0_ffn_b2"],
            Wq=inputs["L1_attn_Wq"], Wk=inputs["L1_attn_Wk"],
            Wv=inputs["L1_attn_Wv"], Wo=inputs["L1_attn_Wo"],
            bo=inputs["L1_attn_bo"],
            rW1=inputs["L1_ffn_rW1"], rb1=inputs["L1_ffn_rb1"],
            lng_d=inputs["L1_ffn_ln_g"], lnb_d=inputs["L1_ffn_ln_b"],
            rW2=inputs["L1_ffn_rW2"], rb2=inputs["L1_ffn_rb2"],
            qW=inputs["L1_ffn_qW"], qb=inputs["L1_ffn_qb"],
            kW=inputs["L1_ffn_kW"], kb=inputs["L1_ffn_kb"],
            vW=inputs["L1_ffn_vW"], vb=inputs["L1_ffn_vb"],
            oW=inputs["L1_ffn_oW"], ob=inputs["L1_ffn_ob"]))
    res = run_bass_kernel_spmd(nc, in_maps, core_ids=list(range(NCORES)))
    return np.concatenate([r["out"] for r in res.results], axis=0)
